# version 03
# baseline (speedup 1.0000x reference)
import numpy as np
from contextlib import ExitStack

import concourse.bass as bass
import concourse.mybir as mybir
from concourse.bass_utils import run_bass_kernel_spmd

F32 = mybir.dt.float32
BF16 = mybir.dt.bfloat16
AF = mybir.ActivationFunctionType
ALU = mybir.AluOpType
AX = mybir.AxisListType

B, S, HID = 2, 4096, 2048
NH, DN, DR, DQ, DV, LORA = 16, 128, 64, 192, 128, 512
SCALE = float(DQ) ** -0.5
EPS_LN = 1e-5
EPS_RMS = 1e-6
HF = 5120
NT = S // 4
NKC = HID // 128
DBG = False


class EW:
    def __init__(self, raw, sem, name, dma=False):
        self.raw = raw
        self.sem = sem
        self.n = 0
        self.name = name
        self.step = 16 if dma else 1
        self.waited = {}

    def wait(self, *toks):
        for t in toks:
            if t is None:
                continue
            sem, n = t
            k = sem.name
            if self.waited.get(k, 0) >= n:
                continue
            self.raw.wait_ge(sem, n)
            self.waited[k] = n

    def done(self, inst):
        self.n += self.step
        inst.then_inc(self.sem, self.step)
        return (self.sem, self.n)

    def tok(self):
        return (self.sem, self.n) if self.n else None


class Em:
    def __init__(self, nc, ctx):
        self.nc = nc
        E = ctx.enter_context
        self.pe = EW(nc.tensor, E(nc.semaphore("s_pe")), "pe")
        self.act = EW(nc.scalar, E(nc.semaphore("s_act")), "act")
        self.dve = EW(nc.vector, E(nc.semaphore("s_dve")), "dve")
        self.gpc = EW(nc.gpsimd, E(nc.semaphore("s_gpc")), "gpc")
        self.qs = EW(nc.sync, E(nc.semaphore("s_qs")), "qs", dma=True)
        self.qg = EW(nc.gpsimd, E(nc.semaphore("s_qg")), "qg", dma=True)
        self.qa = EW(nc.scalar, E(nc.semaphore("s_qa")), "qa", dma=True)
        self.all = [self.pe, self.act, self.dve, self.gpc, self.qs, self.qg, self.qa]

    def barrier(self):
        toks = [e.tok() for e in self.all]
        for e in self.all:
            e.wait(*toks)


def cm(t, c):
    return t[:, :].rearrange("p (c n) -> p c n", c=c)


def build():
    nc = bass.Bass("TRN2", target_bir_lowering=False)

    def din(name, shape, dt=F32):
        return nc.dram_tensor(name, shape, dt, kind="ExternalInput").ap()

    xT = din("xT", [HID, S])
    x_own = din("x_own", [HID, NT])
    cmeta = din("cmeta", [128, 8])
    wqn = din("wqn", [HID, NH * DN])
    wqp = din("wqp", [HID, NH * DR])
    wkc = din("wkc", [HID, LORA])
    wkp = din("wkp", [HID, DR])
    wkbk = din("wkbk", [LORA, NH * DN])
    wkbv = din("wkbv", [LORA, NH * DV])
    woT = din("woT", [NH * DV, HID])
    wfcT = din("wfcT", [HID, HF])
    wpjT = din("wpjT", [HF, HID])
    kvw = din("kvw", [128, 4])
    cosr = din("cosr", [1, DR])
    sinr = din("sinr", [1, DR])

    outT = nc.dram_tensor("outT", [HID, NT], F32, kind="ExternalOutput").ap()

    rms_d = nc.dram_tensor("rms_d", [LORA, S], BF16, kind="Internal").ap()
    ot_d = nc.dram_tensor("ot_d", [NH * DV, NT], BF16, kind="Internal").ap()
    x1_d = nc.dram_tensor("x1_d", [HID, NT], F32, kind="Internal").ap()

    dbg = {}
    if DBG:
        def dout(name, shape, dt=F32):
            dbg[name] = nc.dram_tensor(name, shape, dt, kind="ExternalOutput").ap()
            return dbg[name]
        dout("d_ln1own", [HID, NT], BF16)
        dout("d_rms", [LORA, S], BF16)
        dout("d_kpe", [DR, S], BF16)
        dout("d_qn", [NH * DN, NT], BF16)
        dout("d_qp", [8 * 128, NT], BF16)
        dout("d_kn0", [4 * DN, S], BF16)
        dout("d_ot", [NH * DV, NT], BF16)
        dout("d_x1", [HID, NT], F32)
        dout("d_am", [1, 4], F32)
        dout("d_ln2", [HID, NT], BF16)
        dout("d_h2", [HF, NT // 2], BF16)
        dout("d_y", [HID, NT // 2], F32)

    ctx = ExitStack()
    with ctx:
        em = Em(nc, ctx)
        pe, act, dve, gpc, qs, qg, qa = em.pe, em.act, em.dve, em.gpc, em.qs, em.qg, em.qa
        E = ctx.enter_context

        ones_b = E(nc.sbuf_tensor("ones_b", [128, 1], BF16))
        ones_s = E(nc.sbuf_tensor("ones_s", [128, 1], BF16))
        ln2n30 = E(nc.sbuf_tensor("ln2n30", [1, 1], F32))
        ones_f = E(nc.sbuf_tensor("ones_f", [1, 128], F32))
        onesc_f = E(nc.sbuf_tensor("onesc_f", [128, 1], F32))
        eps_ln_t = E(nc.sbuf_tensor("eps_ln", [1, 1], F32))
        eps_rms_t = E(nc.sbuf_tensor("eps_rms", [1, 1], F32))
        cmeta_t = E(nc.sbuf_tensor("cmeta_t", [128, 8], F32))
        kvw_t = E(nc.sbuf_tensor("kvw_t", [128, 4], F32))
        cosb = E(nc.sbuf_tensor("cosb", [128, DR], F32))
        spb = E(nc.sbuf_tensor("spb", [128, DR], F32))
        nmax = E(nc.sbuf_tensor("nmax", [1, 4], F32))
        scr = E(nc.sbuf_tensor("scr", [1, 16], F32))
        amv = E(nc.sbuf_tensor("amv", [1, 4], F32))
        thr_fc = E(nc.sbuf_tensor("thr_fc", [128, 2], F32))
        thr_pj = E(nc.sbuf_tensor("thr_pj", [128, 2], F32))
        cy_b = E(nc.sbuf_tensor("cy_b", [128, 1], F32))
        srow = E(nc.sbuf_tensor("srow", [1, 4 * 512], F32))
        SM, SQ, SVAR, SRS = 0, 1, 2, 3

        def row(i, w=512):
            return srow[:, 512 * i:512 * i + w]

        def bc_scalar(dst_col128, src_row1, ps):
            t1 = pe.done(pe.raw.matmul(ps[:, 0:1], ones_f[:, :], src_row1, start=True, stop=True))
            dve.wait(t1)
            return dve.done(dve.raw.tensor_copy(dst_col128, ps[:, 0:1]))

        def ln_stats(src_view, T, chunks, n_dims, eps_t, bfc, sqc, ps1, ps2):
            t_mm = None
            t_slot = [None, None]
            for c in range(chunks):
                sl = c % 2
                dve.wait(t_slot[sl])
                tc = dve.done(dve.raw.tensor_copy(bfc[:, T * sl:T * (sl + 1)],
                                                  src_view[:, c, :]))
                act.wait(t_slot[sl])
                ts = act.done(act.raw.activation(out=sqc[:, T * sl:T * (sl + 1)],
                                                 in_=src_view[:, c, :],
                                                 func=AF.Square, bias=0.0, scale=1.0))
                pe.wait(tc, ts)
                pe.raw.matmul(ps1[:, 0:T], ones_b[:, :], bfc[:, T * sl:T * (sl + 1)],
                              start=(c == 0), stop=(c == chunks - 1))
                mmi = pe.raw.matmul(ps2[:, 0:T], ones_b[:, :], sqc[:, T * sl:T * (sl + 1)],
                                    start=(c == 0), stop=(c == chunks - 1))
                t_mm = pe.done(mmi)
                t_slot[sl] = t_mm
            dve.wait(t_mm)
            t1 = dve.done(dve.raw.tensor_single_scalar(row(SM, T), ps1[:, 0:T],
                                                       1.0 / n_dims, ALU.mult))
            t2 = dve.done(dve.raw.tensor_single_scalar(row(SQ, T), ps2[:, 0:T],
                                                       1.0 / n_dims, ALU.mult))
            dve.wait(t1)
            t3 = dve.done(dve.raw.tensor_tensor(row(SVAR, T), row(SM, T), row(SM, T), ALU.mult))
            dve.wait(t2, t3)
            t4 = dve.done(dve.raw.tensor_tensor(row(SVAR, T), row(SQ, T), row(SVAR, T),
                                                ALU.subtract))
            act.wait(t4)
            t5 = act.done(act.raw.activation(out=row(SRS, T), in_=row(SVAR, T), func=AF.Ln,
                                             bias=eps_t[:, :], scale=1.0))
            act.wait(t5)
            return act.done(act.raw.activation(out=row(SRS, T), in_=row(SRS, T), func=AF.Exp,
                                               bias=0.0, scale=-0.5))

        def ln_norm(src_view, T, chunks, mb_sb, rb_sb, ps_a, ps_b, t_rs, out_view,
                    add_view=None):
            pe.wait(t_rs)
            tbm = pe.done(pe.raw.matmul(ps_a[:, 0:T], ones_f[:, :], row(SM, T),
                                        start=True, stop=True))
            tbr = pe.done(pe.raw.matmul(ps_b[:, 0:T], ones_f[:, :], row(SRS, T),
                                        start=True, stop=True))
            dve.wait(tbm)
            tem = dve.done(dve.raw.tensor_copy(mb_sb[:, 0:T], ps_a[:, 0:T]))
            dve.wait(tbr)
            ter = dve.done(dve.raw.tensor_copy(rb_sb[:, 0:T], ps_b[:, 0:T]))
            mb_b = mb_sb[:, 0:T].rearrange("p (o n) -> p o n", o=1).broadcast_to([128, chunks, T])
            rb_b = rb_sb[:, 0:T].rearrange("p (o n) -> p o n", o=1).broadcast_to([128, chunks, T])
            dve.wait(tem)
            tn1 = dve.done(dve.raw.tensor_tensor(src_view, src_view, mb_b, ALU.subtract))
            dve.wait(tn1, ter)
            if add_view is None:
                tn2 = dve.done(dve.raw.tensor_tensor(out_view, src_view, rb_b, ALU.mult))
            else:
                tn2a = dve.done(dve.raw.tensor_tensor(src_view, src_view, rb_b, ALU.mult))
                dve.wait(tn2a)
                tn2 = dve.done(dve.raw.tensor_tensor(out_view, src_view, add_view, ALU.add))
            return tn2, (tbm, tbr)

        t_cm = qs.done(qs.raw.dma_start(out=cmeta_t[:, :], in_=cmeta))
        qs.done(qs.raw.dma_start(out=kvw_t[:, :], in_=kvw))
        dve.done(dve.raw.memset(ones_b[:, :], 1.0))
        dve.done(dve.raw.memset(ones_s[:, :], float(2.0 ** -30)))
        dve.done(dve.raw.memset(ln2n30[:, :], float(-30.0 * np.log(2.0))))
        t_onef = dve.done(dve.raw.memset(ones_f[:, :], 1.0))
        dve.done(dve.raw.memset(onesc_f[:, :], 1.0))
        dve.done(dve.raw.memset(eps_ln_t[:, :], EPS_LN))
        t_ep2 = dve.done(dve.raw.memset(eps_rms_t[:, :], EPS_RMS))

        with ExitStack() as bd_scope:
            masks = bd_scope.enter_context(nc.sbuf_tensor("masks", [128, 8 * 256], BF16))
            kpefull = bd_scope.enter_context(nc.sbuf_tensor("kpefull", [65, S], BF16))

            with ExitStack() as ph:
                cs_row = ph.enter_context(nc.sbuf_tensor("cs_row", [1, 2 * DR], F32))
                iq = ph.enter_context(nc.sbuf_tensor("iq", [128, 256], F32))
                iqa = ph.enter_context(nc.sbuf_tensor("iqa", [128, 256], F32))
                ps_b0 = ph.enter_context(nc.psum_tensor("ps_b0", [128, 512], F32))
                t_cos = qs.done(qs.raw.dma_start(out=cs_row[:, 0:DR], in_=cosr))
                t_sin = qs.done(qs.raw.dma_start(out=cs_row[:, DR:2 * DR], in_=sinr))
                dve.wait(t_sin)
                t_neg = dve.done(dve.raw.tensor_single_scalar(
                    cs_row[:, DR:DR + 32], cs_row[:, DR:DR + 32], -1.0, ALU.mult))
                pe.wait(t_cos, t_onef)
                t1 = pe.done(pe.raw.matmul(ps_b0[:, 0:DR], ones_f[:, :], cs_row[:, 0:DR],
                                           start=True, stop=True))
                dve.wait(t1)
                t_cb = dve.done(dve.raw.tensor_copy(cosb[:, :], ps_b0[:, 0:DR]))
                pe.wait(t_cb, t_neg)
                t2 = pe.done(pe.raw.matmul(ps_b0[:, 0:DR], ones_f[:, :], cs_row[:, DR:2 * DR],
                                           start=True, stop=True))
                dve.wait(t2)
                t_sb = dve.done(dve.raw.tensor_copy(spb[:, :], ps_b0[:, 0:DR]))
                t_io = gpc.done(gpc.raw.iota(iq[:, :], pattern=[[4, 256]], base=0,
                                             channel_multiplier=0,
                                             allow_small_or_imprecise_dtypes=True))
                dve.wait(t_io, t_cm)
                t_iqa = dve.done(dve.raw.tensor_single_scalar(iqa[:, :], iq[:, :],
                                                              cmeta_t[:, 0:1], ALU.subtract))
                dve.wait(t_iqa)
                for j in range(8):
                    mj = dve.raw.tensor_single_scalar(masks[:, 256 * j:256 * (j + 1)],
                                                      iqa[:, :], float(128 * j), ALU.is_ge)
                dve.done(mj)
                em.barrier()

            NB = 256
            NBLK = S // NB
            with ExitStack() as bc_scope:
                ln1own = bc_scope.enter_context(nc.sbuf_tensor("ln1own", [128, NKC * NT], BF16))

                with ExitStack() as ph:
                    P = ph.enter_context
                    wkc_t = P(nc.sbuf_tensor("wkc_t", [128, NKC * 512], BF16))
                    weffk = P(nc.sbuf_tensor("weffk", [128, NKC * DR], BF16))
                    xblk0 = P(nc.sbuf_tensor("xblk0", [128, NKC * NB], F32))
                    xblk1 = P(nc.sbuf_tensor("xblk1", [128, NKC * NB], F32))
                    xbf = P(nc.sbuf_tensor("xbf", [128, NKC * NB], BF16))
                    xsq = P(nc.sbuf_tensor("xsq", [128, NKC * NB], BF16))
                    mb_s = P(nc.sbuf_tensor("mb_s", [128, NB], F32))
                    rb_s = P(nc.sbuf_tensor("rb_s", [128, NB], F32))
                    ln1b = P(nc.sbuf_tensor("ln1b", [128, NKC * NB], BF16))
                    cf32 = P(nc.sbuf_tensor("cf32", [128, 4 * NB], F32))
                    csq = P(nc.sbuf_tensor("csq", [128, 4 * NB], BF16))
                    rr_s = P(nc.sbuf_tensor("rr_s", [128, NB], F32))
                    rmsb = P(nc.sbuf_tensor("rmsb", [128, 4 * NB], BF16))
                    ps_s1 = P(nc.psum_tensor("ps_s1", [1, 512], F32))
                    ps_s2 = P(nc.psum_tensor("ps_s2", [1, 512], F32))
                    ps_c0 = P(nc.psum_tensor("ps_c0", [128, 512], F32))
                    ps_c1 = P(nc.psum_tensor("ps_c1", [128, 512], F32))
                    ps_c2 = P(nc.psum_tensor("ps_c2", [128, 512], F32))
                    ps_c3 = P(nc.psum_tensor("ps_c3", [128, 512], F32))
                    ps_kp = P(nc.psum_tensor("ps_kp", [64, 512], F32))
                    cps = [ps_c0, ps_c1, ps_c2, ps_c3]

                    t_wkc = qg.done(qg.raw.dma_start(
                        out=cm(wkc_t, NKC), in_=wkc.rearrange("(c p) n -> p c n", p=128)))
                    with ExitStack() as ph2:
                        wkpf = ph2.enter_context(nc.sbuf_tensor("wkpf", [128, NKC * DR], F32))
                        wktmp = ph2.enter_context(nc.sbuf_tensor("wktmp", [128, NKC * DR], F32))
                        t_wkp = qs.done(qs.raw.dma_start(
                            out=cm(wkpf, NKC), in_=wkp.rearrange("(c p) n -> p c n", p=128)))
                        dve.wait(t_wkp, t_sb)
                        cb3 = cosb[:, :].rearrange("p (o d) -> p o d", o=1) \
                            .broadcast_to([128, NKC, DR])
                        sb4 = spb[:, :].rearrange("p (o t d) -> p o t d", o=1, t=2) \
                            .broadcast_to([128, NKC, 2, 32])
                        t1 = dve.done(dve.raw.tensor_tensor(cm(wktmp, NKC), cm(wkpf, NKC),
                                                            cb3, ALU.mult))
                        wperm = wkpf[:, :].rearrange("p (c t d) -> p c t d",
                                                     c=NKC, t=2)[:, :, 1::-1, :]
                        weffk4 = weffk[:, :].rearrange("p (c t d) -> p c t d", c=NKC, t=2)
                        t2 = dve.done(dve.raw.tensor_tensor(weffk4, wperm, sb4, ALU.mult))
                        dve.wait(t1, t2)
                        t_wk = dve.done(dve.raw.tensor_tensor(weffk[:, :], weffk[:, :],
                                                              wktmp[:, :], ALU.add))

                    xb_ = [xblk0, xblk1]
                    t_ck = [None, None]
                    t_scr_free = None
                    t_rmsn = None
                    t_rms_dma = None
                    t_kpe_ev = None
                    t_ps_s_rd = None
                    t_r2_prev = None
                    t_ps_c_rd = None
                    for b in range(NBLK):
                        sl = xb_[b % 2]
                        slv = cm(sl, NKC)
                        qs.wait(t_ck[b % 2])
                        t_x = qs.done(qs.raw.dma_start(
                            out=slv,
                            in_=xT[:, NB * b:NB * (b + 1)].rearrange("(c p) n -> p c n", p=128)))
                        dve.wait(t_x, t_scr_free)
                        tc = dve.done(dve.raw.tensor_copy(xbf[:, :], sl[:, :]))
                        act.wait(t_x, t_scr_free)
                        ts = act.done(act.raw.activation(out=xsq[:, :], in_=sl[:, :],
                                                         func=AF.Square, bias=0.0, scale=1.0))
                        pe.wait(tc, ts, t_ps_s_rd)
                        for c in range(NKC):
                            pe.raw.matmul(ps_s1[:, 0:NB], ones_b[:, :],
                                          xbf[:, NB * c:NB * (c + 1)],
                                          start=(c == 0), stop=(c == NKC - 1))
                            mmi = pe.raw.matmul(ps_s2[:, 0:NB], ones_b[:, :],
                                                xsq[:, NB * c:NB * (c + 1)],
                                                start=(c == 0), stop=(c == NKC - 1))
                        t_mm = pe.done(mmi)
                        dve.wait(t_mm)
                        t1 = dve.done(dve.raw.tensor_single_scalar(row(SM, NB), ps_s1[:, 0:NB],
                                                                   1.0 / HID, ALU.mult))
                        t2 = dve.done(dve.raw.tensor_single_scalar(row(SQ, NB), ps_s2[:, 0:NB],
                                                                   1.0 / HID, ALU.mult))
                        dve.wait(t1)
                        t3 = dve.done(dve.raw.tensor_tensor(row(SVAR, NB), row(SM, NB),
                                                            row(SM, NB), ALU.mult))
                        dve.wait(t2, t3)
                        t4 = dve.done(dve.raw.tensor_tensor(row(SVAR, NB), row(SQ, NB),
                                                            row(SVAR, NB), ALU.subtract))
                        act.wait(t4)
                        t5 = act.done(act.raw.activation(out=row(SRS, NB), in_=row(SVAR, NB),
                                                         func=AF.Ln, bias=eps_ln_t[:, :], scale=1.0))
                        act.wait(t5)
                        t6 = act.done(act.raw.activation(out=row(SRS, NB), in_=row(SRS, NB),
                                                         func=AF.Exp, bias=0.0, scale=-0.5))
                        pe.wait(t6, t_ps_c_rd)
                        tbm = pe.done(pe.raw.matmul(ps_c0[:, 0:NB], ones_f[:, :], row(SM, NB),
                                                    start=True, stop=True))
                        tbr = pe.done(pe.raw.matmul(ps_c1[:, 0:NB], ones_f[:, :], row(SRS, NB),
                                                    start=True, stop=True))
                        dve.wait(tbm)
                        tem = dve.done(dve.raw.tensor_copy(mb_s[:, :], ps_c0[:, 0:NB]))
                        dve.wait(tbr)
                        ter = dve.done(dve.raw.tensor_copy(rb_s[:, :], ps_c1[:, 0:NB]))
                        mb_b = mb_s[:, :].rearrange("p (o n) -> p o n", o=1) \
                            .broadcast_to([128, NKC, NB])
                        rb_b = rb_s[:, :].rearrange("p (o n) -> p o n", o=1) \
                            .broadcast_to([128, NKC, NB])
                        dve.wait(tem)
                        tn1 = dve.done(dve.raw.tensor_tensor(slv, slv, mb_b, ALU.subtract))
                        dve.wait(tn1, ter, t_ck[b % 2])
                        tln = dve.done(dve.raw.tensor_tensor(cm(ln1b, NKC), slv, rb_b, ALU.mult))
                        dve.wait(tln)
                        t_rp = dve.done(dve.raw.tensor_copy(
                            cm(ln1own, NKC)[:, :, (NB // 4) * b:(NB // 4) * (b + 1)],
                            cm(ln1b, NKC)[:, :, 0:NB:4]))
                        pe.wait(tln, tem, ter, t_wkc, t_wk, t_rmsn, t_kpe_ev)
                        for c in range(NKC):
                            for m in range(4):
                                pe.raw.matmul(cps[m][:, 0:NB],
                                              wkc_t[:, 512 * c + 128 * m:512 * c + 128 * (m + 1)],
                                              ln1b[:, NB * c:NB * (c + 1)],
                                              start=(c == 0), stop=(c == NKC - 1))
                            mmi = pe.raw.matmul(ps_kp[:, 0:NB], weffk[:, DR * c:DR * (c + 1)],
                                                ln1b[:, NB * c:NB * (c + 1)],
                                                start=(c == 0), stop=(c == NKC - 1))
                        t_ckv = pe.done(mmi)
                        t_ck[b % 2] = t_ckv
                        dve.wait(t_ckv)
                        for m in range(4):
                            cpi = dve.raw.tensor_copy(cf32[:, NB * m:NB * (m + 1)], cps[m][:, 0:NB])
                        dve.done(cpi)
                        t_cf = dve.done(dve.raw.tensor_copy(kpefull[0:64, NB * b:NB * (b + 1)],
                                                            ps_kp[:, 0:NB]))
                        t_kpe_ev = t_cf
                        act.wait(t_cf, t_r2_prev)
                        t_csq = act.done(act.raw.activation(out=csq[:, :], in_=cf32[:, :],
                                                            func=AF.Square, bias=0.0, scale=1.0))
                        pe.wait(t_csq)
                        for m in range(4):
                            mmi = pe.raw.matmul(ps_s1[:, 0:NB], ones_b[:, :],
                                                csq[:, NB * m:NB * (m + 1)],
                                                start=(m == 0), stop=(m == 3))
                        t_r2 = pe.done(mmi)
                        t_r2_prev = t_r2
                        act.wait(t_r2, t_ep2)
                        t_lr = act.done(act.raw.activation(out=row(SQ, NB), in_=ps_s1[:, 0:NB],
                                                           func=AF.Ln, bias=eps_rms_t[:, :],
                                                           scale=1.0 / LORA))
                        t_ps_s_rd = t_lr
                        act.wait(t_lr)
                        t_er = act.done(act.raw.activation(out=row(SQ, NB), in_=row(SQ, NB),
                                                           func=AF.Exp, bias=0.0, scale=-0.5))
                        pe.wait(t_er)
                        t_br2 = pe.done(pe.raw.matmul(ps_c0[:, 0:NB], ones_f[:, :], row(SQ, NB),
                                                      start=True, stop=True))
                        dve.wait(t_br2)
                        t_err = dve.done(dve.raw.tensor_copy(rr_s[:, :], ps_c0[:, 0:NB]))
                        t_ps_c_rd = t_err
                        dve.wait(t_err, t_rms_dma)
                        for m in range(4):
                            sti = dve.raw.scalar_tensor_tensor(
                                rmsb[:, NB * m:NB * (m + 1)], cf32[:, NB * m:NB * (m + 1)],
                                kvw_t[:, m:m + 1], rr_s[:, :], ALU.mult, ALU.mult)
                        t_rmsn = dve.done(sti)
                        qs.wait(t_rmsn)
                        t_rms_dma = qs.done(qs.raw.dma_start(
                            out=rms_d[:, NB * b:NB * (b + 1)].rearrange("(m p) n -> p m n", p=128),
                            in_=cm(rmsb, 4)))
                    t_kones = dve.done(dve.raw.memset(kpefull[64:65, :], 1.0))
                    em.barrier()
                if DBG:
                    qg.done(qg.raw.dma_start(
                        out=dbg["d_ln1own"].rearrange("(c p) n -> p c n", p=128),
                        in_=cm(ln1own, NKC)))
                    qg.done(qg.raw.dma_start(out=dbg["d_kpe"], in_=kpefull[0:64, :]))
                    t_ = qg.done(qg.raw.dma_start(out=dbg["d_rms"], in_=rms_d))
                    qg.wait(t_)
                    em.barrier()

                qn = bd_scope.enter_context(nc.sbuf_tensor("qn", [128, NH * NT], BF16,
                                                           side="right"))
                qph = bd_scope.enter_context(nc.sbuf_tensor("qph", [65, NH * NT], BF16,
                                                            side="right"))
                with ExitStack() as ph:
                    P = ph.enter_context
                    wtam0 = P(nc.sbuf_tensor("wtam0", [128, 4096], F32))
                    wtam1 = P(nc.sbuf_tensor("wtam1", [128, 4096], F32))
                    racc = P(nc.sbuf_tensor("racc", [128, 2], F32))
                    rtmp = P(nc.sbuf_tensor("rtmp", [128, 1], F32))
                    t_amz = dve.done(dve.raw.memset(racc[:, :], 0.0))
                    am_tiles = []
                    for wi, w_ap in enumerate([wfcT, wpjT]):
                        flat = w_ap.rearrange("a b -> (a b)").rearrange(
                            "(t p n) -> t p n", p=128, n=4096)
                        for ti in range(flat.shape[0]):
                            am_tiles.append((wi, flat[ti]))
                    am_state = {"i": 0, "slot_tok": [None, None], "acc": t_amz}

                    def am_step(k=1):
                        for _ in range(k):
                            i = am_state["i"]
                            if i >= len(am_tiles):
                                return
                            wi, tile = am_tiles[i]
                            sl = i % 2
                            bufw = [wtam0, wtam1][sl]
                            qa.wait(am_state["slot_tok"][sl])
                            td = qa.done(qa.raw.dma_start(out=bufw[:, :], in_=tile))
                            dve.wait(td, am_state["acc"])
                            tr = dve.done(dve.raw.tensor_reduce(
                                rtmp[:, :], bufw[:, :], axis=AX.X, op=ALU.add,
                                apply_absolute_value=True))
                            dve.wait(tr)
                            ta = dve.done(dve.raw.tensor_tensor(
                                racc[:, wi:wi + 1], racc[:, wi:wi + 1], rtmp[:, :], ALU.add))
                            am_state["acc"] = ta
                            am_state["slot_tok"][sl] = ta
                            am_state["i"] = i + 1

                    weffq = P(nc.sbuf_tensor("weffq", [128, NKC * 128], BF16))
                    wqs0 = P(nc.sbuf_tensor("wqs0", [128, NKC * 128], BF16))
                    wqs1 = P(nc.sbuf_tensor("wqs1", [128, NKC * 128], BF16))
                    wqpf0 = P(nc.sbuf_tensor("wqpf0", [128, NKC * 128], F32))
                    wqpf1 = P(nc.sbuf_tensor("wqpf1", [128, NKC * 128], F32))
                    wqtmp = P(nc.sbuf_tensor("wqtmp", [128, NKC * 128], F32))
                    weffpf = P(nc.sbuf_tensor("weffpf", [128, NKC * 128], F32))
                    ps_q0 = P(nc.psum_tensor("ps_q0", [128, 512], F32))
                    ps_q1 = P(nc.psum_tensor("ps_q1", [128, 512], F32))

                    t_dma = [None, None]
                    t_cons = [None, None]
                    t_weff_cons = None
                    t_qslot = [None, None]
                    for p in range(8):
                        am_step(2)
                        buf = [wqpf0, wqpf1][p % 2]
                        qs.wait(t_cons[p % 2])
                        t_dma[p % 2] = qs.done(qs.raw.dma_start(
                            out=cm(buf, NKC),
                            in_=wqp[:, 128 * p:128 * (p + 1)].rearrange("(c pp) n -> pp c n",
                                                                        pp=128)))
                        dve.wait(t_dma[p % 2], t_weff_cons)
                        w4 = buf[:, :].rearrange("pp (c h t d) -> pp c h t d", c=NKC, h=2, t=2)
                        cb4 = cosb[:, :].rearrange("pp (o t d) -> pp o t d", o=1, t=2) \
                            .rearrange("pp o t d -> pp o () t d") \
                            .broadcast_to([128, NKC, 2, 2, 32])
                        sb4 = spb[:, :].rearrange("pp (o t d) -> pp o t d", o=1, t=2) \
                            .rearrange("pp o t d -> pp o () t d") \
                            .broadcast_to([128, NKC, 2, 2, 32])
                        ta = dve.done(dve.raw.tensor_tensor(
                            wqtmp[:, :].rearrange("pp (c h t d) -> pp c h t d", c=NKC, h=2, t=2),
                            w4, cb4, ALU.mult))
                        tb = dve.done(dve.raw.tensor_tensor(
                            weffpf[:, :].rearrange("pp (c h t d) -> pp c h t d", c=NKC, h=2, t=2),
                            w4[:, :, :, 1::-1, :], sb4, ALU.mult))
                        t_cons[p % 2] = tb
                        dve.wait(ta, tb)
                        tcq = dve.done(dve.raw.tensor_tensor(weffq[:, :], weffpf[:, :],
                                                             wqtmp[:, :], ALU.add))
                        for hh in range(2):
                            h = 2 * p + hh
                            for tt in range(2):
                                pst = [ps_q0, ps_q1][tt]
                                pe.wait(tcq, t_qslot[tt])
                                for c in range(NKC):
                                    mmi = pe.raw.matmul(
                                        pst[0:64, :],
                                        weffq[:, 128 * c + 64 * hh:128 * c + 64 * (hh + 1)],
                                        ln1own[:, NT * c + 512 * tt:NT * c + 512 * (tt + 1)],
                                        start=(c == 0), stop=(c == NKC - 1))
                                tq = pe.done(mmi)
                                if hh == 1 and tt == 1:
                                    t_weff_cons = tq
                                dve.wait(tq)
                                tev = dve.done(dve.raw.tensor_copy(
                                    qph[0:64, NT * h + 512 * tt:NT * h + 512 * (tt + 1)],
                                    pst[0:64, :]))
                                t_qslot[tt] = tev
                    t_dma = [None, None]
                    t_cons = [None, None]
                    for h in range(NH):
                        am_step(2)
                        buf = [wqs0, wqs1][h % 2]
                        qg.wait(t_cons[h % 2])
                        t_dma[h % 2] = qg.done(qg.raw.dma_start(
                            out=cm(buf, NKC),
                            in_=wqn[:, 128 * h:128 * (h + 1)].rearrange("(c p) n -> p c n",
                                                                        p=128)))
                        for tt in range(2):
                            pst = [ps_q0, ps_q1][tt]
                            pe.wait(t_dma[h % 2], t_qslot[tt])
                            for c in range(NKC):
                                mmi = pe.raw.matmul(
                                    pst[:, :], buf[:, 128 * c:128 * (c + 1)],
                                    ln1own[:, NT * c + 512 * tt:NT * c + 512 * (tt + 1)],
                                    start=(c == 0), stop=(c == NKC - 1))
                            tq = pe.done(mmi)
                            if tt == 1:
                                t_cons[h % 2] = tq
                            dve.wait(tq)
                            tev = dve.done(dve.raw.tensor_copy(
                                qn[:, NT * h + 512 * tt:NT * h + 512 * (tt + 1)], pst[:, :]))
                            t_qslot[tt] = tev
                    ps_am = ph.enter_context(nc.psum_tensor("ps_am", [128, 512], F32))
                    am_step(100)
                    t_z = am_state["acc"]
                    for wi in range(2):
                        pe.wait(t_z)
                        tmm = pe.done(pe.raw.matmul(ps_am[0:1, 0:1], racc[:, wi:wi + 1],
                                                    onesc_f[:, :], start=True, stop=True))
                        dve.wait(tmm)
                        t1 = dve.done(dve.raw.tensor_single_scalar(
                            amv[:, wi:wi + 1], ps_am[0:1, 0:1], 1.0 / (HID * HF), ALU.mult))
                        dve.wait(t1)
                        t2 = dve.done(dve.raw.tensor_single_scalar(
                            amv[:, wi:wi + 1], amv[:, wi:wi + 1], 1e-5, ALU.max))
                        pe.wait(t2)
                    t1 = dve.done(dve.raw.tensor_tensor(amv[:, 2:3], amv[:, 0:1], amv[:, 0:1],
                                                        ALU.mult))
                    dve.wait(t1)
                    t2 = dve.done(dve.raw.tensor_tensor(amv[:, 2:3], amv[:, 2:3], amv[:, 1:2],
                                                        ALU.mult))
                    dve.wait(t2)
                    t3 = dve.done(dve.raw.tensor_single_scalar(amv[:, 2:3], amv[:, 2:3],
                                                               0.125, ALU.mult))
                    for (dst, col, src_, sc) in [(thr_fc, 0, 0, -0.7), (thr_fc, 1, 0, 0.7),
                                                 (thr_pj, 0, 1, -0.7), (thr_pj, 1, 1, 0.7)]:
                        dve.wait(t3)
                        tsc = dve.done(dve.raw.tensor_single_scalar(
                            scr[:, 2:3], amv[:, src_:src_ + 1], sc, ALU.mult))
                        pe.wait(tsc)
                        tmm = pe.done(pe.raw.matmul(ps_am[:, 0:1], ones_f[:, :], scr[:, 2:3],
                                                    start=True, stop=True))
                        dve.wait(tmm)
                        tev = dve.done(dve.raw.tensor_copy(dst[:, col:col + 1], ps_am[:, 0:1]))
                        pe.wait(tev)
                    pe.wait(t3)
                    tmm = pe.done(pe.raw.matmul(ps_am[:, 0:1], ones_f[:, :], amv[:, 2:3],
                                                start=True, stop=True))
                    dve.wait(tmm)
                    tev = dve.done(dve.raw.tensor_copy(cy_b[:, :], ps_am[:, 0:1]))
                    pe.wait(tev)
                em.barrier()
            if DBG:
                qg.done(qg.raw.dma_start(out=dbg["d_qn"].rearrange("(h p) n -> p h n", p=128),
                                          in_=cm(qn, NH)))
                t_ = qg.done(qg.raw.dma_start(
                    out=dbg["d_qp"].rearrange("(h pp) n -> pp h n", pp=64),
                    in_=qph[0:64, :].rearrange("pp (h n) -> pp h n", h=NH)))
                qg.wait(t_)
                em.barrier()

            with ExitStack() as ph:
                P = ph.enter_context
                kq = P(nc.sbuf_tensor("kq", [64, 512], BF16))
                ps_n = P(nc.psum_tensor("ps_n", [1, 512], F32))
                tsg = None
                for sseg in range(8):
                    tsq = dve.done(dve.raw.tensor_tensor(
                        kq[:, :], kpefull[0:64, 512 * sseg:512 * (sseg + 1)],
                        kpefull[0:64, 512 * sseg:512 * (sseg + 1)], ALU.mult))
                    pe.wait(tsq, tsg)
                    t1 = pe.done(pe.raw.matmul(ps_n[:, :], ones_b[0:64, :], kq[:, :],
                                               start=True, stop=True))
                    dve.wait(t1)
                    tsg = dve.done(dve.raw.tensor_reduce(scr[:, 8 + sseg:9 + sseg],
                                                         ps_n[:, :], axis=AX.X, op=ALU.max))
                dve.wait(tsg)
                t1 = dve.done(dve.raw.tensor_tensor(scr[:, 4:5], scr[:, 8:9], scr[:, 9:10],
                                                    ALU.max))
                dve.wait(t1)
                for tp in range(1, 4):
                    ta = dve.done(dve.raw.tensor_tensor(scr[:, 4 + tp:5 + tp],
                                                        scr[:, 3 + tp:4 + tp],
                                                        scr[:, 8 + 2 * tp:9 + 2 * tp], ALU.max))
                    dve.wait(ta)
                    tb = dve.done(dve.raw.tensor_tensor(scr[:, 4 + tp:5 + tp],
                                                        scr[:, 4 + tp:5 + tp],
                                                        scr[:, 9 + 2 * tp:10 + 2 * tp], ALU.max))
                    dve.wait(tb)
            em.barrier()

            em.barrier()
            em.barrier()
            if DBG:
                t_ = qg.done(qg.raw.dma_start(out=dbg["d_am"], in_=amv[0:1, :]))
                qg.wait(t_)

            with ExitStack() as ph:
                P = ph.enter_context
                kng = P(nc.sbuf_tensor("kng", [128, 4 * S], BF16))
                vbuf = P(nc.sbuf_tensor("vbuf", [128, 32 * 512], BF16))
                wkbk_g = P(nc.sbuf_tensor("wkbk_g", [128, 4 * 512], BF16))
                wkbv_g = P(nc.sbuf_tensor("wkbv_g", [128, 4 * 512], BF16))
                rmst0 = P(nc.sbuf_tensor("rmst0", [128, 4 * 512], BF16))
                rmst1 = P(nc.sbuf_tensor("rmst1", [128, 4 * 512], BF16))
                pts6 = [P(nc.sbuf_tensor(f"pt{i}", [128, 512], BF16)) for i in range(6)]
                ksq = P(nc.sbuf_tensor("ksq", [128, 512], BF16))
                ksq2 = P(nc.sbuf_tensor("ksq2", [128, NT], BF16))
                nqrow = P(nc.sbuf_tensor("nqrow", [1, NT], F32))
                nqrow2 = P(nc.sbuf_tensor("nqrow2", [1, NT], F32))
                sr_at = P(nc.sbuf_tensor("sr_at", [1, 1024], F32))
                rb2_s = P(nc.sbuf_tensor("rb2_s", [128, 512], F32))
                onrm0 = P(nc.sbuf_tensor("onrm0", [128, 256], BF16))
                onrm1 = P(nc.sbuf_tensor("onrm1", [128, 256], BF16))
                ps_SA = P(nc.psum_tensor("ps_SA", [128, 512], F32))
                ps_SB = P(nc.psum_tensor("ps_SB", [128, 512], F32))
                ps_SC = P(nc.psum_tensor("ps_SC", [128, 512], F32))
                ps_o0 = P(nc.psum_tensor("ps_o0", [128, 256], F32))
                ps_o1 = P(nc.psum_tensor("ps_o1", [128, 256], F32))
                ps_sm0 = P(nc.psum_tensor("ps_sm0", [1, 256], F32))
                ps_sm1 = P(nc.psum_tensor("ps_sm1", [1, 256], F32))
                ps_kv = P(nc.psum_tensor("ps_kv", [128, 512], F32))

                for g in range(4):
                    t_wk1 = qg.done(qg.raw.dma_start(
                        out=cm(wkbk_g, 4),
                        in_=wkbk[:, 512 * g:512 * (g + 1)].rearrange("(c p) n -> p c n", p=128)))
                    t_wk2 = qg.done(qg.raw.dma_start(
                        out=cm(wkbv_g, 4),
                        in_=wkbv[:, 512 * g:512 * (g + 1)].rearrange("(c p) n -> p c n", p=128)))
                    rb_ = [rmst0, rmst1]
                    t_rdma = [None, None]
                    t_rfree = [None, None]
                    kvbanks = [ps_SA, ps_SB, ps_SC, ps_kv]
                    t_kvslot = [None, None, None, None]
                    ki = 0
                    t_ev = None
                    for s in range(8):
                        bufr = rb_[s % 2]
                        qs.wait(t_rfree[s % 2])
                        t_rdma[s % 2] = qs.done(qs.raw.dma_start(
                            out=cm(bufr, 4),
                            in_=rms_d[:, 512 * s:512 * (s + 1)].rearrange("(c p) n -> p c n",
                                                                          p=128)))
                        pe.wait(t_rdma[s % 2], t_wk1, t_wk2)
                        for h4 in range(4):
                            bank = kvbanks[ki % 4]
                            pe.wait(t_kvslot[ki % 4])
                            for c in range(4):
                                mmi = pe.raw.matmul(
                                    bank[:, :],
                                    wkbk_g[:, 512 * c + 128 * h4:512 * c + 128 * (h4 + 1)],
                                    bufr[:, 512 * c:512 * (c + 1)], start=(c == 0), stop=(c == 3))
                            tkn = pe.done(mmi)
                            dve.wait(tkn)
                            t_ev = dve.done(dve.raw.tensor_copy(
                                kng[:, S * h4 + 512 * s:S * h4 + 512 * (s + 1)], bank[:, :]))
                            t_kvslot[ki % 4] = t_ev
                            ki += 1
                        for sub in range(4):
                            bank = kvbanks[ki % 4]
                            pe.wait(t_kvslot[ki % 4])
                            for c in range(4):
                                mmi = pe.raw.matmul(
                                    bank[:, :],
                                    bufr[:, 512 * c + 128 * sub:512 * c + 128 * (sub + 1)],
                                    wkbv_g[:, 512 * c:512 * (c + 1)], start=(c == 0), stop=(c == 3))
                            tv = pe.done(mmi)
                            dve.wait(tv)
                            t_ev = dve.done(dve.raw.tensor_copy(
                                vbuf[:, 512 * (4 * s + sub):512 * (4 * s + sub + 1)],
                                bank[:, :]))
                            t_kvslot[ki % 4] = t_ev
                            ki += 1
                        t_rfree[s % 2] = t_ev
                    tzk = dve.done(dve.raw.memset(scr[:, 8:16], 0.0))
                    for s in range(8):
                        for h4 in range(4):
                            tsq = dve.done(dve.raw.tensor_tensor(
                                ksq[:, :], kng[:, S * h4 + 512 * s:S * h4 + 512 * (s + 1)],
                                kng[:, S * h4 + 512 * s:S * h4 + 512 * (s + 1)], ALU.mult))
                            pe.wait(tsq, tzk)
                            t1 = pe.done(pe.raw.matmul(ps_kv[0:1, 0:512], ones_b[:, :],
                                                       ksq[:, :], start=True, stop=True))
                            dve.wait(t1)
                            tm = dve.done(dve.raw.tensor_reduce(scr[:, 1:2], ps_kv[0:1, 0:512],
                                                                axis=AX.X, op=ALU.max))
                            dve.wait(tm)
                            tzk = dve.done(dve.raw.tensor_tensor(scr[:, 8 + s:9 + s],
                                                                 scr[:, 8 + s:9 + s],
                                                                 scr[:, 1:2], ALU.max))
                            pe.wait(tzk)
                    dve.wait(tzk)
                    tpv = dve.done(dve.raw.tensor_tensor(scr[:, 0:1], scr[:, 8:9], scr[:, 9:10],
                                                         ALU.max))
                    for tp in range(1, 4):
                        dve.wait(tpv)
                        ta = dve.done(dve.raw.tensor_tensor(scr[:, tp:tp + 1],
                                                            scr[:, tp - 1:tp],
                                                            scr[:, 8 + 2 * tp:9 + 2 * tp],
                                                            ALU.max))
                        dve.wait(ta)
                        tpv = dve.done(dve.raw.tensor_tensor(scr[:, tp:tp + 1], scr[:, tp:tp + 1],
                                                             scr[:, 9 + 2 * tp:10 + 2 * tp],
                                                             ALU.max))
                    dve.wait(tpv)
                    tkk = dve.done(dve.raw.tensor_tensor(scr[:, 0:4], scr[:, 0:4], scr[:, 4:8],
                                                         ALU.add))
                    act.wait(tkk)
                    t1 = act.done(act.raw.activation(out=scr[:, 8:12], in_=scr[:, 0:4],
                                                     func=AF.Ln, bias=eps_rms_t[:, :], scale=1.0))
                    act.wait(t1)
                    t2 = act.done(act.raw.activation(out=scr[:, 8:12], in_=scr[:, 8:12],
                                                     func=AF.Exp, bias=0.0, scale=0.5))
                    dve.wait(t2)
                    tnk = dve.done(dve.raw.tensor_single_scalar(scr[:, 8:12], scr[:, 8:12],
                                                                -1.0, ALU.mult))
                    t_shift_dma = {}
                    for h4 in range(4):
                        H = 4 * g + h4
                        tq1 = dve.done(dve.raw.tensor_tensor(ksq2[:, :], qn[:, NT * H:NT * (H + 1)],
                                                             qn[:, NT * H:NT * (H + 1)], ALU.mult))
                        pe.wait(tq1)
                        tm1 = pe.done(pe.raw.matmul(ps_kv[0:1, 0:512], ones_b[:, :],
                                                    ksq2[:, 0:512], start=True, stop=True))
                        dve.wait(tm1)
                        te1 = dve.done(dve.raw.tensor_copy(nqrow[:, 0:512], ps_kv[0:1, 0:512]))
                        pe.wait(te1)
                        tm2b = pe.done(pe.raw.matmul(ps_kv[0:1, 0:512], ones_b[:, :],
                                                     ksq2[:, 512:1024], start=True, stop=True))
                        dve.wait(tm2b)
                        te2 = dve.done(dve.raw.tensor_copy(nqrow[:, 512:1024], ps_kv[0:1, 0:512]))
                        pe.wait(te2)
                        tq2 = dve.done(dve.raw.tensor_tensor(ksq2[0:64, 0:NT],
                                                             qph[0:64, NT * H:NT * (H + 1)],
                                                             qph[0:64, NT * H:NT * (H + 1)],
                                                             ALU.mult))
                        for tt in range(2):
                            pe.wait(tq2)
                            tm3 = pe.done(pe.raw.matmul(ps_kv[0:1, 0:512], ones_b[0:64, :],
                                                        ksq2[0:64, 512 * tt:512 * (tt + 1)],
                                                        start=True, stop=True))
                            dve.wait(tm3)
                            te3 = dve.done(dve.raw.tensor_tensor(
                                nqrow[:, 512 * tt:512 * (tt + 1)],
                                nqrow[:, 512 * tt:512 * (tt + 1)],
                                ps_kv[0:1, 0:512], ALU.add))
                            pe.wait(te3)
                        act.wait(te3)
                        ta1 = act.done(act.raw.activation(out=nqrow[:, :], in_=nqrow[:, :],
                                                          func=AF.Ln, bias=eps_rms_t[:, :],
                                                          scale=1.0))
                        act.wait(ta1)
                        ta2 = act.done(act.raw.activation(out=nqrow[:, :], in_=nqrow[:, :],
                                                          func=AF.Exp, bias=0.0, scale=0.5))
                        dve.wait(ta2, tnk, t_shift_dma.get(H - 1))
                        tsl = None
                        for tp in range(4):
                            tsl = dve.done(dve.raw.tensor_scalar(
                                out=nqrow2[:, 256 * tp:256 * (tp + 1)],
                                in0=nqrow[:, 256 * tp:256 * (tp + 1)],
                                scalar1=scr[:, 8 + tp:9 + tp], scalar2=float(50.0 / SCALE),
                                op0=ALU.mult, op1=ALU.add))
                        qg.wait(tsl)
                        t_shift_dma[H] = qg.done(qg.raw.dma_start(
                            out=qph[64:65, NT * H:NT * (H + 1)], in_=nqrow2[:, :]))
                    if DBG and g == 0:
                        t_ = qg.done(qg.raw.dma_start(
                            out=dbg["d_kn0"].rearrange("(h p) n -> p h n", p=128),
                            in_=cm(kng, 4)))
                        qg.wait(t_)
                    t_pt_free = [None] * 6
                    t_Sfree = list(t_kvslot[0:3])
                    t_sm_free = [None, None]
                    t_o_free = [None, None]
                    t_rb_free = t_kvslot[3]
                    t_ot_dma = [None, None]
                    pti = 0
                    sti = 0
                    epar = 0
                    epi = None
                    Sbanks = [ps_SA, ps_SB, ps_SC]
                    ps_os = [ps_o0, ps_o1]
                    ps_sms = [ps_sm0, ps_sm1]
                    onrms = [onrm0, onrm1]
                    t_attn_kv_ready = t_ev

                    def emit_spv(pend, par, nkb, h4, first, epi_local=None):
                        slot, st0, t_e, t_m = pend
                        pe.wait(t_e, t_m)
                        if first:
                            pe.wait(t_sm_free[par], t_o_free[par])
                        for i in range(2):
                            kb = 2 * st0 + i
                            pe.raw.matmul(ps_sms[par][:, :], ones_s[:, :],
                                          pts6[slot][:, 256 * i:256 * (i + 1)],
                                          start=(kb == 0), stop=(kb == nkb - 1))
                            mmi = pe.raw.matmul(
                                ps_os[par][:, :],
                                vbuf[:, 512 * kb + 128 * h4:512 * kb + 128 * (h4 + 1)],
                                pts6[slot][:, 256 * i:256 * (i + 1)],
                                start=(kb == 0), stop=(kb == nkb - 1))
                        t_pv = pe.done(mmi)
                        t_pt_free[slot] = t_pv
                        return t_pv

                    def flush_epilogue(e):
                        nonlocal t_rb_free
                        par = e["par"]
                        pe.wait(e["t_r"], t_rb_free)
                        t_bb = pe.done(pe.raw.matmul(ps_kv[:, 0:256], ones_f[:, :],
                                                     sr_at[:, 512 + 256 * par:768 + 256 * par],
                                                     start=True, stop=True))
                        dve.wait(t_bb)
                        t_rbe = dve.done(dve.raw.tensor_copy(
                            rb2_s[:, 256 * par:256 * (par + 1)], ps_kv[:, 0:256]))
                        t_rb_free = t_rbe
                        dve.wait(t_rbe, t_ot_dma[par])
                        t_on = dve.done(dve.raw.tensor_tensor(
                            onrms[par][:, :], ps_os[par][:, :],
                            rb2_s[:, 256 * par:256 * (par + 1)], ALU.mult))
                        t_o_free[par] = t_on
                        qg.wait(t_on)
                        t_ot_dma[par] = qg.done(qg.raw.dma_start(
                            out=ot_d[128 * e["H"]:128 * (e["H"] + 1),
                                     256 * e["tp"]:256 * (e["tp"] + 1)],
                            in_=onrms[par][:, :]))

                    for tp in range(4):
                        nkb = 8 * (tp + 1)
                        for h4 in range(4):
                            H = 4 * g + h4
                            qrhs_n = qn[:, NT * H + 256 * tp:NT * H + 256 * (tp + 1)]
                            qrhs_p = qph[0:65, NT * H + 256 * tp:NT * H + 256 * (tp + 1)]
                            pe.wait(t_shift_dma[H], t_attn_kv_ready)
                            ntile = nkb // 2
                            pq = []
                            par = epar % 2
                            first_spv = True
                            for st in range(ntile):
                                sb = sti % 3
                                sti += 1
                                Sps = Sbanks[sb]
                                pe.wait(t_Sfree[sb])
                                for i in range(2):
                                    kb = 2 * st + i
                                    colo = 256 * i
                                    pe.raw.matmul(
                                        Sps[:, colo:colo + 256],
                                        kng[:, S * h4 + 128 * kb:S * h4 + 128 * (kb + 1)],
                                        qrhs_n, start=(i == 0), stop=False)
                                    mmi = pe.raw.matmul(
                                        Sps[:, colo:colo + 256],
                                        kpefull[0:65, 128 * kb:128 * (kb + 1)],
                                        qrhs_p, start=False, stop=(i == 1))
                                t_S = pe.done(mmi)
                                if len(pq) >= 2:
                                    emit_spv(pq.pop(0), par, nkb, h4, first_spv)
                                    first_spv = False
                                if st == 0 and epi is not None:
                                    flush_epilogue(epi)
                                    epi = None
                                slot = pti % 6
                                pti += 1
                                act.wait(t_S, t_pt_free[slot])
                                t_e = act.done(act.raw.activation(out=pts6[slot][:, :],
                                                                  in_=Sps[:, :], func=AF.Exp,
                                                                  bias=0.0, scale=SCALE))
                                t_Sfree[sb] = t_e
                                t_m = None
                                for i in range(2):
                                    kb = 2 * st + i
                                    j = kb - (nkb - 8)
                                    if j >= 0:
                                        dve.wait(t_e)
                                        t_m = dve.done(dve.raw.tensor_tensor(
                                            pts6[slot][:, 256 * i:256 * (i + 1)],
                                            pts6[slot][:, 256 * i:256 * (i + 1)],
                                            masks[:, 256 * j:256 * (j + 1)], ALU.mult))
                                pq.append((slot, st, t_e, t_m))
                            for pend in pq:
                                t_pv = emit_spv(pend, par, nkb, h4, first_spv)
                                first_spv = False
                            pq = []
                            dve.wait(t_pv, t_rb_free if False else None)
                            t_sev = dve.done(dve.raw.tensor_copy(
                                sr_at[:, 256 * par:256 * (par + 1)], ps_sms[par][:, :]))
                            t_sm_free[par] = t_sev
                            act.wait(t_sev)
                            t_l = act.done(act.raw.activation(
                                out=sr_at[:, 512 + 256 * par:768 + 256 * par],
                                in_=sr_at[:, 256 * par:256 * (par + 1)], func=AF.Ln,
                                bias=0.0, scale=1.0))
                            act.wait(t_l)
                            t_r = act.done(act.raw.activation(
                                out=sr_at[:, 512 + 256 * par:768 + 256 * par],
                                in_=sr_at[:, 512 + 256 * par:768 + 256 * par], func=AF.Exp,
                                bias=ln2n30[:, :], scale=-1.0))
                            if epi is not None:
                                flush_epilogue(epi)
                            epi = dict(par=par, t_r=t_r, H=H, tp=tp)
                            epar += 1
                    flush_epilogue(epi)
                    epi = None
                    em.barrier()
            em.barrier()
        if DBG:
            t_ = qg.done(qg.raw.dma_start(out=dbg["d_ot"], in_=ot_d))
            qg.wait(t_)
            em.barrier()

        NTH = NT // 2
        eg_scope = ctx.enter_context(ExitStack())
        ln2 = eg_scope.enter_context(nc.sbuf_tensor("ln2", [128, NKC * NT], BF16))
        with ExitStack() as ph:
            P = ph.enter_context
            ots = P(nc.sbuf_tensor("ots", [128, NKC * NT], BF16))
            attn = P(nc.sbuf_tensor("attn", [128, NKC * NTH], F32))
            xow = P(nc.sbuf_tensor("xow", [128, NKC * NTH], F32))
            wos0 = P(nc.sbuf_tensor("wos0", [128, NKC * 128], BF16))
            wos1 = P(nc.sbuf_tensor("wos1", [128, NKC * 128], BF16))
            bfc = P(nc.sbuf_tensor("bfc", [128, 1024], BF16))
            sqc = P(nc.sbuf_tensor("sqc", [128, 1024], BF16))
            mb2 = P(nc.sbuf_tensor("mb2", [128, 512], F32))
            rb3 = P(nc.sbuf_tensor("rb3", [128, 512], F32))
            ps_e0 = P(nc.psum_tensor("ps_e0", [128, 512], F32))
            ps_e1 = P(nc.psum_tensor("ps_e1", [128, 512], F32))
            ps_f1 = P(nc.psum_tensor("ps_f1", [1, 512], F32))
            ps_f2 = P(nc.psum_tensor("ps_f2", [1, 512], F32))
            t_ot = qs.done(qs.raw.dma_start(out=cm(ots, NKC),
                                            in_=ot_d.rearrange("(c p) n -> p c n", p=128)))
            t_x1dma = None
            for hf in range(2):
                t_xo = qs.done(qs.raw.dma_start(
                    out=cm(xow, NKC),
                    in_=x_own[:, NTH * hf:NTH * (hf + 1)].rearrange("(c p) n -> p c n", p=128)))
                t_wdma = [None, None]
                t_wfree = [None, None]
                t_eslot = [None, None]
                for m in range(NKC):
                    bw = [wos0, wos1][m % 2]
                    qg.wait(t_wfree[m % 2])
                    t_wdma[m % 2] = qg.done(qg.raw.dma_start(
                        out=cm(bw, NKC),
                        in_=woT[:, 128 * m:128 * (m + 1)].rearrange("(c p) n -> p c n", p=128)))
                    pst = [ps_e0, ps_e1][m % 2]
                    pe.wait(t_wdma[m % 2], t_ot, t_eslot[m % 2])
                    for c in range(NKC):
                        mmi = pe.raw.matmul(pst[:, :], bw[:, 128 * c:128 * (c + 1)],
                                            ots[:, NT * c + NTH * hf:NT * c + NTH * (hf + 1)],
                                            start=(c == 0), stop=(c == NKC - 1))
                    tq = pe.done(mmi)
                    t_wfree[m % 2] = tq
                    dve.wait(tq)
                    tev = dve.done(dve.raw.tensor_copy(attn[:, NTH * m:NTH * (m + 1)],
                                                       pst[:, :]))
                    t_eslot[m % 2] = tev
                em.barrier()
                av = cm(attn, NKC)
                t_rs = ln_stats(av, NTH, NKC, HID, eps_ln_t, bfc, sqc, ps_f1, ps_f2)
                dve.wait(t_xo)
                tn, _ = ln_norm(av, NTH, NKC, mb2, rb3, ps_e0, ps_e1, t_rs,
                                out_view=cm(xow, NKC), add_view=cm(xow, NKC))
                qs.wait(tn)
                t_x1dma = qs.done(qs.raw.dma_start(
                    out=x1_d[:, NTH * hf:NTH * (hf + 1)].rearrange("(c p) n -> p c n", p=128),
                    in_=cm(xow, NKC)))
                t_rs2 = ln_stats(cm(xow, NKC), NTH, NKC, HID, eps_ln_t, bfc, sqc, ps_f1, ps_f2)
                dve.wait(t_x1dma)
                tn2, _ = ln_norm(cm(xow, NKC), NTH, NKC, mb2, rb3, ps_e0, ps_e1, t_rs2,
                                 out_view=cm(ln2, NKC)[:, :, NTH * hf:NTH * (hf + 1)])
                em.barrier()
        if DBG:
            qg.done(qg.raw.dma_start(out=dbg["d_x1"].rearrange("(c p) n -> p c n", p=128),
                                      in_=x1_d.rearrange("(c p) n -> p c n", p=128)))
            t_ = qg.done(qg.raw.dma_start(
                out=dbg["d_ln2"].rearrange("(c p) n -> p c n", p=128), in_=cm(ln2, NKC)))
            qg.wait(t_)
            em.barrier()

        NTH = NT // 2
        t_out_final = None
        with ExitStack() as ph:
            P = ph.enter_context
            h2 = P(nc.sbuf_tensor("h2", [128, 40 * NTH], BF16))
            yb = P(nc.sbuf_tensor("yb", [128, NKC * NTH], F32))
            ws0 = P(nc.sbuf_tensor("ws0", [128, NKC * 128], F32))
            ws1 = P(nc.sbuf_tensor("ws1", [128, NKC * 128], F32))
            ta1 = P(nc.sbuf_tensor("ta1", [128, NKC * 128], BF16))
            tern = P(nc.sbuf_tensor("tern", [128, NKC * 128], BF16))
            wp0 = P(nc.sbuf_tensor("wp0", [128, 10 * 128], F32))
            wp1 = P(nc.sbuf_tensor("wp1", [128, 10 * 128], F32))
            tpa1 = P(nc.sbuf_tensor("tpa1", [128, 10 * 128], BF16))
            ternp = P(nc.sbuf_tensor("ternp", [128, 10 * 128], BF16))
            hfp = P(nc.sbuf_tensor("hfp", [128, NTH], F32))
            x1h = P(nc.sbuf_tensor("x1h", [128, NKC * NTH], F32))
            bfc2 = P(nc.sbuf_tensor("bfc2", [128, 1024], BF16))
            sqc2 = P(nc.sbuf_tensor("sqc2", [128, 1024], BF16))
            mb4 = P(nc.sbuf_tensor("mb4", [128, 512], F32))
            rb4 = P(nc.sbuf_tensor("rb4", [128, 512], F32))
            ps_g0 = P(nc.psum_tensor("ps_g0", [128, 512], F32))
            ps_g1 = P(nc.psum_tensor("ps_g1", [128, 512], F32))
            ps_h1 = P(nc.psum_tensor("ps_h1", [1, 512], F32))
            ps_h2 = P(nc.psum_tensor("ps_h2", [1, 512], F32))
            for hf in range(2):
                t_wfree = [None, None]
                t_h2w = None
                t_fcmm = None
                t_fctt = None
                t_gslot = [None, None]
                for m in range(40):
                    bw = [ws0, ws1][m % 2]
                    qs.wait(t_wfree[m % 2])
                    t_d = qs.done(qs.raw.dma_start(
                        out=cm(bw, NKC),
                        in_=wfcT[:, 128 * m:128 * (m + 1)].rearrange("(c p) n -> p c n", p=128)))
                    act.wait(t_d, t_fcmm, t_fctt)
                    tsa = act.done(act.raw.activation(out=ta1[:, :], in_=bw[:, :], func=AF.Sign,
                                                      bias=thr_fc[:, 0:1], scale=1.0))
                    tsb = act.done(act.raw.activation(out=tern[:, :], in_=bw[:, :], func=AF.Sign,
                                                      bias=thr_fc[:, 1:2], scale=1.0))
                    t_wfree[m % 2] = tsb
                    dve.wait(tsa, tsb)
                    tt_ = dve.done(dve.raw.tensor_tensor(tern[:, :], tern[:, :], ta1[:, :],
                                                         ALU.add))
                    t_fctt = tt_
                    pst = [ps_g0, ps_g1][m % 2]
                    pe.wait(tt_, t_gslot[m % 2])
                    for c in range(NKC):
                        mmi = pe.raw.matmul(pst[:, 0:NTH], tern[:, 128 * c:128 * (c + 1)],
                                            ln2[:, NT * c + NTH * hf:NT * c + NTH * (hf + 1)],
                                            start=(c == 0), stop=(c == NKC - 1))
                    th = pe.done(mmi)
                    t_fcmm = th
                    dve.wait(th)
                    tcp = dve.done(dve.raw.tensor_copy(hfp[:, :], pst[:, 0:NTH]))
                    t_gslot[m % 2] = tcp
                    dve.wait(tcp)
                    t_h2w = dve.done(dve.raw.scalar_tensor_tensor(
                        h2[:, NTH * m:NTH * (m + 1)], hfp[:, :], 0.0, hfp[:, :],
                        ALU.max, ALU.mult))
                t_wfree = [None, None]
                t_yev = None
                t_yslot = [None, None]
                t_pjmm = None
                t_pjtt = None
                for o in range(NKC):
                    pst = [ps_g0, ps_g1][o % 2]
                    for kh in range(4):
                        bw = [wp0, wp1][kh % 2]
                        qs.wait(t_wfree[kh % 2])
                        t_d = qs.done(qs.raw.dma_start(
                            out=cm(bw, 10),
                            in_=wpjT[1280 * kh:1280 * (kh + 1), 128 * o:128 * (o + 1)]
                            .rearrange("(c p) n -> p c n", p=128)))
                        act.wait(t_d, t_pjmm, t_pjtt)
                        tsa = act.done(act.raw.activation(out=tpa1[:, :], in_=bw[:, :],
                                                          func=AF.Sign, bias=thr_pj[:, 0:1],
                                                          scale=1.0))
                        tsb = act.done(act.raw.activation(out=ternp[:, :], in_=bw[:, :],
                                                          func=AF.Sign, bias=thr_pj[:, 1:2],
                                                          scale=1.0))
                        t_wfree[kh % 2] = tsb
                        dve.wait(tsa, tsb)
                        tt_ = dve.done(dve.raw.tensor_tensor(ternp[:, :], ternp[:, :],
                                                             tpa1[:, :], ALU.add))
                        t_pjtt = tt_
                        pe.wait(tt_, t_h2w)
                        if kh == 0:
                            pe.wait(t_yslot[o % 2])
                        for c in range(10):
                            cc = 10 * kh + c
                            mmi = pe.raw.matmul(pst[:, 0:NTH], ternp[:, 128 * c:128 * (c + 1)],
                                                h2[:, NTH * cc:NTH * (cc + 1)],
                                                start=(cc == 0), stop=(cc == 39))
                        ty = pe.done(mmi)
                        t_pjmm = ty
                    dve.wait(ty)
                    t_yev = dve.done(dve.raw.tensor_single_scalar(
                        cm(yb, NKC)[:, o, :], pst[:, 0:NTH], cy_b[:, 0:1], ALU.mult))
                    t_yslot[o % 2] = t_yev
                em.barrier()
                if DBG and hf == 0:
                    qg.done(qg.raw.dma_start(
                        out=dbg["d_h2"].rearrange("(c p) n -> p c n", p=128),
                        in_=cm(h2, 40)))
                    t_ = qg.done(qg.raw.dma_start(
                        out=dbg["d_y"].rearrange("(c p) n -> p c n", p=128),
                        in_=cm(yb, NKC)))
                    qg.wait(t_)
                    em.barrier()
                t_x1 = qs.done(qs.raw.dma_start(
                    out=cm(x1h, NKC),
                    in_=x1_d[:, NTH * hf:NTH * (hf + 1)].rearrange("(c p) n -> p c n", p=128)))
                yv = cm(yb, NKC)
                t_rs = ln_stats(yv, NTH, NKC, HID, eps_ln_t, bfc2, sqc2, ps_h1, ps_h2)
                dve.wait(t_x1)
                tn, _ = ln_norm(yv, NTH, NKC, mb4, rb4, ps_g0, ps_g1, t_rs,
                                out_view=cm(x1h, NKC), add_view=cm(x1h, NKC))
                qs.wait(tn)
                t_out_final = qs.done(qs.raw.dma_start(
                    out=outT[:, NTH * hf:NTH * (hf + 1)].rearrange("(c p) n -> p c n", p=128),
                    in_=cm(x1h, NKC)))
                em.barrier()
        qs.wait(t_out_final)
        em.barrier()
    return nc, dbg


def _prep(inputs):
    x = np.ascontiguousarray(inputs["L0_attn_hidden_states"])
    Wq = inputs["L0_attn_Wq"]
    Wkva = inputs["L0_attn_Wkva"]
    kvaw = inputs["L0_attn_kv_a_norm_weight"]
    Wkvb = inputs["L0_attn_Wkvb"]
    Wo = inputs["L0_attn_Wo"]
    cos = inputs["L0_attn_cos"]
    sin = inputs["L0_attn_sin"]
    wfc = inputs["L0_ffn_w_fc"]
    wpj = inputs["L0_ffn_w_proj"]

    f32 = np.float32
    WqT = np.ascontiguousarray(Wq.T.astype(f32, copy=False))
    wqn = np.ascontiguousarray(WqT.reshape(HID, NH, DQ)[:, :, :DN].reshape(HID, NH * DN))
    wqp = np.ascontiguousarray(WqT.reshape(HID, NH, DQ)[:, :, DN:].reshape(HID, NH * DR))
    WkvaT = np.ascontiguousarray(Wkva.T.astype(f32, copy=False))
    wkc = np.ascontiguousarray(WkvaT[:, :LORA])
    wkp = np.ascontiguousarray(WkvaT[:, LORA:])
    WkvbT = np.ascontiguousarray(Wkvb.T.astype(f32, copy=False))
    wkbk = np.ascontiguousarray(WkvbT.reshape(LORA, NH, DN + DV)[:, :, :DN]
                                .reshape(LORA, NH * DN))
    wkbv = np.ascontiguousarray(WkvbT.reshape(LORA, NH, DN + DV)[:, :, DN:]
                                .reshape(LORA, NH * DV))
    woT = np.ascontiguousarray(Wo.T.astype(f32, copy=False))
    wfcT = np.ascontiguousarray(wfc.T.astype(f32, copy=False))
    wpjT = np.ascontiguousarray(wpj.T.astype(f32, copy=False))
    kvw = np.ascontiguousarray(np.asarray(kvaw).reshape(4, 128).T.astype(f32, copy=False))
    cosr = np.ascontiguousarray(np.asarray(cos).astype(f32, copy=False)).reshape(1, DR)
    sinr = np.ascontiguousarray(np.asarray(sin).astype(f32, copy=False)).reshape(1, DR)

    in_maps = []
    for c in range(8):
        beta, r = c // 4, c % 4
        xb = np.asarray(x[beta]).T.astype(f32, copy=False)
        idx = (np.arange(S) // 4) * 4 + (np.arange(S) + r) % 4
        xrot = np.ascontiguousarray(xb[:, idx])
        xown = np.ascontiguousarray(xb[:, r::4])
        ki = np.arange(128)
        A = 4 * (ki // 4) + (ki + r) % 4 - r
        cmeta_v = np.zeros((128, 8), f32)
        cmeta_v[:, 0] = A.astype(f32)
        in_maps.append({
            "xT": xrot, "x_own": xown, "cmeta": cmeta_v,
            "wqn": wqn, "wqp": wqp, "wkc": wkc, "wkp": wkp,
            "wkbk": wkbk, "wkbv": wkbv, "woT": woT,
            "wfcT": wfcT, "wpjT": wpjT, "kvw": kvw,
            "cosr": cosr, "sinr": sinr,
        })
    return in_maps


def kernel(**inputs):
    in_maps = _prep(inputs)
    nc, _ = build()
    res = run_bass_kernel_spmd(nc, in_maps, core_ids=list(range(8)))
    out = np.empty((B, S, HID), np.float32)
    for c in range(8):
        beta, r = c // 4, c % 4
        out[beta, r::4, :] = res.results[c]["outT"].T
    return out


if __name__ == "__main__":
    nc, _ = build()
    print("built:", nc.n_instructions(), "instructions")


# version 04
# speedup vs baseline: 1.1241x; 1.1241x over previous
import numpy as np
from contextlib import ExitStack

import concourse.bass as bass
import concourse.mybir as mybir
from concourse.bass_utils import run_bass_kernel_spmd

F32 = mybir.dt.float32
BF16 = mybir.dt.bfloat16
AF = mybir.ActivationFunctionType
ALU = mybir.AluOpType
AX = mybir.AxisListType

B, S, HID = 2, 4096, 2048
NH, DN, DR, DQ, DV, LORA = 16, 128, 64, 192, 128, 512
SCALE = float(DQ) ** -0.5
EPS_LN = 1e-5
EPS_RMS = 1e-6
HF = 5120
NT = S // 4
NKC = HID // 128
DBG = False


class EW:
    def __init__(self, raw, sem, name, dma=False):
        self.raw = raw
        self.sem = sem
        self.n = 0
        self.name = name
        self.step = 16 if dma else 1
        self.waited = {}

    def wait(self, *toks):
        for t in toks:
            if t is None:
                continue
            sem, n = t
            k = sem.name
            if self.waited.get(k, 0) >= n:
                continue
            self.raw.wait_ge(sem, n)
            self.waited[k] = n

    def done(self, inst):
        self.n += self.step
        inst.then_inc(self.sem, self.step)
        return (self.sem, self.n)

    def tok(self):
        return (self.sem, self.n) if self.n else None


class Em:
    def __init__(self, nc, ctx):
        self.nc = nc
        E = ctx.enter_context
        self.pe = EW(nc.tensor, E(nc.semaphore("s_pe")), "pe")
        self.act = EW(nc.scalar, E(nc.semaphore("s_act")), "act")
        self.dve = EW(nc.vector, E(nc.semaphore("s_dve")), "dve")
        self.gpc = EW(nc.gpsimd, E(nc.semaphore("s_gpc")), "gpc")
        self.qs = EW(nc.sync, E(nc.semaphore("s_qs")), "qs", dma=True)
        self.qg = EW(nc.gpsimd, E(nc.semaphore("s_qg")), "qg", dma=True)
        self.qa = EW(nc.scalar, E(nc.semaphore("s_qa")), "qa", dma=True)
        self.all = [self.pe, self.act, self.dve, self.gpc, self.qs, self.qg, self.qa]

    def barrier(self):
        toks = [e.tok() for e in self.all]
        for e in self.all:
            e.wait(*toks)


def cm(t, c):
    return t[:, :].rearrange("p (c n) -> p c n", c=c)


def build():
    nc = bass.Bass("TRN2", target_bir_lowering=False)

    def din(name, shape, dt=F32):
        return nc.dram_tensor(name, shape, dt, kind="ExternalInput").ap()

    xT = din("xT", [HID, S])
    x_own = din("x_own", [HID, NT])
    cmeta = din("cmeta", [128, 8])
    wqn = din("wqn", [HID, NH * DN])
    wqp = din("wqp", [HID, NH * DR])
    wkc = din("wkc", [HID, LORA])
    wkp = din("wkp", [HID, DR])
    wkbk = din("wkbk", [LORA, NH * DN])
    wkbv = din("wkbv", [LORA, NH * DV])
    woT = din("woT", [NH * DV, HID])
    wfcT = din("wfcT", [HID, HF])
    wpjT = din("wpjT", [HF, HID])
    kvw = din("kvw", [128, 4])
    cosr = din("cosr", [1, DR])
    sinr = din("sinr", [1, DR])

    outT = nc.dram_tensor("outT", [HID, NT], F32, kind="ExternalOutput").ap()

    rms_d = nc.dram_tensor("rms_d", [LORA, S], BF16, kind="Internal").ap()
    ot_d = nc.dram_tensor("ot_d", [NH * DV, NT], BF16, kind="Internal").ap()
    x1_d = nc.dram_tensor("x1_d", [HID, NT], F32, kind="Internal").ap()

    dbg = {}
    if DBG:
        def dout(name, shape, dt=F32):
            dbg[name] = nc.dram_tensor(name, shape, dt, kind="ExternalOutput").ap()
            return dbg[name]
        dout("d_ln1own", [HID, NT], BF16)
        dout("d_rms", [LORA, S], BF16)
        dout("d_kpe", [DR, S], BF16)
        dout("d_qn", [NH * DN, NT], BF16)
        dout("d_qp", [8 * 128, NT], BF16)
        dout("d_kn0", [4 * DN, S], BF16)
        dout("d_ot", [NH * DV, NT], BF16)
        dout("d_x1", [HID, NT], F32)
        dout("d_am", [1, 4], F32)
        dout("d_ln2", [HID, NT], BF16)
        dout("d_h2", [HF, NT // 2], BF16)
        dout("d_y", [HID, NT // 2], F32)

    ctx = ExitStack()
    with ctx:
        em = Em(nc, ctx)
        pe, act, dve, gpc, qs, qg, qa = em.pe, em.act, em.dve, em.gpc, em.qs, em.qg, em.qa
        E = ctx.enter_context

        ones_b = E(nc.sbuf_tensor("ones_b", [128, 1], BF16))
        ones_s = E(nc.sbuf_tensor("ones_s", [128, 1], BF16))
        ln2n30 = E(nc.sbuf_tensor("ln2n30", [1, 1], F32))
        ones_f = E(nc.sbuf_tensor("ones_f", [1, 128], F32))
        onesc_f = E(nc.sbuf_tensor("onesc_f", [128, 1], F32))
        eps_ln_t = E(nc.sbuf_tensor("eps_ln", [1, 1], F32))
        eps_rms_t = E(nc.sbuf_tensor("eps_rms", [1, 1], F32))
        cmeta_t = E(nc.sbuf_tensor("cmeta_t", [128, 8], F32))
        kvw_t = E(nc.sbuf_tensor("kvw_t", [128, 4], F32))
        cosb = E(nc.sbuf_tensor("cosb", [128, DR], F32))
        spb = E(nc.sbuf_tensor("spb", [128, DR], F32))
        nmax = E(nc.sbuf_tensor("nmax", [1, 4], F32))
        scr = E(nc.sbuf_tensor("scr", [1, 16], F32))
        amv = E(nc.sbuf_tensor("amv", [1, 4], F32))
        thr_fc = E(nc.sbuf_tensor("thr_fc", [128, 2], F32))
        thr_pj = E(nc.sbuf_tensor("thr_pj", [128, 2], F32))
        cy_b = E(nc.sbuf_tensor("cy_b", [128, 1], F32))
        srow = E(nc.sbuf_tensor("srow", [1, 4 * 512], F32))
        SM, SQ, SVAR, SRS = 0, 1, 2, 3

        def row(i, w=512):
            return srow[:, 512 * i:512 * i + w]

        def bc_scalar(dst_col128, src_row1, ps):
            t1 = pe.done(pe.raw.matmul(ps[:, 0:1], ones_f[:, :], src_row1, start=True, stop=True))
            dve.wait(t1)
            return dve.done(dve.raw.tensor_copy(dst_col128, ps[:, 0:1]))

        def ln_stats(src_view, T, chunks, n_dims, eps_t, bfc, sqc, ps1, ps2):
            t_mm = None
            t_slot = [None, None]
            for c in range(chunks):
                sl = c % 2
                dve.wait(t_slot[sl])
                tc = dve.done(dve.raw.tensor_copy(bfc[:, T * sl:T * (sl + 1)],
                                                  src_view[:, c, :]))
                act.wait(t_slot[sl])
                ts = act.done(act.raw.activation(out=sqc[:, T * sl:T * (sl + 1)],
                                                 in_=src_view[:, c, :],
                                                 func=AF.Square, bias=0.0, scale=1.0))
                pe.wait(tc, ts)
                pe.raw.matmul(ps1[:, 0:T], ones_b[:, :], bfc[:, T * sl:T * (sl + 1)],
                              start=(c == 0), stop=(c == chunks - 1))
                mmi = pe.raw.matmul(ps2[:, 0:T], ones_b[:, :], sqc[:, T * sl:T * (sl + 1)],
                                    start=(c == 0), stop=(c == chunks - 1))
                t_mm = pe.done(mmi)
                t_slot[sl] = t_mm
            dve.wait(t_mm)
            t1 = dve.done(dve.raw.tensor_single_scalar(row(SM, T), ps1[:, 0:T],
                                                       1.0 / n_dims, ALU.mult))
            t2 = dve.done(dve.raw.tensor_single_scalar(row(SQ, T), ps2[:, 0:T],
                                                       1.0 / n_dims, ALU.mult))
            dve.wait(t1)
            t3 = dve.done(dve.raw.tensor_tensor(row(SVAR, T), row(SM, T), row(SM, T), ALU.mult))
            dve.wait(t2, t3)
            t4 = dve.done(dve.raw.tensor_tensor(row(SVAR, T), row(SQ, T), row(SVAR, T),
                                                ALU.subtract))
            act.wait(t4)
            t5 = act.done(act.raw.activation(out=row(SRS, T), in_=row(SVAR, T), func=AF.Ln,
                                             bias=eps_t[:, :], scale=1.0))
            act.wait(t5)
            return act.done(act.raw.activation(out=row(SRS, T), in_=row(SRS, T), func=AF.Exp,
                                               bias=0.0, scale=-0.5))

        def ln_norm(src_view, T, chunks, mb_sb, rb_sb, ps_a, ps_b, t_rs, out_view,
                    add_view=None):
            pe.wait(t_rs)
            tbm = pe.done(pe.raw.matmul(ps_a[:, 0:T], ones_f[:, :], row(SM, T),
                                        start=True, stop=True))
            tbr = pe.done(pe.raw.matmul(ps_b[:, 0:T], ones_f[:, :], row(SRS, T),
                                        start=True, stop=True))
            dve.wait(tbm)
            tem = dve.done(dve.raw.tensor_copy(mb_sb[:, 0:T], ps_a[:, 0:T]))
            dve.wait(tbr)
            ter = dve.done(dve.raw.tensor_copy(rb_sb[:, 0:T], ps_b[:, 0:T]))
            mb_b = mb_sb[:, 0:T].rearrange("p (o n) -> p o n", o=1).broadcast_to([128, chunks, T])
            rb_b = rb_sb[:, 0:T].rearrange("p (o n) -> p o n", o=1).broadcast_to([128, chunks, T])
            dve.wait(tem)
            tn1 = dve.done(dve.raw.tensor_tensor(src_view, src_view, mb_b, ALU.subtract))
            dve.wait(tn1, ter)
            if add_view is None:
                tn2 = dve.done(dve.raw.tensor_tensor(out_view, src_view, rb_b, ALU.mult))
            else:
                tn2a = dve.done(dve.raw.tensor_tensor(src_view, src_view, rb_b, ALU.mult))
                dve.wait(tn2a)
                tn2 = dve.done(dve.raw.tensor_tensor(out_view, src_view, add_view, ALU.add))
            return tn2, (tbm, tbr)

        t_cm = qs.done(qs.raw.dma_start(out=cmeta_t[:, :], in_=cmeta))
        qs.done(qs.raw.dma_start(out=kvw_t[:, :], in_=kvw))
        dve.done(dve.raw.memset(ones_b[:, :], 1.0))
        dve.done(dve.raw.memset(ones_s[:, :], float(2.0 ** -30)))
        dve.done(dve.raw.memset(ln2n30[:, :], float(-30.0 * np.log(2.0))))
        t_onef = dve.done(dve.raw.memset(ones_f[:, :], 1.0))
        dve.done(dve.raw.memset(onesc_f[:, :], 1.0))
        dve.done(dve.raw.memset(eps_ln_t[:, :], EPS_LN))
        t_ep2 = dve.done(dve.raw.memset(eps_rms_t[:, :], EPS_RMS))

        with ExitStack() as bd_scope:
            masks = bd_scope.enter_context(nc.sbuf_tensor("masks", [128, 8 * 256], BF16))
            kpefull = bd_scope.enter_context(nc.sbuf_tensor("kpefull", [65, S], BF16))

            with ExitStack() as ph:
                cs_row = ph.enter_context(nc.sbuf_tensor("cs_row", [1, 2 * DR], F32))
                iq = ph.enter_context(nc.sbuf_tensor("iq", [128, 256], F32))
                iqa = ph.enter_context(nc.sbuf_tensor("iqa", [128, 256], F32))
                ps_b0 = ph.enter_context(nc.psum_tensor("ps_b0", [128, 512], F32))
                t_cos = qs.done(qs.raw.dma_start(out=cs_row[:, 0:DR], in_=cosr))
                t_sin = qs.done(qs.raw.dma_start(out=cs_row[:, DR:2 * DR], in_=sinr))
                dve.wait(t_sin)
                t_neg = dve.done(dve.raw.tensor_single_scalar(
                    cs_row[:, DR:DR + 32], cs_row[:, DR:DR + 32], -1.0, ALU.mult))
                pe.wait(t_cos, t_onef)
                t1 = pe.done(pe.raw.matmul(ps_b0[:, 0:DR], ones_f[:, :], cs_row[:, 0:DR],
                                           start=True, stop=True))
                dve.wait(t1)
                t_cb = dve.done(dve.raw.tensor_copy(cosb[:, :], ps_b0[:, 0:DR]))
                pe.wait(t_cb, t_neg)
                t2 = pe.done(pe.raw.matmul(ps_b0[:, 0:DR], ones_f[:, :], cs_row[:, DR:2 * DR],
                                           start=True, stop=True))
                dve.wait(t2)
                t_sb = dve.done(dve.raw.tensor_copy(spb[:, :], ps_b0[:, 0:DR]))
                t_io = gpc.done(gpc.raw.iota(iq[:, :], pattern=[[4, 256]], base=0,
                                             channel_multiplier=0,
                                             allow_small_or_imprecise_dtypes=True))
                dve.wait(t_io, t_cm)
                t_iqa = dve.done(dve.raw.tensor_single_scalar(iqa[:, :], iq[:, :],
                                                              cmeta_t[:, 0:1], ALU.subtract))
                dve.wait(t_iqa)
                for j in range(8):
                    mj = dve.raw.tensor_single_scalar(masks[:, 256 * j:256 * (j + 1)],
                                                      iqa[:, :], float(128 * j), ALU.is_ge)
                dve.done(mj)
                em.barrier()

            NB = 256
            NBLK = S // NB
            with ExitStack() as bc_scope:
                ln1own = bc_scope.enter_context(nc.sbuf_tensor("ln1own", [128, NKC * NT], BF16))

                with ExitStack() as ph:
                    P = ph.enter_context
                    wkc_t = P(nc.sbuf_tensor("wkc_t", [128, NKC * 512], BF16))
                    weffk = P(nc.sbuf_tensor("weffk", [128, NKC * DR], BF16))
                    xblk0 = P(nc.sbuf_tensor("xblk0", [128, NKC * NB], F32))
                    xblk1 = P(nc.sbuf_tensor("xblk1", [128, NKC * NB], F32))
                    xbf = P(nc.sbuf_tensor("xbf", [128, NKC * NB], BF16))
                    xsq = P(nc.sbuf_tensor("xsq", [128, NKC * NB], BF16))
                    mb_s = P(nc.sbuf_tensor("mb_s", [128, NB], F32))
                    rb_s = P(nc.sbuf_tensor("rb_s", [128, NB], F32))
                    ln1b = P(nc.sbuf_tensor("ln1b", [128, NKC * NB], BF16))
                    cf32 = P(nc.sbuf_tensor("cf32", [128, 4 * NB], F32))
                    csq = P(nc.sbuf_tensor("csq", [128, 4 * NB], BF16))
                    rr_s = P(nc.sbuf_tensor("rr_s", [128, NB], F32))
                    rmsb = P(nc.sbuf_tensor("rmsb", [128, 4 * NB], BF16))
                    ps_s1 = P(nc.psum_tensor("ps_s1", [1, 512], F32))
                    ps_s2 = P(nc.psum_tensor("ps_s2", [1, 512], F32))
                    ps_c0 = P(nc.psum_tensor("ps_c0", [128, 512], F32))
                    ps_c1 = P(nc.psum_tensor("ps_c1", [128, 512], F32))
                    ps_c2 = P(nc.psum_tensor("ps_c2", [128, 512], F32))
                    ps_c3 = P(nc.psum_tensor("ps_c3", [128, 512], F32))
                    ps_kp = P(nc.psum_tensor("ps_kp", [64, 512], F32))
                    cps = [ps_c0, ps_c1, ps_c2, ps_c3]

                    t_wkc = qg.done(qg.raw.dma_start(
                        out=cm(wkc_t, NKC), in_=wkc.rearrange("(c p) n -> p c n", p=128)))
                    with ExitStack() as ph2:
                        wkpf = ph2.enter_context(nc.sbuf_tensor("wkpf", [128, NKC * DR], F32))
                        wktmp = ph2.enter_context(nc.sbuf_tensor("wktmp", [128, NKC * DR], F32))
                        t_wkp = qs.done(qs.raw.dma_start(
                            out=cm(wkpf, NKC), in_=wkp.rearrange("(c p) n -> p c n", p=128)))
                        dve.wait(t_wkp, t_sb)
                        cb3 = cosb[:, :].rearrange("p (o d) -> p o d", o=1) \
                            .broadcast_to([128, NKC, DR])
                        sb4 = spb[:, :].rearrange("p (o t d) -> p o t d", o=1, t=2) \
                            .broadcast_to([128, NKC, 2, 32])
                        t1 = dve.done(dve.raw.tensor_tensor(cm(wktmp, NKC), cm(wkpf, NKC),
                                                            cb3, ALU.mult))
                        wperm = wkpf[:, :].rearrange("p (c t d) -> p c t d",
                                                     c=NKC, t=2)[:, :, 1::-1, :]
                        weffk4 = weffk[:, :].rearrange("p (c t d) -> p c t d", c=NKC, t=2)
                        t2 = dve.done(dve.raw.tensor_tensor(weffk4, wperm, sb4, ALU.mult))
                        dve.wait(t1, t2)
                        t_wk = dve.done(dve.raw.tensor_tensor(weffk[:, :], weffk[:, :],
                                                              wktmp[:, :], ALU.add))

                    xb_ = [xblk0, xblk1]
                    t_ck = [None, None]
                    t_scr_free = None
                    t_rmsn = None
                    t_rms_dma = None
                    t_kpe_ev = None
                    t_ps_s_rd = None
                    t_r2_prev = None
                    t_ps_c_rd = None
                    for b in range(NBLK):
                        sl = xb_[b % 2]
                        slv = cm(sl, NKC)
                        qs.wait(t_ck[b % 2])
                        t_x = qs.done(qs.raw.dma_start(
                            out=slv,
                            in_=xT[:, NB * b:NB * (b + 1)].rearrange("(c p) n -> p c n", p=128)))
                        dve.wait(t_x, t_scr_free)
                        tc = dve.done(dve.raw.tensor_copy(xbf[:, :], sl[:, :]))
                        act.wait(t_x, t_scr_free)
                        ts = act.done(act.raw.activation(out=xsq[:, :], in_=sl[:, :],
                                                         func=AF.Square, bias=0.0, scale=1.0))
                        pe.wait(tc, ts, t_ps_s_rd)
                        for c in range(NKC):
                            pe.raw.matmul(ps_s1[:, 0:NB], ones_b[:, :],
                                          xbf[:, NB * c:NB * (c + 1)],
                                          start=(c == 0), stop=(c == NKC - 1))
                            mmi = pe.raw.matmul(ps_s2[:, 0:NB], ones_b[:, :],
                                                xsq[:, NB * c:NB * (c + 1)],
                                                start=(c == 0), stop=(c == NKC - 1))
                        t_mm = pe.done(mmi)
                        dve.wait(t_mm)
                        t1 = dve.done(dve.raw.tensor_single_scalar(row(SM, NB), ps_s1[:, 0:NB],
                                                                   1.0 / HID, ALU.mult))
                        t2 = dve.done(dve.raw.tensor_single_scalar(row(SQ, NB), ps_s2[:, 0:NB],
                                                                   1.0 / HID, ALU.mult))
                        dve.wait(t1)
                        t3 = dve.done(dve.raw.tensor_tensor(row(SVAR, NB), row(SM, NB),
                                                            row(SM, NB), ALU.mult))
                        dve.wait(t2, t3)
                        t4 = dve.done(dve.raw.tensor_tensor(row(SVAR, NB), row(SQ, NB),
                                                            row(SVAR, NB), ALU.subtract))
                        act.wait(t4)
                        t5 = act.done(act.raw.activation(out=row(SRS, NB), in_=row(SVAR, NB),
                                                         func=AF.Ln, bias=eps_ln_t[:, :], scale=1.0))
                        act.wait(t5)
                        t6 = act.done(act.raw.activation(out=row(SRS, NB), in_=row(SRS, NB),
                                                         func=AF.Exp, bias=0.0, scale=-0.5))
                        pe.wait(t6, t_ps_c_rd)
                        tbm = pe.done(pe.raw.matmul(ps_c0[:, 0:NB], ones_f[:, :], row(SM, NB),
                                                    start=True, stop=True))
                        tbr = pe.done(pe.raw.matmul(ps_c1[:, 0:NB], ones_f[:, :], row(SRS, NB),
                                                    start=True, stop=True))
                        dve.wait(tbm)
                        tem = dve.done(dve.raw.tensor_copy(mb_s[:, :], ps_c0[:, 0:NB]))
                        dve.wait(tbr)
                        ter = dve.done(dve.raw.tensor_copy(rb_s[:, :], ps_c1[:, 0:NB]))
                        mb_b = mb_s[:, :].rearrange("p (o n) -> p o n", o=1) \
                            .broadcast_to([128, NKC, NB])
                        rb_b = rb_s[:, :].rearrange("p (o n) -> p o n", o=1) \
                            .broadcast_to([128, NKC, NB])
                        dve.wait(tem)
                        tn1 = dve.done(dve.raw.tensor_tensor(slv, slv, mb_b, ALU.subtract))
                        dve.wait(tn1, ter, t_ck[b % 2])
                        tln = dve.done(dve.raw.tensor_tensor(cm(ln1b, NKC), slv, rb_b, ALU.mult))
                        dve.wait(tln)
                        t_rp = dve.done(dve.raw.tensor_copy(
                            cm(ln1own, NKC)[:, :, (NB // 4) * b:(NB // 4) * (b + 1)],
                            cm(ln1b, NKC)[:, :, 0:NB:4]))
                        pe.wait(tln, tem, ter, t_wkc, t_wk, t_rmsn, t_kpe_ev)
                        for c in range(NKC):
                            for m in range(4):
                                pe.raw.matmul(cps[m][:, 0:NB],
                                              wkc_t[:, 512 * c + 128 * m:512 * c + 128 * (m + 1)],
                                              ln1b[:, NB * c:NB * (c + 1)],
                                              start=(c == 0), stop=(c == NKC - 1))
                            mmi = pe.raw.matmul(ps_kp[:, 0:NB], weffk[:, DR * c:DR * (c + 1)],
                                                ln1b[:, NB * c:NB * (c + 1)],
                                                start=(c == 0), stop=(c == NKC - 1))
                        t_ckv = pe.done(mmi)
                        t_ck[b % 2] = t_ckv
                        dve.wait(t_ckv)
                        for m in range(4):
                            cpi = dve.raw.tensor_copy(cf32[:, NB * m:NB * (m + 1)], cps[m][:, 0:NB])
                        dve.done(cpi)
                        t_cf = dve.done(dve.raw.tensor_copy(kpefull[0:64, NB * b:NB * (b + 1)],
                                                            ps_kp[:, 0:NB]))
                        t_kpe_ev = t_cf
                        act.wait(t_cf, t_r2_prev)
                        t_csq = act.done(act.raw.activation(out=csq[:, :], in_=cf32[:, :],
                                                            func=AF.Square, bias=0.0, scale=1.0))
                        pe.wait(t_csq)
                        for m in range(4):
                            mmi = pe.raw.matmul(ps_s1[:, 0:NB], ones_b[:, :],
                                                csq[:, NB * m:NB * (m + 1)],
                                                start=(m == 0), stop=(m == 3))
                        t_r2 = pe.done(mmi)
                        t_r2_prev = t_r2
                        act.wait(t_r2, t_ep2)
                        t_lr = act.done(act.raw.activation(out=row(SQ, NB), in_=ps_s1[:, 0:NB],
                                                           func=AF.Ln, bias=eps_rms_t[:, :],
                                                           scale=1.0 / LORA))
                        t_ps_s_rd = t_lr
                        act.wait(t_lr)
                        t_er = act.done(act.raw.activation(out=row(SQ, NB), in_=row(SQ, NB),
                                                           func=AF.Exp, bias=0.0, scale=-0.5))
                        pe.wait(t_er)
                        t_br2 = pe.done(pe.raw.matmul(ps_c0[:, 0:NB], ones_f[:, :], row(SQ, NB),
                                                      start=True, stop=True))
                        dve.wait(t_br2)
                        t_err = dve.done(dve.raw.tensor_copy(rr_s[:, :], ps_c0[:, 0:NB]))
                        t_ps_c_rd = t_err
                        dve.wait(t_err, t_rms_dma)
                        for m in range(4):
                            sti = dve.raw.scalar_tensor_tensor(
                                rmsb[:, NB * m:NB * (m + 1)], cf32[:, NB * m:NB * (m + 1)],
                                kvw_t[:, m:m + 1], rr_s[:, :], ALU.mult, ALU.mult)
                        t_rmsn = dve.done(sti)
                        qs.wait(t_rmsn)
                        t_rms_dma = qs.done(qs.raw.dma_start(
                            out=rms_d[:, NB * b:NB * (b + 1)].rearrange("(m p) n -> p m n", p=128),
                            in_=cm(rmsb, 4)))
                    t_kones = dve.done(dve.raw.memset(kpefull[64:65, :], 1.0))
                    em.barrier()
                if DBG:
                    qg.done(qg.raw.dma_start(
                        out=dbg["d_ln1own"].rearrange("(c p) n -> p c n", p=128),
                        in_=cm(ln1own, NKC)))
                    qg.done(qg.raw.dma_start(out=dbg["d_kpe"], in_=kpefull[0:64, :]))
                    t_ = qg.done(qg.raw.dma_start(out=dbg["d_rms"], in_=rms_d))
                    qg.wait(t_)
                    em.barrier()

                qn = bd_scope.enter_context(nc.sbuf_tensor("qn", [128, NH * NT], BF16,
                                                           side="right"))
                qph = bd_scope.enter_context(nc.sbuf_tensor("qph", [65, NH * NT], BF16,
                                                            side="right"))
                with ExitStack() as ph:
                    P = ph.enter_context
                    wtam0 = P(nc.sbuf_tensor("wtam0", [128, 4096], F32))
                    wtam1 = P(nc.sbuf_tensor("wtam1", [128, 4096], F32))
                    racc = P(nc.sbuf_tensor("racc", [128, 2], F32))
                    rtmp = P(nc.sbuf_tensor("rtmp", [128, 1], F32))
                    t_amz = dve.done(dve.raw.memset(racc[:, :], 0.0))
                    am_tiles = []
                    for wi, w_ap in enumerate([wfcT, wpjT]):
                        flat = w_ap.rearrange("a b -> (a b)").rearrange(
                            "(t p n) -> t p n", p=128, n=4096)
                        for ti in range(flat.shape[0]):
                            am_tiles.append((wi, flat[ti]))
                    am_state = {"i": 0, "slot_tok": [None, None], "acc": t_amz}

                    def am_step(k=1):
                        for _ in range(k):
                            i = am_state["i"]
                            if i >= len(am_tiles):
                                return
                            wi, tile = am_tiles[i]
                            sl = i % 2
                            bufw = [wtam0, wtam1][sl]
                            qa.wait(am_state["slot_tok"][sl])
                            td = qa.done(qa.raw.dma_start(out=bufw[:, :], in_=tile))
                            dve.wait(td, am_state["acc"])
                            tr = dve.done(dve.raw.tensor_reduce(
                                rtmp[:, :], bufw[:, :], axis=AX.X, op=ALU.add,
                                apply_absolute_value=True))
                            dve.wait(tr)
                            ta = dve.done(dve.raw.tensor_tensor(
                                racc[:, wi:wi + 1], racc[:, wi:wi + 1], rtmp[:, :], ALU.add))
                            am_state["acc"] = ta
                            am_state["slot_tok"][sl] = ta
                            am_state["i"] = i + 1

                    weffq = P(nc.sbuf_tensor("weffq", [128, NKC * 128], BF16))
                    wqs0 = P(nc.sbuf_tensor("wqs0", [128, NKC * 128], BF16))
                    wqs1 = P(nc.sbuf_tensor("wqs1", [128, NKC * 128], BF16))
                    wqpf0 = P(nc.sbuf_tensor("wqpf0", [128, NKC * 128], F32))
                    wqpf1 = P(nc.sbuf_tensor("wqpf1", [128, NKC * 128], F32))
                    wqtmp = P(nc.sbuf_tensor("wqtmp", [128, NKC * 128], F32))
                    weffpf = P(nc.sbuf_tensor("weffpf", [128, NKC * 128], F32))
                    ps_q0 = P(nc.psum_tensor("ps_q0", [128, 512], F32))
                    ps_q1 = P(nc.psum_tensor("ps_q1", [128, 512], F32))

                    t_dma = [None, None]
                    t_cons = [None, None]
                    t_weff_cons = None
                    t_qslot = [None, None]
                    for p in range(8):
                        am_step(2)
                        buf = [wqpf0, wqpf1][p % 2]
                        qs.wait(t_cons[p % 2])
                        t_dma[p % 2] = qs.done(qs.raw.dma_start(
                            out=cm(buf, NKC),
                            in_=wqp[:, 128 * p:128 * (p + 1)].rearrange("(c pp) n -> pp c n",
                                                                        pp=128)))
                        dve.wait(t_dma[p % 2], t_weff_cons)
                        w4 = buf[:, :].rearrange("pp (c h t d) -> pp c h t d", c=NKC, h=2, t=2)
                        cb4 = cosb[:, :].rearrange("pp (o t d) -> pp o t d", o=1, t=2) \
                            .rearrange("pp o t d -> pp o () t d") \
                            .broadcast_to([128, NKC, 2, 2, 32])
                        sb4 = spb[:, :].rearrange("pp (o t d) -> pp o t d", o=1, t=2) \
                            .rearrange("pp o t d -> pp o () t d") \
                            .broadcast_to([128, NKC, 2, 2, 32])
                        ta = dve.done(dve.raw.tensor_tensor(
                            wqtmp[:, :].rearrange("pp (c h t d) -> pp c h t d", c=NKC, h=2, t=2),
                            w4, cb4, ALU.mult))
                        tb = dve.done(dve.raw.tensor_tensor(
                            weffpf[:, :].rearrange("pp (c h t d) -> pp c h t d", c=NKC, h=2, t=2),
                            w4[:, :, :, 1::-1, :], sb4, ALU.mult))
                        t_cons[p % 2] = tb
                        dve.wait(ta, tb)
                        tcq = dve.done(dve.raw.tensor_tensor(weffq[:, :], weffpf[:, :],
                                                             wqtmp[:, :], ALU.add))
                        for hh in range(2):
                            h = 2 * p + hh
                            for tt in range(2):
                                pst = [ps_q0, ps_q1][tt]
                                pe.wait(tcq, t_qslot[tt])
                                for c in range(NKC):
                                    mmi = pe.raw.matmul(
                                        pst[0:64, :],
                                        weffq[:, 128 * c + 64 * hh:128 * c + 64 * (hh + 1)],
                                        ln1own[:, NT * c + 512 * tt:NT * c + 512 * (tt + 1)],
                                        start=(c == 0), stop=(c == NKC - 1))
                                tq = pe.done(mmi)
                                if hh == 1 and tt == 1:
                                    t_weff_cons = tq
                                dve.wait(tq)
                                tev = dve.done(dve.raw.tensor_copy(
                                    qph[0:64, NT * h + 512 * tt:NT * h + 512 * (tt + 1)],
                                    pst[0:64, :]))
                                t_qslot[tt] = tev
                    t_dma = [None, None]
                    t_cons = [None, None]
                    for h in range(NH):
                        am_step(2)
                        buf = [wqs0, wqs1][h % 2]
                        qg.wait(t_cons[h % 2])
                        t_dma[h % 2] = qg.done(qg.raw.dma_start(
                            out=cm(buf, NKC),
                            in_=wqn[:, 128 * h:128 * (h + 1)].rearrange("(c p) n -> p c n",
                                                                        p=128)))
                        for tt in range(2):
                            pst = [ps_q0, ps_q1][tt]
                            pe.wait(t_dma[h % 2], t_qslot[tt])
                            for c in range(NKC):
                                mmi = pe.raw.matmul(
                                    pst[:, :], buf[:, 128 * c:128 * (c + 1)],
                                    ln1own[:, NT * c + 512 * tt:NT * c + 512 * (tt + 1)],
                                    start=(c == 0), stop=(c == NKC - 1))
                            tq = pe.done(mmi)
                            if tt == 1:
                                t_cons[h % 2] = tq
                            dve.wait(tq)
                            tev = dve.done(dve.raw.tensor_copy(
                                qn[:, NT * h + 512 * tt:NT * h + 512 * (tt + 1)], pst[:, :]))
                            t_qslot[tt] = tev
                    ps_am = ph.enter_context(nc.psum_tensor("ps_am", [128, 512], F32))
                    am_step(100)
                    t_z = am_state["acc"]
                    for wi in range(2):
                        pe.wait(t_z)
                        tmm = pe.done(pe.raw.matmul(ps_am[0:1, 0:1], racc[:, wi:wi + 1],
                                                    onesc_f[:, :], start=True, stop=True))
                        dve.wait(tmm)
                        t1 = dve.done(dve.raw.tensor_single_scalar(
                            amv[:, wi:wi + 1], ps_am[0:1, 0:1], 1.0 / (HID * HF), ALU.mult))
                        dve.wait(t1)
                        t2 = dve.done(dve.raw.tensor_single_scalar(
                            amv[:, wi:wi + 1], amv[:, wi:wi + 1], 1e-5, ALU.max))
                        pe.wait(t2)
                    t1 = dve.done(dve.raw.tensor_tensor(amv[:, 2:3], amv[:, 0:1], amv[:, 0:1],
                                                        ALU.mult))
                    dve.wait(t1)
                    t2 = dve.done(dve.raw.tensor_tensor(amv[:, 2:3], amv[:, 2:3], amv[:, 1:2],
                                                        ALU.mult))
                    dve.wait(t2)
                    t3 = dve.done(dve.raw.tensor_single_scalar(amv[:, 2:3], amv[:, 2:3],
                                                               0.125, ALU.mult))
                    for (dst, col, src_, sc) in [(thr_fc, 0, 0, -0.7), (thr_fc, 1, 0, 0.7),
                                                 (thr_pj, 0, 1, -0.7), (thr_pj, 1, 1, 0.7)]:
                        dve.wait(t3)
                        tsc = dve.done(dve.raw.tensor_single_scalar(
                            scr[:, 2:3], amv[:, src_:src_ + 1], sc, ALU.mult))
                        pe.wait(tsc)
                        tmm = pe.done(pe.raw.matmul(ps_am[:, 0:1], ones_f[:, :], scr[:, 2:3],
                                                    start=True, stop=True))
                        dve.wait(tmm)
                        tev = dve.done(dve.raw.tensor_copy(dst[:, col:col + 1], ps_am[:, 0:1]))
                        pe.wait(tev)
                    pe.wait(t3)
                    tmm = pe.done(pe.raw.matmul(ps_am[:, 0:1], ones_f[:, :], amv[:, 2:3],
                                                start=True, stop=True))
                    dve.wait(tmm)
                    tev = dve.done(dve.raw.tensor_copy(cy_b[:, :], ps_am[:, 0:1]))
                    pe.wait(tev)
                em.barrier()
            if DBG:
                qg.done(qg.raw.dma_start(out=dbg["d_qn"].rearrange("(h p) n -> p h n", p=128),
                                          in_=cm(qn, NH)))
                t_ = qg.done(qg.raw.dma_start(
                    out=dbg["d_qp"].rearrange("(h pp) n -> pp h n", pp=64),
                    in_=qph[0:64, :].rearrange("pp (h n) -> pp h n", h=NH)))
                qg.wait(t_)
                em.barrier()

            with ExitStack() as ph:
                P = ph.enter_context
                kq = P(nc.sbuf_tensor("kq", [64, 512], BF16))
                ps_n = P(nc.psum_tensor("ps_n", [1, 512], F32))
                tsg = None
                for sseg in range(8):
                    tsq = dve.done(dve.raw.tensor_tensor(
                        kq[:, :], kpefull[0:64, 512 * sseg:512 * (sseg + 1)],
                        kpefull[0:64, 512 * sseg:512 * (sseg + 1)], ALU.mult))
                    pe.wait(tsq, tsg)
                    t1 = pe.done(pe.raw.matmul(ps_n[:, :], ones_b[0:64, :], kq[:, :],
                                               start=True, stop=True))
                    dve.wait(t1)
                    tsg = dve.done(dve.raw.tensor_reduce(scr[:, 8 + sseg:9 + sseg],
                                                         ps_n[:, :], axis=AX.X, op=ALU.max))
                dve.wait(tsg)
                t1 = dve.done(dve.raw.tensor_tensor(scr[:, 4:5], scr[:, 8:9], scr[:, 9:10],
                                                    ALU.max))
                dve.wait(t1)
                for tp in range(1, 4):
                    ta = dve.done(dve.raw.tensor_tensor(scr[:, 4 + tp:5 + tp],
                                                        scr[:, 3 + tp:4 + tp],
                                                        scr[:, 8 + 2 * tp:9 + 2 * tp], ALU.max))
                    dve.wait(ta)
                    tb = dve.done(dve.raw.tensor_tensor(scr[:, 4 + tp:5 + tp],
                                                        scr[:, 4 + tp:5 + tp],
                                                        scr[:, 9 + 2 * tp:10 + 2 * tp], ALU.max))
                    dve.wait(tb)
            em.barrier()

            em.barrier()
            em.barrier()
            if DBG:
                t_ = qg.done(qg.raw.dma_start(out=dbg["d_am"], in_=amv[0:1, :]))
                qg.wait(t_)

            NG = 8
            with ExitStack() as ph:
                P = ph.enter_context
                kngs = [P(nc.sbuf_tensor("kngA", [128, 2 * S], BF16)),
                        P(nc.sbuf_tensor("kngB", [128, 2 * S], BF16))]
                vbufs = [P(nc.sbuf_tensor("vbufA", [128, 32 * 256], BF16)),
                         P(nc.sbuf_tensor("vbufB", [128, 32 * 256], BF16))]
                wkbks = [P(nc.sbuf_tensor("wkbk0", [128, 4 * 256], BF16)),
                         P(nc.sbuf_tensor("wkbk1", [128, 4 * 256], BF16))]
                wkbvs = [P(nc.sbuf_tensor("wkbv0", [128, 4 * 256], BF16)),
                         P(nc.sbuf_tensor("wkbv1", [128, 4 * 256], BF16))]
                rmst0 = P(nc.sbuf_tensor("rmst0", [128, 4 * 512], BF16))
                rmst1 = P(nc.sbuf_tensor("rmst1", [128, 4 * 512], BF16))
                pts6 = [P(nc.sbuf_tensor(f"pt{i}", [128, 512], BF16)) for i in range(6)]
                ksq = P(nc.sbuf_tensor("ksq", [128, 512], BF16))
                ksq2 = P(nc.sbuf_tensor("ksq2", [128, NT], BF16))
                nqrow = P(nc.sbuf_tensor("nqrow", [1, NT], F32))
                nqrow2 = P(nc.sbuf_tensor("nqrow2", [1, NT], F32))
                sr_at = P(nc.sbuf_tensor("sr_at", [1, 1024], F32))
                rb2_s = P(nc.sbuf_tensor("rb2_s", [128, 512], F32))
                onrm0 = P(nc.sbuf_tensor("onrm0", [128, 256], BF16))
                onrm1 = P(nc.sbuf_tensor("onrm1", [128, 256], BF16))
                ps_SA = P(nc.psum_tensor("ps_SA", [128, 512], F32))
                ps_SB = P(nc.psum_tensor("ps_SB", [128, 512], F32))
                ps_SC = P(nc.psum_tensor("ps_SC", [128, 512], F32))
                ps_o0 = P(nc.psum_tensor("ps_o0", [128, 256], F32))
                ps_o1 = P(nc.psum_tensor("ps_o1", [128, 256], F32))
                ps_sm0 = P(nc.psum_tensor("ps_sm0", [1, 256], F32))
                ps_sm1 = P(nc.psum_tensor("ps_sm1", [1, 256], F32))
                ps_kv = P(nc.psum_tensor("ps_kv", [128, 512], F32))

                Sbanks = [ps_SA, ps_SB, ps_SC]
                ps_os = [ps_o0, ps_o1]
                ps_sms = [ps_sm0, ps_sm1]
                onrms = [onrm0, onrm1]
                bank = {"tok": None}
                t_rfree = [None, None]
                t_wkfree = [None, None]
                kv_ready = {}
                shift_ready = {}
                t_shift_dma = {}
                t_sm_free = [None, None]
                t_o_free = [None, None]
                t_pt_free = [None] * 6
                t_Sfree = [None, None, None]
                t_ot_dma = [None, None]
                state = {"pti": 0, "sti": 0, "epar": 0, "epi": None,
                         "ksq2_rd": None, "nqrow_rd": None}

                def kvb_closures(gg):
                    kb_ = kngs[gg % 2]
                    vb_ = vbufs[gg % 2]
                    wk_ = wkbks[gg % 2]
                    wv_ = wkbvs[gg % 2]
                    cls = []

                    def dma_w():
                        qg.wait(t_wkfree[gg % 2])
                        t1 = qg.done(qg.raw.dma_start(
                            out=cm(wk_, 4),
                            in_=wkbk[:, 256 * gg:256 * (gg + 1)]
                            .rearrange("(c p) n -> p c n", p=128)))
                        t2 = qg.done(qg.raw.dma_start(
                            out=cm(wv_, 4),
                            in_=wkbv[:, 256 * gg:256 * (gg + 1)]
                            .rearrange("(c p) n -> p c n", p=128)))
                        kv_ready.setdefault(gg, []).extend([t1, t2])
                    cls.append(dma_w)

                    def mk_stile(s):
                        def fn():
                            bufr = [rmst0, rmst1][s % 2]
                            qs.wait(t_rfree[s % 2])
                            t_r = qs.done(qs.raw.dma_start(
                                out=cm(bufr, 4),
                                in_=rms_d[:, 512 * s:512 * (s + 1)]
                                .rearrange("(c p) n -> p c n", p=128)))
                            tw = kv_ready[gg][:2]
                            for h2 in range(2):
                                pe.wait(t_r, bank["tok"], *tw)
                                for c in range(4):
                                    mmi = pe.raw.matmul(
                                        ps_kv[:, :],
                                        cm(wk_, 4)[:, c, 128 * h2:128 * (h2 + 1)],
                                        bufr[:, 512 * c:512 * (c + 1)],
                                        start=(c == 0), stop=(c == 3))
                                tkn = pe.done(mmi)
                                dve.wait(tkn)
                                tev = dve.done(dve.raw.tensor_copy(
                                    kb_[:, S * h2 + 512 * s:S * h2 + 512 * (s + 1)],
                                    ps_kv[:, :]))
                                bank["tok"] = tev
                            for half in range(2):
                                pe.wait(bank["tok"])
                                for i in range(2):
                                    sub = 2 * half + i
                                    for c in range(4):
                                        mmi = pe.raw.matmul(
                                            ps_kv[:, 256 * i:256 * (i + 1)],
                                            bufr[:, 512 * c + 128 * sub:512 * c + 128 * (sub + 1)],
                                            cm(wv_, 4)[:, c, :],
                                            start=(c == 0 and i == 0), stop=(c == 3 and i == 1))
                                tkn = pe.done(mmi)
                                dve.wait(tkn)
                                tev = dve.done(dve.raw.tensor_copy(
                                    vb_[:, 256 * (4 * s + 2 * half):256 * (4 * s + 2 * half + 2)],
                                    ps_kv[:, :]))
                                bank["tok"] = tev
                            t_rfree[s % 2] = bank["tok"]
                            kv_ready.setdefault(gg, []).append(bank["tok"])
                        return fn
                    for s in range(8):
                        cls.append(mk_stile(s))
                    return cls

                def norm_closures(gg):
                    kb_ = kngs[gg % 2]
                    cls = []

                    def mk_seg(sseg):
                        def fn():
                            tzk = None
                            for h2 in range(2):
                                tsq = dve.done(dve.raw.tensor_tensor(
                                    ksq[:, :],
                                    kb_[:, S * h2 + 512 * sseg:S * h2 + 512 * (sseg + 1)],
                                    kb_[:, S * h2 + 512 * sseg:S * h2 + 512 * (sseg + 1)],
                                    ALU.mult))
                                pe.wait(tsq, bank["tok"])
                                t1 = pe.done(pe.raw.matmul(ps_kv[0:1, 0:512], ones_b[:, :],
                                                           ksq[:, :], start=True, stop=True))
                                dve.wait(t1)
                                tm = dve.done(dve.raw.tensor_reduce(
                                    scr[:, 1:2], ps_kv[0:1, 0:512], axis=AX.X, op=ALU.max))
                                bank["tok"] = tm
                                dve.wait(tm)
                                tzk = dve.done(dve.raw.tensor_tensor(
                                    scr[:, 8 + sseg:9 + sseg], scr[:, 8 + sseg:9 + sseg],
                                    scr[:, 1:2], ALU.max))
                            shift_ready[(gg, "seg", sseg)] = tzk
                        return fn

                    def seg_init():
                        t0 = dve.done(dve.raw.memset(scr[:, 8:16], 0.0))
                        shift_ready[(gg, "seginit")] = t0
                    cls.append(seg_init)
                    for sseg in range(8):
                        cls.append(mk_seg(sseg))

                    def prefixes():
                        tpv = dve.done(dve.raw.tensor_tensor(scr[:, 0:1], scr[:, 8:9],
                                                             scr[:, 9:10], ALU.max))
                        for tp in range(1, 4):
                            dve.wait(tpv)
                            ta = dve.done(dve.raw.tensor_tensor(
                                scr[:, tp:tp + 1], scr[:, tp - 1:tp],
                                scr[:, 8 + 2 * tp:9 + 2 * tp], ALU.max))
                            dve.wait(ta)
                            tpv = dve.done(dve.raw.tensor_tensor(
                                scr[:, tp:tp + 1], scr[:, tp:tp + 1],
                                scr[:, 9 + 2 * tp:10 + 2 * tp], ALU.max))
                        dve.wait(tpv)
                        tkk = dve.done(dve.raw.tensor_tensor(scr[:, 0:4], scr[:, 0:4],
                                                             scr[:, 4:8], ALU.add))
                        act.wait(tkk)
                        t1 = act.done(act.raw.activation(out=scr[:, 8:12], in_=scr[:, 0:4],
                                                         func=AF.Ln, bias=eps_rms_t[:, :],
                                                         scale=1.0))
                        act.wait(t1)
                        t2 = act.done(act.raw.activation(out=scr[:, 8:12], in_=scr[:, 8:12],
                                                         func=AF.Exp, bias=0.0, scale=0.5))
                        dve.wait(t2)
                        tnk = dve.done(dve.raw.tensor_single_scalar(scr[:, 8:12], scr[:, 8:12],
                                                                    -1.0, ALU.mult))
                        shift_ready[(gg, "nk")] = tnk
                    cls.append(prefixes)

                    def mk_shift(h2):
                        def fn():
                            H = 2 * gg + h2
                            dve.wait(state["ksq2_rd"])
                            tq1 = dve.done(dve.raw.tensor_tensor(
                                ksq2[:, :], qn[:, NT * H:NT * (H + 1)],
                                qn[:, NT * H:NT * (H + 1)], ALU.mult))
                            for tt in range(2):
                                pe.wait(tq1, bank["tok"])
                                tm1 = pe.done(pe.raw.matmul(
                                    ps_kv[0:1, 0:512], ones_b[:, :],
                                    ksq2[:, 512 * tt:512 * (tt + 1)], start=True, stop=True))
                                dve.wait(tm1, state["nqrow_rd"])
                                te1 = dve.done(dve.raw.tensor_copy(
                                    nqrow[:, 512 * tt:512 * (tt + 1)], ps_kv[0:1, 0:512]))
                                bank["tok"] = te1
                            tq2 = dve.done(dve.raw.tensor_tensor(
                                ksq2[0:64, 0:NT], qph[0:64, NT * H:NT * (H + 1)],
                                qph[0:64, NT * H:NT * (H + 1)], ALU.mult))
                            for tt in range(2):
                                pe.wait(tq2, bank["tok"])
                                tm3 = pe.done(pe.raw.matmul(
                                    ps_kv[0:1, 0:512], ones_b[0:64, :],
                                    ksq2[0:64, 512 * tt:512 * (tt + 1)], start=True, stop=True))
                                dve.wait(tm3)
                                te3 = dve.done(dve.raw.tensor_tensor(
                                    nqrow[:, 512 * tt:512 * (tt + 1)],
                                    nqrow[:, 512 * tt:512 * (tt + 1)],
                                    ps_kv[0:1, 0:512], ALU.add))
                                bank["tok"] = te3
                                if tt == 1:
                                    state["ksq2_rd"] = tm3
                            act.wait(bank["tok"])
                            ta1 = act.done(act.raw.activation(out=nqrow[:, :], in_=nqrow[:, :],
                                                              func=AF.Ln, bias=eps_rms_t[:, :],
                                                              scale=1.0))
                            act.wait(ta1)
                            ta2 = act.done(act.raw.activation(out=nqrow[:, :], in_=nqrow[:, :],
                                                              func=AF.Exp, bias=0.0, scale=0.5))
                            state["nqrow_rd"] = ta2
                            dve.wait(ta2, shift_ready[(gg, "nk")], t_shift_dma.get(H - 1))
                            tsl = None
                            for tp in range(4):
                                tsl = dve.done(dve.raw.tensor_scalar(
                                    out=nqrow2[:, 256 * tp:256 * (tp + 1)],
                                    in0=nqrow[:, 256 * tp:256 * (tp + 1)],
                                    scalar1=scr[:, 8 + tp:9 + tp], scalar2=float(50.0 / SCALE),
                                    op0=ALU.mult, op1=ALU.add))
                            qg.wait(tsl)
                            t_shift_dma[H] = qg.done(qg.raw.dma_start(
                                out=qph[64:65, NT * H:NT * (H + 1)], in_=nqrow2[:, :]))
                        return fn
                    cls.append(mk_shift(0))
                    cls.append(mk_shift(1))
                    return cls

                def emit_spv(pend, par, nkb, Hloc, gg, first):
                    slot, st0, t_e, t_m = pend
                    vb_ = vbufs[gg % 2]
                    pe.wait(t_e, t_m)
                    if first:
                        pe.wait(t_sm_free[par], t_o_free[par])
                    for i in range(2):
                        kb = 2 * st0 + i
                        pe.raw.matmul(ps_sms[par][:, :], ones_s[:, :],
                                      pts6[slot][:, 256 * i:256 * (i + 1)],
                                      start=(kb == 0), stop=(kb == nkb - 1))
                        mmi = pe.raw.matmul(
                            ps_os[par][:, :],
                            vb_[:, 256 * kb + 128 * Hloc:256 * kb + 128 * (Hloc + 1)],
                            pts6[slot][:, 256 * i:256 * (i + 1)],
                            start=(kb == 0), stop=(kb == nkb - 1))
                    t_pv = pe.done(mmi)
                    t_pt_free[slot] = t_pv
                    return t_pv

                def flush_epilogue(e):
                    par = e["par"]
                    pe.wait(e["t_r"], bank["tok"])
                    t_bb = pe.done(pe.raw.matmul(ps_kv[:, 0:256], ones_f[:, :],
                                                 sr_at[:, 512 + 256 * par:768 + 256 * par],
                                                 start=True, stop=True))
                    dve.wait(t_bb)
                    t_rbe = dve.done(dve.raw.tensor_copy(
                        rb2_s[:, 256 * par:256 * (par + 1)], ps_kv[:, 0:256]))
                    bank["tok"] = t_rbe
                    dve.wait(t_rbe, t_ot_dma[par])
                    t_on = dve.done(dve.raw.tensor_tensor(
                        onrms[par][:, :], ps_os[par][:, :],
                        rb2_s[:, 256 * par:256 * (par + 1)], ALU.mult))
                    t_o_free[par] = t_on
                    qg.wait(t_on)
                    t_ot_dma[par] = qg.done(qg.raw.dma_start(
                        out=ot_d[128 * e["H"]:128 * (e["H"] + 1),
                                 256 * e["tp"]:256 * (e["tp"] + 1)],
                        in_=onrms[par][:, :]))

                def emit_attention(gg, inject):
                    kb_ = kngs[gg % 2]
                    inj = list(inject)
                    ii = 0

                    def do_inject():
                        nonlocal ii
                        if ii < len(inj):
                            inj[ii]()
                            ii += 1
                    for tp in range(4):
                        nkb = 8 * (tp + 1)
                        for h2 in range(2):
                            H = 2 * gg + h2
                            qrhs_n = qn[:, NT * H + 256 * tp:NT * H + 256 * (tp + 1)]
                            qrhs_p = qph[0:65, NT * H + 256 * tp:NT * H + 256 * (tp + 1)]
                            pe.wait(t_shift_dma[H], *kv_ready[gg])
                            ntile = nkb // 2
                            pq = []
                            par = state["epar"] % 2
                            first_spv = True
                            for st in range(ntile):
                                sb = state["sti"] % 3
                                state["sti"] += 1
                                Sps = Sbanks[sb]
                                pe.wait(t_Sfree[sb])
                                for i in range(2):
                                    kb = 2 * st + i
                                    colo = 256 * i
                                    pe.raw.matmul(
                                        Sps[:, colo:colo + 256],
                                        kb_[:, S * h2 + 128 * kb:S * h2 + 128 * (kb + 1)],
                                        qrhs_n, start=(i == 0), stop=False)
                                    mmi = pe.raw.matmul(
                                        Sps[:, colo:colo + 256],
                                        kpefull[0:65, 128 * kb:128 * (kb + 1)],
                                        qrhs_p, start=False, stop=(i == 1))
                                t_S = pe.done(mmi)
                                if len(pq) >= 2:
                                    emit_spv(pq.pop(0), par, nkb, h2, gg, first_spv)
                                    first_spv = False
                                if st == 2 and state["epi"] is not None:
                                    flush_epilogue(state["epi"])
                                    state["epi"] = None
                                slot = state["pti"] % 6
                                state["pti"] += 1
                                act.wait(t_S, t_pt_free[slot])
                                t_e = act.done(act.raw.activation(out=pts6[slot][:, :],
                                                                  in_=Sps[:, :], func=AF.Exp,
                                                                  bias=0.0, scale=SCALE))
                                t_Sfree[sb] = t_e
                                t_m = None
                                for i in range(2):
                                    kb = 2 * st + i
                                    j = kb - (nkb - 8)
                                    if j >= 0:
                                        dve.wait(t_e)
                                        t_m = dve.done(dve.raw.tensor_tensor(
                                            pts6[slot][:, 256 * i:256 * (i + 1)],
                                            pts6[slot][:, 256 * i:256 * (i + 1)],
                                            masks[:, 256 * j:256 * (j + 1)], ALU.mult))
                                pq.append((slot, st, t_e, t_m))
                            for pend in pq:
                                t_pv = emit_spv(pend, par, nkb, h2, gg, first_spv)
                                first_spv = False
                            pq = []
                            dve.wait(t_pv)
                            t_sev = dve.done(dve.raw.tensor_copy(
                                sr_at[:, 256 * par:256 * (par + 1)], ps_sms[par][:, :]))
                            t_sm_free[par] = t_sev
                            act.wait(t_sev)
                            t_l = act.done(act.raw.activation(
                                out=sr_at[:, 512 + 256 * par:768 + 256 * par],
                                in_=sr_at[:, 256 * par:256 * (par + 1)], func=AF.Ln,
                                bias=0.0, scale=1.0))
                            act.wait(t_l)
                            t_r = act.done(act.raw.activation(
                                out=sr_at[:, 512 + 256 * par:768 + 256 * par],
                                in_=sr_at[:, 512 + 256 * par:768 + 256 * par], func=AF.Exp,
                                bias=ln2n30[:, :], scale=-1.0))
                            if state["epi"] is not None:
                                flush_epilogue(state["epi"])
                            state["epi"] = dict(par=par, t_r=t_r, H=H, tp=tp)
                            state["epar"] += 1
                            do_inject()
                            do_inject()
                            do_inject()
                    while ii < len(inj):
                        do_inject()

                for fn in kvb_closures(0) + norm_closures(0):
                    fn()
                for gg in range(NG):
                    nxt = kvb_closures(gg + 1) + norm_closures(gg + 1) if gg + 1 < NG else []
                    emit_attention(gg, nxt)
                if state["epi"] is not None:
                    flush_epilogue(state["epi"])
                    state["epi"] = None
                em.barrier()
            em.barrier()
        if DBG:
            t_ = qg.done(qg.raw.dma_start(out=dbg["d_ot"], in_=ot_d))
            qg.wait(t_)
            em.barrier()

        NTH = NT // 2
        eg_scope = ctx.enter_context(ExitStack())
        ln2 = eg_scope.enter_context(nc.sbuf_tensor("ln2", [128, NKC * NT], BF16))
        with ExitStack() as ph:
            P = ph.enter_context
            ots = P(nc.sbuf_tensor("ots", [128, NKC * NT], BF16))
            attn = P(nc.sbuf_tensor("attn", [128, NKC * NTH], F32))
            xow = P(nc.sbuf_tensor("xow", [128, NKC * NTH], F32))
            wos0 = P(nc.sbuf_tensor("wos0", [128, NKC * 128], BF16))
            wos1 = P(nc.sbuf_tensor("wos1", [128, NKC * 128], BF16))
            bfc = P(nc.sbuf_tensor("bfc", [128, 1024], BF16))
            sqc = P(nc.sbuf_tensor("sqc", [128, 1024], BF16))
            mb2 = P(nc.sbuf_tensor("mb2", [128, 512], F32))
            rb3 = P(nc.sbuf_tensor("rb3", [128, 512], F32))
            ps_e0 = P(nc.psum_tensor("ps_e0", [128, 512], F32))
            ps_e1 = P(nc.psum_tensor("ps_e1", [128, 512], F32))
            ps_f1 = P(nc.psum_tensor("ps_f1", [1, 512], F32))
            ps_f2 = P(nc.psum_tensor("ps_f2", [1, 512], F32))
            t_ot = qs.done(qs.raw.dma_start(out=cm(ots, NKC),
                                            in_=ot_d.rearrange("(c p) n -> p c n", p=128)))
            t_x1dma = None
            for hf in range(2):
                t_xo = qs.done(qs.raw.dma_start(
                    out=cm(xow, NKC),
                    in_=x_own[:, NTH * hf:NTH * (hf + 1)].rearrange("(c p) n -> p c n", p=128)))
                t_wdma = [None, None]
                t_wfree = [None, None]
                t_eslot = [None, None]
                for m in range(NKC):
                    bw = [wos0, wos1][m % 2]
                    qg.wait(t_wfree[m % 2])
                    t_wdma[m % 2] = qg.done(qg.raw.dma_start(
                        out=cm(bw, NKC),
                        in_=woT[:, 128 * m:128 * (m + 1)].rearrange("(c p) n -> p c n", p=128)))
                    pst = [ps_e0, ps_e1][m % 2]
                    pe.wait(t_wdma[m % 2], t_ot, t_eslot[m % 2])
                    for c in range(NKC):
                        mmi = pe.raw.matmul(pst[:, :], bw[:, 128 * c:128 * (c + 1)],
                                            ots[:, NT * c + NTH * hf:NT * c + NTH * (hf + 1)],
                                            start=(c == 0), stop=(c == NKC - 1))
                    tq = pe.done(mmi)
                    t_wfree[m % 2] = tq
                    dve.wait(tq)
                    tev = dve.done(dve.raw.tensor_copy(attn[:, NTH * m:NTH * (m + 1)],
                                                       pst[:, :]))
                    t_eslot[m % 2] = tev
                em.barrier()
                av = cm(attn, NKC)
                t_rs = ln_stats(av, NTH, NKC, HID, eps_ln_t, bfc, sqc, ps_f1, ps_f2)
                dve.wait(t_xo)
                tn, _ = ln_norm(av, NTH, NKC, mb2, rb3, ps_e0, ps_e1, t_rs,
                                out_view=cm(xow, NKC), add_view=cm(xow, NKC))
                qs.wait(tn)
                t_x1dma = qs.done(qs.raw.dma_start(
                    out=x1_d[:, NTH * hf:NTH * (hf + 1)].rearrange("(c p) n -> p c n", p=128),
                    in_=cm(xow, NKC)))
                t_rs2 = ln_stats(cm(xow, NKC), NTH, NKC, HID, eps_ln_t, bfc, sqc, ps_f1, ps_f2)
                dve.wait(t_x1dma)
                tn2, _ = ln_norm(cm(xow, NKC), NTH, NKC, mb2, rb3, ps_e0, ps_e1, t_rs2,
                                 out_view=cm(ln2, NKC)[:, :, NTH * hf:NTH * (hf + 1)])
                em.barrier()
        if DBG:
            qg.done(qg.raw.dma_start(out=dbg["d_x1"].rearrange("(c p) n -> p c n", p=128),
                                      in_=x1_d.rearrange("(c p) n -> p c n", p=128)))
            t_ = qg.done(qg.raw.dma_start(
                out=dbg["d_ln2"].rearrange("(c p) n -> p c n", p=128), in_=cm(ln2, NKC)))
            qg.wait(t_)
            em.barrier()

        NTH = NT // 2
        t_out_final = None
        with ExitStack() as ph:
            P = ph.enter_context
            h2 = P(nc.sbuf_tensor("h2", [128, 40 * NTH], BF16))
            yb = P(nc.sbuf_tensor("yb", [128, NKC * NTH], F32))
            ws0 = P(nc.sbuf_tensor("ws0", [128, NKC * 128], F32))
            ws1 = P(nc.sbuf_tensor("ws1", [128, NKC * 128], F32))
            ta1 = P(nc.sbuf_tensor("ta1", [128, NKC * 128], BF16))
            tern = P(nc.sbuf_tensor("tern", [128, NKC * 128], BF16))
            wp0 = P(nc.sbuf_tensor("wp0", [128, 10 * 128], F32))
            wp1 = P(nc.sbuf_tensor("wp1", [128, 10 * 128], F32))
            tpa1 = P(nc.sbuf_tensor("tpa1", [128, 10 * 128], BF16))
            ternp = P(nc.sbuf_tensor("ternp", [128, 10 * 128], BF16))
            hfp = P(nc.sbuf_tensor("hfp", [128, NTH], F32))
            x1h = P(nc.sbuf_tensor("x1h", [128, NKC * NTH], F32))
            bfc2 = P(nc.sbuf_tensor("bfc2", [128, 1024], BF16))
            sqc2 = P(nc.sbuf_tensor("sqc2", [128, 1024], BF16))
            mb4 = P(nc.sbuf_tensor("mb4", [128, 512], F32))
            rb4 = P(nc.sbuf_tensor("rb4", [128, 512], F32))
            ps_g0 = P(nc.psum_tensor("ps_g0", [128, 512], F32))
            ps_g1 = P(nc.psum_tensor("ps_g1", [128, 512], F32))
            ps_h1 = P(nc.psum_tensor("ps_h1", [1, 512], F32))
            ps_h2 = P(nc.psum_tensor("ps_h2", [1, 512], F32))
            for hf in range(2):
                t_wfree = [None, None]
                t_h2w = None
                t_fcmm = None
                t_fctt = None
                t_gslot = [None, None]
                for m in range(40):
                    bw = [ws0, ws1][m % 2]
                    qs.wait(t_wfree[m % 2])
                    t_d = qs.done(qs.raw.dma_start(
                        out=cm(bw, NKC),
                        in_=wfcT[:, 128 * m:128 * (m + 1)].rearrange("(c p) n -> p c n", p=128)))
                    act.wait(t_d, t_fcmm, t_fctt)
                    tsa = act.done(act.raw.activation(out=ta1[:, :], in_=bw[:, :], func=AF.Sign,
                                                      bias=thr_fc[:, 0:1], scale=1.0))
                    tsb = act.done(act.raw.activation(out=tern[:, :], in_=bw[:, :], func=AF.Sign,
                                                      bias=thr_fc[:, 1:2], scale=1.0))
                    t_wfree[m % 2] = tsb
                    dve.wait(tsa, tsb)
                    tt_ = dve.done(dve.raw.tensor_tensor(tern[:, :], tern[:, :], ta1[:, :],
                                                         ALU.add))
                    t_fctt = tt_
                    pst = [ps_g0, ps_g1][m % 2]
                    pe.wait(tt_, t_gslot[m % 2])
                    for c in range(NKC):
                        mmi = pe.raw.matmul(pst[:, 0:NTH], tern[:, 128 * c:128 * (c + 1)],
                                            ln2[:, NT * c + NTH * hf:NT * c + NTH * (hf + 1)],
                                            start=(c == 0), stop=(c == NKC - 1))
                    th = pe.done(mmi)
                    t_fcmm = th
                    dve.wait(th)
                    tcp = dve.done(dve.raw.tensor_copy(hfp[:, :], pst[:, 0:NTH]))
                    t_gslot[m % 2] = tcp
                    dve.wait(tcp)
                    t_h2w = dve.done(dve.raw.scalar_tensor_tensor(
                        h2[:, NTH * m:NTH * (m + 1)], hfp[:, :], 0.0, hfp[:, :],
                        ALU.max, ALU.mult))
                t_wfree = [None, None]
                t_yev = None
                t_yslot = [None, None]
                t_pjmm = None
                t_pjtt = None
                for o in range(NKC):
                    pst = [ps_g0, ps_g1][o % 2]
                    for kh in range(4):
                        bw = [wp0, wp1][kh % 2]
                        qs.wait(t_wfree[kh % 2])
                        t_d = qs.done(qs.raw.dma_start(
                            out=cm(bw, 10),
                            in_=wpjT[1280 * kh:1280 * (kh + 1), 128 * o:128 * (o + 1)]
                            .rearrange("(c p) n -> p c n", p=128)))
                        act.wait(t_d, t_pjmm, t_pjtt)
                        tsa = act.done(act.raw.activation(out=tpa1[:, :], in_=bw[:, :],
                                                          func=AF.Sign, bias=thr_pj[:, 0:1],
                                                          scale=1.0))
                        tsb = act.done(act.raw.activation(out=ternp[:, :], in_=bw[:, :],
                                                          func=AF.Sign, bias=thr_pj[:, 1:2],
                                                          scale=1.0))
                        t_wfree[kh % 2] = tsb
                        dve.wait(tsa, tsb)
                        tt_ = dve.done(dve.raw.tensor_tensor(ternp[:, :], ternp[:, :],
                                                             tpa1[:, :], ALU.add))
                        t_pjtt = tt_
                        pe.wait(tt_, t_h2w)
                        if kh == 0:
                            pe.wait(t_yslot[o % 2])
                        for c in range(10):
                            cc = 10 * kh + c
                            mmi = pe.raw.matmul(pst[:, 0:NTH], ternp[:, 128 * c:128 * (c + 1)],
                                                h2[:, NTH * cc:NTH * (cc + 1)],
                                                start=(cc == 0), stop=(cc == 39))
                        ty = pe.done(mmi)
                        t_pjmm = ty
                    dve.wait(ty)
                    t_yev = dve.done(dve.raw.tensor_single_scalar(
                        cm(yb, NKC)[:, o, :], pst[:, 0:NTH], cy_b[:, 0:1], ALU.mult))
                    t_yslot[o % 2] = t_yev
                em.barrier()
                if DBG and hf == 0:
                    qg.done(qg.raw.dma_start(
                        out=dbg["d_h2"].rearrange("(c p) n -> p c n", p=128),
                        in_=cm(h2, 40)))
                    t_ = qg.done(qg.raw.dma_start(
                        out=dbg["d_y"].rearrange("(c p) n -> p c n", p=128),
                        in_=cm(yb, NKC)))
                    qg.wait(t_)
                    em.barrier()
                t_x1 = qs.done(qs.raw.dma_start(
                    out=cm(x1h, NKC),
                    in_=x1_d[:, NTH * hf:NTH * (hf + 1)].rearrange("(c p) n -> p c n", p=128)))
                yv = cm(yb, NKC)
                t_rs = ln_stats(yv, NTH, NKC, HID, eps_ln_t, bfc2, sqc2, ps_h1, ps_h2)
                dve.wait(t_x1)
                tn, _ = ln_norm(yv, NTH, NKC, mb4, rb4, ps_g0, ps_g1, t_rs,
                                out_view=cm(x1h, NKC), add_view=cm(x1h, NKC))
                qs.wait(tn)
                t_out_final = qs.done(qs.raw.dma_start(
                    out=outT[:, NTH * hf:NTH * (hf + 1)].rearrange("(c p) n -> p c n", p=128),
                    in_=cm(x1h, NKC)))
                em.barrier()
        qs.wait(t_out_final)
        em.barrier()
    return nc, dbg


def _prep(inputs):
    x = np.ascontiguousarray(inputs["L0_attn_hidden_states"])
    Wq = inputs["L0_attn_Wq"]
    Wkva = inputs["L0_attn_Wkva"]
    kvaw = inputs["L0_attn_kv_a_norm_weight"]
    Wkvb = inputs["L0_attn_Wkvb"]
    Wo = inputs["L0_attn_Wo"]
    cos = inputs["L0_attn_cos"]
    sin = inputs["L0_attn_sin"]
    wfc = inputs["L0_ffn_w_fc"]
    wpj = inputs["L0_ffn_w_proj"]

    f32 = np.float32
    WqT = np.ascontiguousarray(Wq.T.astype(f32, copy=False))
    wqn = np.ascontiguousarray(WqT.reshape(HID, NH, DQ)[:, :, :DN].reshape(HID, NH * DN))
    wqp = np.ascontiguousarray(WqT.reshape(HID, NH, DQ)[:, :, DN:].reshape(HID, NH * DR))
    WkvaT = np.ascontiguousarray(Wkva.T.astype(f32, copy=False))
    wkc = np.ascontiguousarray(WkvaT[:, :LORA])
    wkp = np.ascontiguousarray(WkvaT[:, LORA:])
    WkvbT = np.ascontiguousarray(Wkvb.T.astype(f32, copy=False))
    wkbk = np.ascontiguousarray(WkvbT.reshape(LORA, NH, DN + DV)[:, :, :DN]
                                .reshape(LORA, NH * DN))
    wkbv = np.ascontiguousarray(WkvbT.reshape(LORA, NH, DN + DV)[:, :, DN:]
                                .reshape(LORA, NH * DV))
    woT = np.ascontiguousarray(Wo.T.astype(f32, copy=False))
    wfcT = np.ascontiguousarray(wfc.T.astype(f32, copy=False))
    wpjT = np.ascontiguousarray(wpj.T.astype(f32, copy=False))
    kvw = np.ascontiguousarray(np.asarray(kvaw).reshape(4, 128).T.astype(f32, copy=False))
    cosr = np.ascontiguousarray(np.asarray(cos).astype(f32, copy=False)).reshape(1, DR)
    sinr = np.ascontiguousarray(np.asarray(sin).astype(f32, copy=False)).reshape(1, DR)

    in_maps = []
    for c in range(8):
        beta, r = c // 4, c % 4
        xb = np.asarray(x[beta]).T.astype(f32, copy=False)
        idx = (np.arange(S) // 4) * 4 + (np.arange(S) + r) % 4
        xrot = np.ascontiguousarray(xb[:, idx])
        xown = np.ascontiguousarray(xb[:, r::4])
        ki = np.arange(128)
        A = 4 * (ki // 4) + (ki + r) % 4 - r
        cmeta_v = np.zeros((128, 8), f32)
        cmeta_v[:, 0] = A.astype(f32)
        in_maps.append({
            "xT": xrot, "x_own": xown, "cmeta": cmeta_v,
            "wqn": wqn, "wqp": wqp, "wkc": wkc, "wkp": wkp,
            "wkbk": wkbk, "wkbv": wkbv, "woT": woT,
            "wfcT": wfcT, "wpjT": wpjT, "kvw": kvw,
            "cosr": cosr, "sinr": sinr,
        })
    return in_maps


def kernel(**inputs):
    in_maps = _prep(inputs)
    nc, _ = build()
    res = run_bass_kernel_spmd(nc, in_maps, core_ids=list(range(8)))
    out = np.empty((B, S, HID), np.float32)
    for c in range(8):
        beta, r = c // 4, c % 4
        out[beta, r::4, :] = res.results[c]["outT"].T
    return out


if __name__ == "__main__":
    nc, _ = build()
    print("built:", nc.n_instructions(), "instructions")


# version 08
# speedup vs baseline: 1.1577x; 1.0299x over previous
import numpy as np
from contextlib import ExitStack

import concourse.bass as bass
import concourse.mybir as mybir
from concourse.bass_utils import run_bass_kernel_spmd

F32 = mybir.dt.float32
BF16 = mybir.dt.bfloat16
AF = mybir.ActivationFunctionType
ALU = mybir.AluOpType
AX = mybir.AxisListType

B, S, HID = 2, 4096, 2048
NH, DN, DR, DQ, DV, LORA = 16, 128, 64, 192, 128, 512
SCALE = float(DQ) ** -0.5
EPS_LN = 1e-5
EPS_RMS = 1e-6
HF = 5120
NT = S // 4
NKC = HID // 128
DBG = False


class EW:
    def __init__(self, raw, sem, name, dma=False):
        self.raw = raw
        self.sem = sem
        self.n = 0
        self.name = name
        self.step = 16 if dma else 1
        self.waited = {}

    def wait(self, *toks):
        for t in toks:
            if t is None:
                continue
            sem, n = t
            k = sem.name
            if self.waited.get(k, 0) >= n:
                continue
            self.raw.wait_ge(sem, n)
            self.waited[k] = n

    def done(self, inst):
        self.n += self.step
        inst.then_inc(self.sem, self.step)
        return (self.sem, self.n)

    def tok(self):
        return (self.sem, self.n) if self.n else None


class Em:
    def __init__(self, nc, ctx):
        self.nc = nc
        E = ctx.enter_context
        self.pe = EW(nc.tensor, E(nc.semaphore("s_pe")), "pe")
        self.act = EW(nc.scalar, E(nc.semaphore("s_act")), "act")
        self.dve = EW(nc.vector, E(nc.semaphore("s_dve")), "dve")
        self.gpc = EW(nc.gpsimd, E(nc.semaphore("s_gpc")), "gpc")
        self.qs = EW(nc.sync, E(nc.semaphore("s_qs")), "qs", dma=True)
        self.qg = EW(nc.gpsimd, E(nc.semaphore("s_qg")), "qg", dma=True)
        self.qa = EW(nc.scalar, E(nc.semaphore("s_qa")), "qa", dma=True)
        self.all = [self.pe, self.act, self.dve, self.gpc, self.qs, self.qg, self.qa]

    def barrier(self):
        toks = [e.tok() for e in self.all]
        for e in self.all:
            e.wait(*toks)


def cm(t, c):
    return t[:, :].rearrange("p (c n) -> p c n", c=c)


def build():
    nc = bass.Bass("TRN2", target_bir_lowering=False)

    def din(name, shape, dt=F32):
        return nc.dram_tensor(name, shape, dt, kind="ExternalInput").ap()

    xT = din("xT", [HID, S])
    x_own = din("x_own", [HID, NT])
    cmeta = din("cmeta", [128, 8])
    wqn = din("wqn", [HID, NH * DN])
    wqp = din("wqp", [HID, NH * DR])
    wkc = din("wkc", [HID, LORA])
    wkp = din("wkp", [HID, DR])
    wkbk = din("wkbk", [LORA, NH * DN])
    wkbv = din("wkbv", [LORA, NH * DV])
    woT = din("woT", [NH * DV, HID])
    wfcT = din("wfcT", [HID, HF])
    wpjT = din("wpjT", [HF, HID])
    kvw = din("kvw", [128, 4])
    cosr = din("cosr", [1, DR])
    sinr = din("sinr", [1, DR])

    outT = nc.dram_tensor("outT", [HID, NT], F32, kind="ExternalOutput").ap()

    rms_d = nc.dram_tensor("rms_d", [LORA, S], BF16, kind="Internal").ap()
    ot_d = nc.dram_tensor("ot_d", [NH * DV, NT], BF16, kind="Internal").ap()
    x1_d = nc.dram_tensor("x1_d", [HID, NT], F32, kind="Internal").ap()
    sqq_d = nc.dram_tensor("sqq_d", [NH, NT], F32, kind="Internal").ap()

    dbg = {}
    if DBG:
        def dout(name, shape, dt=F32):
            dbg[name] = nc.dram_tensor(name, shape, dt, kind="ExternalOutput").ap()
            return dbg[name]
        dout("d_ln1own", [HID, NT], BF16)
        dout("d_rms", [LORA, S], BF16)
        dout("d_kpe", [DR, S], BF16)
        dout("d_qn", [NH * DN, NT], BF16)
        dout("d_qp", [8 * 128, NT], BF16)
        dout("d_kn0", [2 * DN, S], BF16)
        dout("d_shift", [1, NH * NT], BF16)
        dout("d_sqq", [NH, NT], F32)
        dout("d_ot", [NH * DV, NT], BF16)
        dout("d_x1", [HID, NT], F32)
        dout("d_am", [1, 4], F32)
        dout("d_ln2", [HID, NT], BF16)
        dout("d_h2", [HF, NT // 2], BF16)
        dout("d_y", [HID, NT // 2], F32)

    ctx = ExitStack()
    with ctx:
        em = Em(nc, ctx)
        pe, act, dve, gpc, qs, qg, qa = em.pe, em.act, em.dve, em.gpc, em.qs, em.qg, em.qa
        E = ctx.enter_context

        ones_b = E(nc.sbuf_tensor("ones_b", [128, 1], BF16))
        ones_s = E(nc.sbuf_tensor("ones_s", [128, 1], BF16))
        ln2n30 = E(nc.sbuf_tensor("ln2n30", [1, 1], F32))
        ones_f = E(nc.sbuf_tensor("ones_f", [1, 128], F32))
        onesc_f = E(nc.sbuf_tensor("onesc_f", [128, 1], F32))
        eps_ln_t = E(nc.sbuf_tensor("eps_ln", [1, 1], F32))
        eps_rms_t = E(nc.sbuf_tensor("eps_rms", [1, 1], F32))
        cmeta_t = E(nc.sbuf_tensor("cmeta_t", [128, 8], F32))
        kvw_t = E(nc.sbuf_tensor("kvw_t", [128, 4], F32))
        cosb = E(nc.sbuf_tensor("cosb", [128, DR], F32))
        spb = E(nc.sbuf_tensor("spb", [128, DR], F32))
        nmax = E(nc.sbuf_tensor("nmax", [1, 4], F32))
        scr = E(nc.sbuf_tensor("scr", [1, 16], F32))
        amv = E(nc.sbuf_tensor("amv", [1, 4], F32))
        thr_fc = E(nc.sbuf_tensor("thr_fc", [128, 2], F32))
        thr_pj = E(nc.sbuf_tensor("thr_pj", [128, 2], F32))
        cy_b = E(nc.sbuf_tensor("cy_b", [128, 1], F32))
        srow = E(nc.sbuf_tensor("srow", [1, 4 * 512], F32))
        SM, SQ, SVAR, SRS = 0, 1, 2, 3

        def row(i, w=512):
            return srow[:, 512 * i:512 * i + w]

        def bc_scalar(dst_col128, src_row1, ps):
            t1 = pe.done(pe.raw.matmul(ps[:, 0:1], ones_f[:, :], src_row1, start=True, stop=True))
            dve.wait(t1)
            return dve.done(dve.raw.tensor_copy(dst_col128, ps[:, 0:1]))

        def ln_stats(src_view, T, chunks, n_dims, eps_t, bfc, sqc, ps1, ps2):
            t_mm = None
            t_slot = [None, None]
            for c in range(chunks):
                sl = c % 2
                dve.wait(t_slot[sl])
                tc = dve.done(dve.raw.tensor_copy(bfc[:, T * sl:T * (sl + 1)],
                                                  src_view[:, c, :]))
                act.wait(t_slot[sl])
                ts = act.done(act.raw.activation(out=sqc[:, T * sl:T * (sl + 1)],
                                                 in_=src_view[:, c, :],
                                                 func=AF.Square, bias=0.0, scale=1.0))
                pe.wait(tc, ts)
                pe.raw.matmul(ps1[:, 0:T], ones_b[:, :], bfc[:, T * sl:T * (sl + 1)],
                              start=(c == 0), stop=(c == chunks - 1))
                mmi = pe.raw.matmul(ps2[:, 0:T], ones_b[:, :], sqc[:, T * sl:T * (sl + 1)],
                                    start=(c == 0), stop=(c == chunks - 1))
                t_mm = pe.done(mmi)
                t_slot[sl] = t_mm
            dve.wait(t_mm)
            t1 = dve.done(dve.raw.tensor_single_scalar(row(SM, T), ps1[:, 0:T],
                                                       1.0 / n_dims, ALU.mult))
            t2 = dve.done(dve.raw.tensor_single_scalar(row(SQ, T), ps2[:, 0:T],
                                                       1.0 / n_dims, ALU.mult))
            dve.wait(t1)
            t3 = dve.done(dve.raw.tensor_tensor(row(SVAR, T), row(SM, T), row(SM, T), ALU.mult))
            dve.wait(t2, t3)
            t4 = dve.done(dve.raw.tensor_tensor(row(SVAR, T), row(SQ, T), row(SVAR, T),
                                                ALU.subtract))
            act.wait(t4)
            t5 = act.done(act.raw.activation(out=row(SRS, T), in_=row(SVAR, T), func=AF.Ln,
                                             bias=eps_t[:, :], scale=1.0))
            act.wait(t5)
            return act.done(act.raw.activation(out=row(SRS, T), in_=row(SRS, T), func=AF.Exp,
                                               bias=0.0, scale=-0.5))

        def ln_norm(src_view, T, chunks, mb_sb, rb_sb, ps_a, ps_b, t_rs, out_view,
                    add_view=None):
            pe.wait(t_rs)
            tbm = pe.done(pe.raw.matmul(ps_a[:, 0:T], ones_f[:, :], row(SM, T),
                                        start=True, stop=True))
            tbr = pe.done(pe.raw.matmul(ps_b[:, 0:T], ones_f[:, :], row(SRS, T),
                                        start=True, stop=True))
            dve.wait(tbm)
            tem = dve.done(dve.raw.tensor_copy(mb_sb[:, 0:T], ps_a[:, 0:T]))
            dve.wait(tbr)
            ter = dve.done(dve.raw.tensor_copy(rb_sb[:, 0:T], ps_b[:, 0:T]))
            mb_b = mb_sb[:, 0:T].rearrange("p (o n) -> p o n", o=1).broadcast_to([128, chunks, T])
            rb_b = rb_sb[:, 0:T].rearrange("p (o n) -> p o n", o=1).broadcast_to([128, chunks, T])
            dve.wait(tem)
            tn1 = dve.done(dve.raw.tensor_tensor(src_view, src_view, mb_b, ALU.subtract))
            dve.wait(tn1, ter)
            if add_view is None:
                tn2 = dve.done(dve.raw.tensor_tensor(out_view, src_view, rb_b, ALU.mult))
            else:
                tn2a = dve.done(dve.raw.tensor_tensor(src_view, src_view, rb_b, ALU.mult))
                dve.wait(tn2a)
                tn2 = dve.done(dve.raw.tensor_tensor(out_view, src_view, add_view, ALU.add))
            return tn2, (tbm, tbr)

        t_cm = qs.done(qs.raw.dma_start(out=cmeta_t[:, :], in_=cmeta))
        qs.done(qs.raw.dma_start(out=kvw_t[:, :], in_=kvw))
        dve.done(dve.raw.memset(ones_b[:, :], 1.0))
        dve.done(dve.raw.memset(ones_s[:, :], float(2.0 ** -30)))
        dve.done(dve.raw.memset(ln2n30[:, :], float(-30.0 * np.log(2.0))))
        t_onef = dve.done(dve.raw.memset(ones_f[:, :], 1.0))
        dve.done(dve.raw.memset(onesc_f[:, :], 1.0))
        dve.done(dve.raw.memset(eps_ln_t[:, :], EPS_LN))
        t_ep2 = dve.done(dve.raw.memset(eps_rms_t[:, :], EPS_RMS))

        with ExitStack() as bd_scope:
            masks = bd_scope.enter_context(nc.sbuf_tensor("masks", [128, 8 * 256], BF16))
            kpefull = bd_scope.enter_context(nc.sbuf_tensor("kpefull", [65, S], BF16))

            with ExitStack() as ph:
                cs_row = ph.enter_context(nc.sbuf_tensor("cs_row", [1, 2 * DR], F32))
                iq = ph.enter_context(nc.sbuf_tensor("iq", [128, 256], F32))
                iqa = ph.enter_context(nc.sbuf_tensor("iqa", [128, 256], F32))
                ps_b0 = ph.enter_context(nc.psum_tensor("ps_b0", [128, 512], F32))
                t_cos = qs.done(qs.raw.dma_start(out=cs_row[:, 0:DR], in_=cosr))
                t_sin = qs.done(qs.raw.dma_start(out=cs_row[:, DR:2 * DR], in_=sinr))
                dve.wait(t_sin)
                t_neg = dve.done(dve.raw.tensor_single_scalar(
                    cs_row[:, DR:DR + 32], cs_row[:, DR:DR + 32], -1.0, ALU.mult))
                pe.wait(t_cos, t_onef)
                t1 = pe.done(pe.raw.matmul(ps_b0[:, 0:DR], ones_f[:, :], cs_row[:, 0:DR],
                                           start=True, stop=True))
                dve.wait(t1)
                t_cb = dve.done(dve.raw.tensor_copy(cosb[:, :], ps_b0[:, 0:DR]))
                pe.wait(t_cb, t_neg)
                t2 = pe.done(pe.raw.matmul(ps_b0[:, 0:DR], ones_f[:, :], cs_row[:, DR:2 * DR],
                                           start=True, stop=True))
                dve.wait(t2)
                t_sb = dve.done(dve.raw.tensor_copy(spb[:, :], ps_b0[:, 0:DR]))
                t_io = gpc.done(gpc.raw.iota(iq[:, :], pattern=[[4, 256]], base=0,
                                             channel_multiplier=0,
                                             allow_small_or_imprecise_dtypes=True))
                dve.wait(t_io, t_cm)
                t_iqa = dve.done(dve.raw.tensor_single_scalar(iqa[:, :], iq[:, :],
                                                              cmeta_t[:, 0:1], ALU.subtract))
                dve.wait(t_iqa)
                for j in range(8):
                    mj = dve.raw.tensor_single_scalar(masks[:, 256 * j:256 * (j + 1)],
                                                      iqa[:, :], float(128 * j), ALU.is_ge)
                dve.done(mj)
                em.barrier()

            NB = 256
            NBLK = S // NB
            with ExitStack() as bc_scope:
                ln1own = bc_scope.enter_context(nc.sbuf_tensor("ln1own", [128, NKC * NT], BF16))

                with ExitStack() as ph:
                    P = ph.enter_context
                    wkc_t = P(nc.sbuf_tensor("wkc_t", [128, NKC * 512], BF16))
                    weffk = P(nc.sbuf_tensor("weffk", [128, NKC * DR], BF16))
                    xblk0 = P(nc.sbuf_tensor("xblk0", [128, NKC * NB], F32))
                    xblk1 = P(nc.sbuf_tensor("xblk1", [128, NKC * NB], F32))
                    xbf = P(nc.sbuf_tensor("xbf", [128, NKC * NB], BF16))
                    xsq = P(nc.sbuf_tensor("xsq", [128, NKC * NB], BF16))
                    mb_s = P(nc.sbuf_tensor("mb_s", [128, NB], F32))
                    rb_s = P(nc.sbuf_tensor("rb_s", [128, NB], F32))
                    ln1b = P(nc.sbuf_tensor("ln1b", [128, NKC * NB], BF16))
                    cf32 = P(nc.sbuf_tensor("cf32", [128, 4 * NB], F32))
                    csq = P(nc.sbuf_tensor("csq", [128, 4 * NB], BF16))
                    rr_s = P(nc.sbuf_tensor("rr_s", [128, NB], F32))
                    rmsb = P(nc.sbuf_tensor("rmsb", [128, 4 * NB], BF16))
                    ps_s1 = P(nc.psum_tensor("ps_s1", [1, 512], F32))
                    ps_s2 = P(nc.psum_tensor("ps_s2", [1, 512], F32))
                    ps_c0 = P(nc.psum_tensor("ps_c0", [128, 512], F32))
                    ps_c1 = P(nc.psum_tensor("ps_c1", [128, 512], F32))
                    ps_c2 = P(nc.psum_tensor("ps_c2", [128, 512], F32))
                    ps_c3 = P(nc.psum_tensor("ps_c3", [128, 512], F32))
                    ps_kp = P(nc.psum_tensor("ps_kp", [64, 512], F32))
                    cps = [ps_c0, ps_c1, ps_c2, ps_c3]

                    t_wkc = qg.done(qg.raw.dma_start(
                        out=cm(wkc_t, NKC), in_=wkc.rearrange("(c p) n -> p c n", p=128)))
                    with ExitStack() as ph2:
                        wkpf = ph2.enter_context(nc.sbuf_tensor("wkpf", [128, NKC * DR], F32))
                        wktmp = ph2.enter_context(nc.sbuf_tensor("wktmp", [128, NKC * DR], F32))
                        t_wkp = qs.done(qs.raw.dma_start(
                            out=cm(wkpf, NKC), in_=wkp.rearrange("(c p) n -> p c n", p=128)))
                        dve.wait(t_wkp, t_sb)
                        cb3 = cosb[:, :].rearrange("p (o d) -> p o d", o=1) \
                            .broadcast_to([128, NKC, DR])
                        sb4 = spb[:, :].rearrange("p (o t d) -> p o t d", o=1, t=2) \
                            .broadcast_to([128, NKC, 2, 32])
                        t1 = dve.done(dve.raw.tensor_tensor(cm(wktmp, NKC), cm(wkpf, NKC),
                                                            cb3, ALU.mult))
                        wperm = wkpf[:, :].rearrange("p (c t d) -> p c t d",
                                                     c=NKC, t=2)[:, :, 1::-1, :]
                        weffk4 = weffk[:, :].rearrange("p (c t d) -> p c t d", c=NKC, t=2)
                        t2 = dve.done(dve.raw.tensor_tensor(weffk4, wperm, sb4, ALU.mult))
                        dve.wait(t1, t2)
                        t_wk = dve.done(dve.raw.tensor_tensor(weffk[:, :], weffk[:, :],
                                                              wktmp[:, :], ALU.add))

                    xb_ = [xblk0, xblk1]
                    t_ck = [None, None]
                    t_scr_free = None
                    t_rmsn = None
                    t_rms_dma = None
                    t_kpe_ev = None
                    t_ps_s_rd = None
                    t_r2_prev = None
                    t_ps_c_rd = None
                    t_xd = [None, None]

                    def x_dma(b):
                        slb = xb_[b % 2]
                        qs.wait(t_ck[b % 2])
                        t_xd[b % 2] = qs.done(qs.raw.dma_start(
                            out=cm(slb, NKC),
                            in_=xT[:, NB * b:NB * (b + 1)].rearrange("(c p) n -> p c n",
                                                                     p=128)))
                    x_dma(0)
                    for b in range(NBLK):
                        sl = xb_[b % 2]
                        slv = cm(sl, NKC)
                        t_x = t_xd[b % 2]
                        if b + 1 < NBLK:
                            x_dma(b + 1)
                        dve.wait(t_x, t_scr_free)
                        tc = dve.done(dve.raw.tensor_copy(xbf[:, :], sl[:, :]))
                        act.wait(t_x, t_scr_free)
                        ts = act.done(act.raw.activation(out=xsq[:, :], in_=sl[:, :],
                                                         func=AF.Square, bias=0.0, scale=1.0))
                        pe.wait(tc, ts, t_ps_s_rd)
                        for c in range(NKC):
                            pe.raw.matmul(ps_s1[:, 0:NB], ones_b[:, :],
                                          xbf[:, NB * c:NB * (c + 1)],
                                          start=(c == 0), stop=(c == NKC - 1))
                            mmi = pe.raw.matmul(ps_s2[:, 0:NB], ones_b[:, :],
                                                xsq[:, NB * c:NB * (c + 1)],
                                                start=(c == 0), stop=(c == NKC - 1))
                        t_mm = pe.done(mmi)
                        dve.wait(t_mm)
                        t1 = dve.done(dve.raw.tensor_single_scalar(row(SM, NB), ps_s1[:, 0:NB],
                                                                   1.0 / HID, ALU.mult))
                        t2 = dve.done(dve.raw.tensor_single_scalar(row(SQ, NB), ps_s2[:, 0:NB],
                                                                   1.0 / HID, ALU.mult))
                        dve.wait(t1)
                        t3 = dve.done(dve.raw.tensor_tensor(row(SVAR, NB), row(SM, NB),
                                                            row(SM, NB), ALU.mult))
                        dve.wait(t2, t3)
                        t4 = dve.done(dve.raw.tensor_tensor(row(SVAR, NB), row(SQ, NB),
                                                            row(SVAR, NB), ALU.subtract))
                        act.wait(t4)
                        t5 = act.done(act.raw.activation(out=row(SRS, NB), in_=row(SVAR, NB),
                                                         func=AF.Ln, bias=eps_ln_t[:, :], scale=1.0))
                        act.wait(t5)
                        t6 = act.done(act.raw.activation(out=row(SRS, NB), in_=row(SRS, NB),
                                                         func=AF.Exp, bias=0.0, scale=-0.5))
                        pe.wait(t6, t_ps_c_rd)
                        tbm = pe.done(pe.raw.matmul(ps_c0[:, 0:NB], ones_f[:, :], row(SM, NB),
                                                    start=True, stop=True))
                        tbr = pe.done(pe.raw.matmul(ps_c1[:, 0:NB], ones_f[:, :], row(SRS, NB),
                                                    start=True, stop=True))
                        dve.wait(tbm)
                        tem = dve.done(dve.raw.tensor_copy(mb_s[:, :], ps_c0[:, 0:NB]))
                        dve.wait(tbr)
                        ter = dve.done(dve.raw.tensor_copy(rb_s[:, :], ps_c1[:, 0:NB]))
                        mb_b = mb_s[:, :].rearrange("p (o n) -> p o n", o=1) \
                            .broadcast_to([128, NKC, NB])
                        rb_b = rb_s[:, :].rearrange("p (o n) -> p o n", o=1) \
                            .broadcast_to([128, NKC, NB])
                        dve.wait(tem)
                        tn1 = dve.done(dve.raw.tensor_tensor(slv, slv, mb_b, ALU.subtract))
                        dve.wait(tn1, ter, t_ck[b % 2])
                        tln = dve.done(dve.raw.tensor_tensor(cm(ln1b, NKC), slv, rb_b, ALU.mult))
                        dve.wait(tln)
                        t_rp = dve.done(dve.raw.tensor_copy(
                            cm(ln1own, NKC)[:, :, (NB // 4) * b:(NB // 4) * (b + 1)],
                            cm(ln1b, NKC)[:, :, 0:NB:4]))
                        pe.wait(tln, tem, ter, t_wkc, t_wk, t_rmsn, t_kpe_ev)
                        for c in range(NKC):
                            for m in range(4):
                                pe.raw.matmul(cps[m][:, 0:NB],
                                              wkc_t[:, 512 * c + 128 * m:512 * c + 128 * (m + 1)],
                                              ln1b[:, NB * c:NB * (c + 1)],
                                              start=(c == 0), stop=(c == NKC - 1))
                            mmi = pe.raw.matmul(ps_kp[:, 0:NB], weffk[:, DR * c:DR * (c + 1)],
                                                ln1b[:, NB * c:NB * (c + 1)],
                                                start=(c == 0), stop=(c == NKC - 1))
                        t_ckv = pe.done(mmi)
                        t_ck[b % 2] = t_ckv
                        dve.wait(t_ckv)
                        for m in range(4):
                            cpi = dve.raw.tensor_copy(cf32[:, NB * m:NB * (m + 1)], cps[m][:, 0:NB])
                        dve.done(cpi)
                        t_cf = dve.done(dve.raw.tensor_copy(kpefull[0:64, NB * b:NB * (b + 1)],
                                                            ps_kp[:, 0:NB]))
                        t_kpe_ev = t_cf
                        act.wait(t_cf, t_r2_prev)
                        t_csq = act.done(act.raw.activation(out=csq[:, :], in_=cf32[:, :],
                                                            func=AF.Square, bias=0.0, scale=1.0))
                        pe.wait(t_csq)
                        for m in range(4):
                            mmi = pe.raw.matmul(ps_s1[:, 0:NB], ones_b[:, :],
                                                csq[:, NB * m:NB * (m + 1)],
                                                start=(m == 0), stop=(m == 3))
                        t_r2 = pe.done(mmi)
                        t_r2_prev = t_r2
                        act.wait(t_r2, t_ep2)
                        t_lr = act.done(act.raw.activation(out=row(SQ, NB), in_=ps_s1[:, 0:NB],
                                                           func=AF.Ln, bias=eps_rms_t[:, :],
                                                           scale=1.0 / LORA))
                        t_ps_s_rd = t_lr
                        act.wait(t_lr)
                        t_er = act.done(act.raw.activation(out=row(SQ, NB), in_=row(SQ, NB),
                                                           func=AF.Exp, bias=0.0, scale=-0.5))
                        pe.wait(t_er)
                        t_br2 = pe.done(pe.raw.matmul(ps_c0[:, 0:NB], ones_f[:, :], row(SQ, NB),
                                                      start=True, stop=True))
                        dve.wait(t_br2)
                        t_err = dve.done(dve.raw.tensor_copy(rr_s[:, :], ps_c0[:, 0:NB]))
                        t_ps_c_rd = t_err
                        dve.wait(t_err, t_rms_dma)
                        for m in range(4):
                            sti = dve.raw.scalar_tensor_tensor(
                                rmsb[:, NB * m:NB * (m + 1)], cf32[:, NB * m:NB * (m + 1)],
                                kvw_t[:, m:m + 1], rr_s[:, :], ALU.mult, ALU.mult)
                        t_rmsn = dve.done(sti)
                        qs.wait(t_rmsn)
                        t_rms_dma = qs.done(qs.raw.dma_start(
                            out=rms_d[:, NB * b:NB * (b + 1)].rearrange("(m p) n -> p m n", p=128),
                            in_=cm(rmsb, 4)))
                    t_kones = dve.done(dve.raw.memset(kpefull[64:65, :], 1.0))
                    em.barrier()
                if DBG:
                    qg.done(qg.raw.dma_start(
                        out=dbg["d_ln1own"].rearrange("(c p) n -> p c n", p=128),
                        in_=cm(ln1own, NKC)))
                    qg.done(qg.raw.dma_start(out=dbg["d_kpe"], in_=kpefull[0:64, :]))
                    t_ = qg.done(qg.raw.dma_start(out=dbg["d_rms"], in_=rms_d))
                    qg.wait(t_)
                    em.barrier()

                qn = bd_scope.enter_context(nc.sbuf_tensor("qn", [128, NH * NT], BF16,
                                                           side="right"))
                qph = bd_scope.enter_context(nc.sbuf_tensor("qph", [65, NH * NT], BF16,
                                                            side="right"))
                with ExitStack() as ph:
                    P = ph.enter_context
                    wtam0 = P(nc.sbuf_tensor("wtam0", [128, 4096], F32))
                    wtam1 = P(nc.sbuf_tensor("wtam1", [128, 4096], F32))
                    racc = P(nc.sbuf_tensor("racc", [128, 2], F32))
                    rtmp = P(nc.sbuf_tensor("rtmp", [128, 1], F32))
                    t_amz = dve.done(dve.raw.memset(racc[:, :], 0.0))
                    am_tiles = []
                    for wi, w_ap in enumerate([wfcT, wpjT]):
                        flat = w_ap.rearrange("a b -> (a b)").rearrange(
                            "(t p n) -> t p n", p=128, n=4096)
                        for ti in range(flat.shape[0]):
                            am_tiles.append((wi, flat[ti]))
                    am_state = {"i": 0, "slot_tok": [None, None], "acc": t_amz}

                    def am_step(k=1):
                        for _ in range(k):
                            i = am_state["i"]
                            if i >= len(am_tiles):
                                return
                            wi, tile = am_tiles[i]
                            sl = i % 2
                            bufw = [wtam0, wtam1][sl]
                            qa.wait(am_state["slot_tok"][sl])
                            td = qa.done(qa.raw.dma_start(out=bufw[:, :], in_=tile))
                            dve.wait(td, am_state["acc"])
                            tr = dve.done(dve.raw.tensor_reduce(
                                rtmp[:, :], bufw[:, :], axis=AX.X, op=ALU.add,
                                apply_absolute_value=True))
                            dve.wait(tr)
                            ta = dve.done(dve.raw.tensor_tensor(
                                racc[:, wi:wi + 1], racc[:, wi:wi + 1], rtmp[:, :], ALU.add))
                            am_state["acc"] = ta
                            am_state["slot_tok"][sl] = ta
                            am_state["i"] = i + 1

                    weffq = P(nc.sbuf_tensor("weffq", [128, NKC * 128], BF16))
                    wqs0 = P(nc.sbuf_tensor("wqs0", [128, NKC * 128], BF16))
                    wqs1 = P(nc.sbuf_tensor("wqs1", [128, NKC * 128], BF16))
                    wqpf0 = P(nc.sbuf_tensor("wqpf0", [128, NKC * 128], F32))
                    wqpf1 = P(nc.sbuf_tensor("wqpf1", [128, NKC * 128], F32))
                    wqtmp = P(nc.sbuf_tensor("wqtmp", [128, NKC * 128], F32))
                    weffpf = P(nc.sbuf_tensor("weffpf", [128, NKC * 128], F32))
                    ps_q0 = P(nc.psum_tensor("ps_q0", [128, 512], F32))
                    ps_q1 = P(nc.psum_tensor("ps_q1", [128, 512], F32))

                    t_dma = [None, None]
                    t_cons = [None, None]
                    t_weff_cons = None
                    t_qslot = [None, None]

                    def qp_dma(p):
                        buf_ = [wqpf0, wqpf1][p % 2]
                        qs.wait(t_cons[p % 2])
                        t_dma[p % 2] = qs.done(qs.raw.dma_start(
                            out=cm(buf_, NKC),
                            in_=wqp[:, 128 * p:128 * (p + 1)]
                            .rearrange("(c pp) n -> pp c n", pp=128)))
                    qp_dma(0)
                    for p in range(8):
                        am_step(2)
                        buf = [wqpf0, wqpf1][p % 2]
                        t_dcur = t_dma[p % 2]
                        if p + 1 < 8:
                            qp_dma(p + 1)
                        dve.wait(t_dcur, t_weff_cons)
                        w4 = buf[:, :].rearrange("pp (c h t d) -> pp c h t d", c=NKC, h=2, t=2)
                        cb4 = cosb[:, :].rearrange("pp (o t d) -> pp o t d", o=1, t=2) \
                            .rearrange("pp o t d -> pp o () t d") \
                            .broadcast_to([128, NKC, 2, 2, 32])
                        sb4 = spb[:, :].rearrange("pp (o t d) -> pp o t d", o=1, t=2) \
                            .rearrange("pp o t d -> pp o () t d") \
                            .broadcast_to([128, NKC, 2, 2, 32])
                        ta = dve.done(dve.raw.tensor_tensor(
                            wqtmp[:, :].rearrange("pp (c h t d) -> pp c h t d", c=NKC, h=2, t=2),
                            w4, cb4, ALU.mult))
                        tb = dve.done(dve.raw.tensor_tensor(
                            weffpf[:, :].rearrange("pp (c h t d) -> pp c h t d", c=NKC, h=2, t=2),
                            w4[:, :, :, 1::-1, :], sb4, ALU.mult))
                        t_cons[p % 2] = tb
                        dve.wait(ta, tb)
                        tcq = dve.done(dve.raw.tensor_tensor(weffq[:, :], weffpf[:, :],
                                                             wqtmp[:, :], ALU.add))
                        for hh in range(2):
                            h = 2 * p + hh
                            for tt in range(2):
                                pst = [ps_q0, ps_q1][tt]
                                pe.wait(tcq, t_qslot[tt])
                                for c in range(NKC):
                                    mmi = pe.raw.matmul(
                                        pst[0:64, :],
                                        weffq[:, 128 * c + 64 * hh:128 * c + 64 * (hh + 1)],
                                        ln1own[:, NT * c + 512 * tt:NT * c + 512 * (tt + 1)],
                                        start=(c == 0), stop=(c == NKC - 1))
                                tq = pe.done(mmi)
                                if hh == 1 and tt == 1:
                                    t_weff_cons = tq
                                dve.wait(tq)
                                tev = dve.done(dve.raw.tensor_copy(
                                    qph[0:64, NT * h + 512 * tt:NT * h + 512 * (tt + 1)],
                                    pst[0:64, :]))
                                t_qslot[tt] = tev
                    t_dma = [None, None]
                    t_cons = [None, None]

                    def qn_dma(h):
                        buf_ = [wqs0, wqs1][h % 2]
                        qg.wait(t_cons[h % 2])
                        t_dma[h % 2] = qg.done(qg.raw.dma_start(
                            out=cm(buf_, NKC),
                            in_=wqn[:, 128 * h:128 * (h + 1)]
                            .rearrange("(c p) n -> p c n", p=128)))
                    qn_dma(0)
                    for h in range(NH):
                        am_step(2)
                        buf = [wqs0, wqs1][h % 2]
                        t_dcur = t_dma[h % 2]
                        if h + 1 < NH:
                            qn_dma(h + 1)
                        for tt in range(2):
                            pst = [ps_q0, ps_q1][tt]
                            pe.wait(t_dcur, t_qslot[tt])
                            for c in range(NKC):
                                mmi = pe.raw.matmul(
                                    pst[:, :], buf[:, 128 * c:128 * (c + 1)],
                                    ln1own[:, NT * c + 512 * tt:NT * c + 512 * (tt + 1)],
                                    start=(c == 0), stop=(c == NKC - 1))
                            tq = pe.done(mmi)
                            if tt == 1:
                                t_cons[h % 2] = tq
                            dve.wait(tq)
                            tev = dve.done(dve.raw.tensor_copy(
                                qn[:, NT * h + 512 * tt:NT * h + 512 * (tt + 1)], pst[:, :]))
                            t_qslot[tt] = tev
                    with ExitStack() as ph2:
                        sqsc2 = ph2.enter_context(nc.sbuf_tensor("sqsc2", [128, NT], BF16))
                        qrow = ph2.enter_context(nc.sbuf_tensor("qrow", [1, NT], F32))
                        ps_nq = ph2.enter_context(nc.psum_tensor("ps_nq", [1, 512], F32))
                        t_qdma = None
                        t_rd = None
                        t_psrd = None
                        for H in range(NH):
                            dve.wait(t_rd)
                            tq1 = dve.done(dve.raw.tensor_tensor(
                                sqsc2[:, :], qn[:, NT * H:NT * (H + 1)],
                                qn[:, NT * H:NT * (H + 1)], ALU.mult))
                            for tt in range(2):
                                pe.wait(tq1, t_rd, t_psrd)
                                tm1 = pe.done(pe.raw.matmul(
                                    ps_nq[:, :], ones_b[:, :],
                                    sqsc2[:, 512 * tt:512 * (tt + 1)], start=True, stop=True))
                                dve.wait(tm1, t_qdma)
                                te1 = dve.done(dve.raw.tensor_copy(
                                    qrow[:, 512 * tt:512 * (tt + 1)], ps_nq[:, :]))
                                t_psrd = te1
                            tq2 = dve.done(dve.raw.tensor_tensor(
                                sqsc2[0:64, 0:NT], qph[0:64, NT * H:NT * (H + 1)],
                                qph[0:64, NT * H:NT * (H + 1)], ALU.mult))
                            for tt in range(2):
                                pe.wait(tq2, t_psrd)
                                tm3 = pe.done(pe.raw.matmul(
                                    ps_nq[:, :], ones_b[0:64, :],
                                    sqsc2[0:64, 512 * tt:512 * (tt + 1)], start=True, stop=True))
                                t_rd = tm3
                                dve.wait(tm3)
                                te3 = dve.done(dve.raw.tensor_tensor(
                                    qrow[:, 512 * tt:512 * (tt + 1)],
                                    qrow[:, 512 * tt:512 * (tt + 1)], ps_nq[:, :], ALU.add))
                                t_psrd = te3
                            act.wait(te3)
                            ta1 = act.done(act.raw.activation(out=qrow[:, :], in_=qrow[:, :],
                                                              func=AF.Ln, bias=eps_rms_t[:, :],
                                                              scale=1.0))
                            act.wait(ta1)
                            ta2 = act.done(act.raw.activation(out=qrow[:, :], in_=qrow[:, :],
                                                              func=AF.Exp, bias=0.0, scale=0.5))
                            qs.wait(ta2)
                            t_qdma = qs.done(qs.raw.dma_start(out=sqq_d[H:H + 1, :],
                                                              in_=qrow[:, :]))
                            t_rd = ta2

                    em.barrier()
                    ps_am = ph.enter_context(nc.psum_tensor("ps_am", [128, 512], F32))
                    am_step(100)
                    t_z = am_state["acc"]
                    for wi in range(2):
                        pe.wait(t_z)
                        tmm = pe.done(pe.raw.matmul(ps_am[0:1, 0:1], racc[:, wi:wi + 1],
                                                    onesc_f[:, :], start=True, stop=True))
                        dve.wait(tmm)
                        t1 = dve.done(dve.raw.tensor_single_scalar(
                            amv[:, wi:wi + 1], ps_am[0:1, 0:1], 1.0 / (HID * HF), ALU.mult))
                        dve.wait(t1)
                        t2 = dve.done(dve.raw.tensor_single_scalar(
                            amv[:, wi:wi + 1], amv[:, wi:wi + 1], 1e-5, ALU.max))
                        pe.wait(t2)
                    t1 = dve.done(dve.raw.tensor_tensor(amv[:, 2:3], amv[:, 0:1], amv[:, 0:1],
                                                        ALU.mult))
                    dve.wait(t1)
                    t2 = dve.done(dve.raw.tensor_tensor(amv[:, 2:3], amv[:, 2:3], amv[:, 1:2],
                                                        ALU.mult))
                    dve.wait(t2)
                    t3 = dve.done(dve.raw.tensor_single_scalar(amv[:, 2:3], amv[:, 2:3],
                                                               0.125, ALU.mult))
                    for (dst, col, src_, sc) in [(thr_fc, 0, 0, -0.7), (thr_fc, 1, 0, 0.7),
                                                 (thr_pj, 0, 1, -0.7), (thr_pj, 1, 1, 0.7)]:
                        dve.wait(t3)
                        tsc = dve.done(dve.raw.tensor_single_scalar(
                            scr[:, 2:3], amv[:, src_:src_ + 1], sc, ALU.mult))
                        pe.wait(tsc)
                        tmm = pe.done(pe.raw.matmul(ps_am[:, 0:1], ones_f[:, :], scr[:, 2:3],
                                                    start=True, stop=True))
                        dve.wait(tmm)
                        tev = dve.done(dve.raw.tensor_copy(dst[:, col:col + 1], ps_am[:, 0:1]))
                        pe.wait(tev)
                    pe.wait(t3)
                    tmm = pe.done(pe.raw.matmul(ps_am[:, 0:1], ones_f[:, :], amv[:, 2:3],
                                                start=True, stop=True))
                    dve.wait(tmm)
                    tev = dve.done(dve.raw.tensor_copy(cy_b[:, :], ps_am[:, 0:1]))
                    pe.wait(tev)
                em.barrier()
            if DBG:
                qg.done(qg.raw.dma_start(out=dbg["d_qn"].rearrange("(h p) n -> p h n", p=128),
                                          in_=cm(qn, NH)))
                t_ = qg.done(qg.raw.dma_start(
                    out=dbg["d_qp"].rearrange("(h pp) n -> pp h n", pp=64),
                    in_=qph[0:64, :].rearrange("pp (h n) -> pp h n", h=NH)))
                qg.wait(t_)
                em.barrier()

            with ExitStack() as ph:
                P = ph.enter_context
                kq = P(nc.sbuf_tensor("kq", [64, 512], BF16))
                ps_n = P(nc.psum_tensor("ps_n", [1, 512], F32))
                tsg = None
                for sseg in range(8):
                    tsq = dve.done(dve.raw.tensor_tensor(
                        kq[:, :], kpefull[0:64, 512 * sseg:512 * (sseg + 1)],
                        kpefull[0:64, 512 * sseg:512 * (sseg + 1)], ALU.mult))
                    pe.wait(tsq, tsg)
                    t1 = pe.done(pe.raw.matmul(ps_n[:, :], ones_b[0:64, :], kq[:, :],
                                               start=True, stop=True))
                    dve.wait(t1)
                    tsg = dve.done(dve.raw.tensor_reduce(scr[:, 8 + sseg:9 + sseg],
                                                         ps_n[:, :], axis=AX.X, op=ALU.max))
                dve.wait(tsg)
                t1 = dve.done(dve.raw.tensor_tensor(scr[:, 4:5], scr[:, 8:9], scr[:, 9:10],
                                                    ALU.max))
                dve.wait(t1)
                for tp in range(1, 4):
                    ta = dve.done(dve.raw.tensor_tensor(scr[:, 4 + tp:5 + tp],
                                                        scr[:, 3 + tp:4 + tp],
                                                        scr[:, 8 + 2 * tp:9 + 2 * tp], ALU.max))
                    dve.wait(ta)
                    tb = dve.done(dve.raw.tensor_tensor(scr[:, 4 + tp:5 + tp],
                                                        scr[:, 4 + tp:5 + tp],
                                                        scr[:, 9 + 2 * tp:10 + 2 * tp], ALU.max))
                    dve.wait(tb)
            em.barrier()

            em.barrier()
            em.barrier()
            if DBG:
                t_ = qg.done(qg.raw.dma_start(out=dbg["d_am"], in_=amv[0:1, :]))
                qg.wait(t_)

            NG = 8
            with ExitStack() as ph:
                P = ph.enter_context
                kngs = [P(nc.sbuf_tensor("kngA", [128, 2 * S], BF16)),
                        P(nc.sbuf_tensor("kngB", [128, 2 * S], BF16))]
                vbufs = [P(nc.sbuf_tensor("vbufA", [128, 32 * 256], BF16)),
                         P(nc.sbuf_tensor("vbufB", [128, 32 * 256], BF16))]
                wkbks = [P(nc.sbuf_tensor("wkbk0", [128, 4 * 256], BF16)),
                         P(nc.sbuf_tensor("wkbk1", [128, 4 * 256], BF16))]
                wkbvs = [P(nc.sbuf_tensor("wkbv0", [128, 4 * 256], BF16)),
                         P(nc.sbuf_tensor("wkbv1", [128, 4 * 256], BF16))]
                rmst0 = P(nc.sbuf_tensor("rmst0", [128, 4 * 512], BF16))
                rmst1 = P(nc.sbuf_tensor("rmst1", [128, 4 * 512], BF16))
                pts6 = [P(nc.sbuf_tensor(f"pt{i}", [128, 512], BF16)) for i in range(6)]
                ksq = P(nc.sbuf_tensor("ksq", [128, 1024], BF16))
                nqrow = P(nc.sbuf_tensor("nqrow", [1, NT], F32))
                nqrow2 = P(nc.sbuf_tensor("nqrow2", [1, NT], F32))
                sr_at = P(nc.sbuf_tensor("sr_at", [1, 1024], F32))
                rb2_s = P(nc.sbuf_tensor("rb2_s", [128, 512], F32))
                onrm0 = P(nc.sbuf_tensor("onrm0", [128, 256], BF16))
                onrm1 = P(nc.sbuf_tensor("onrm1", [128, 256], BF16))
                ps_SA = P(nc.psum_tensor("ps_SA", [128, 512], F32))
                ps_SB = P(nc.psum_tensor("ps_SB", [128, 512], F32))
                ps_SC = P(nc.psum_tensor("ps_SC", [128, 512], F32))
                ps_o0 = P(nc.psum_tensor("ps_o0", [128, 256], F32))
                ps_o1 = P(nc.psum_tensor("ps_o1", [128, 256], F32))
                ps_sm0 = P(nc.psum_tensor("ps_sm0", [1, 256], F32))
                ps_sm1 = P(nc.psum_tensor("ps_sm1", [1, 256], F32))
                ps_kv = P(nc.psum_tensor("ps_kv", [128, 512], F32))

                Sbanks = [ps_SA, ps_SB, ps_SC]
                ps_os = [ps_o0, ps_o1]
                ps_sms = [ps_sm0, ps_sm1]
                onrms = [onrm0, onrm1]
                bank = {"tok": None}
                t_rfree = [None, None]
                t_wkfree = [None, None]
                kv_ready = {}
                shift_ready = {}
                t_shift_dma = {}
                t_sm_free = [None, None]
                t_o_free = [None, None]
                t_pt_free = [None] * 6
                t_Sfree = [None, None, None]
                t_ot_dma = [None, None]
                state = {"pti": 0, "sti": 0, "epar": 0, "epi": None,
                         "ksq2_rd": None, "nqrow_rd": None}

                def kvb_closures(gg):
                    t_rdma = [None, None]
                    kb_ = kngs[gg % 2]
                    vb_ = vbufs[gg % 2]
                    wk_ = wkbks[gg % 2]
                    wv_ = wkbvs[gg % 2]
                    cls = []

                    def dma_w():
                        qg.wait(t_wkfree[gg % 2])
                        t1 = qg.done(qg.raw.dma_start(
                            out=cm(wk_, 4),
                            in_=wkbk[:, 256 * gg:256 * (gg + 1)]
                            .rearrange("(c p) n -> p c n", p=128)))
                        t2 = qg.done(qg.raw.dma_start(
                            out=cm(wv_, 4),
                            in_=wkbv[:, 256 * gg:256 * (gg + 1)]
                            .rearrange("(c p) n -> p c n", p=128)))
                        kv_ready.setdefault(gg, []).extend([t1, t2])
                    cls.append(dma_w)

                    def rms_dma(s):
                        bufr_ = [rmst0, rmst1][s % 2]
                        qs.wait(t_rfree[s % 2])
                        t_rdma[s % 2] = qs.done(qs.raw.dma_start(
                            out=cm(bufr_, 4),
                            in_=rms_d[:, 512 * s:512 * (s + 1)]
                            .rearrange("(c p) n -> p c n", p=128)))

                    def mk_stile(s):
                        def fn():
                            bufr = [rmst0, rmst1][s % 2]
                            if s == 0:
                                rms_dma(0)
                            t_r = t_rdma[s % 2]
                            if s + 1 < 8:
                                rms_dma(s + 1)
                            tw = kv_ready[gg][:2]
                            for h2 in range(2):
                                pe.wait(t_r, bank["tok"], *tw)
                                for c in range(4):
                                    mmi = pe.raw.matmul(
                                        ps_kv[:, :],
                                        cm(wk_, 4)[:, c, 128 * h2:128 * (h2 + 1)],
                                        bufr[:, 512 * c:512 * (c + 1)],
                                        start=(c == 0), stop=(c == 3))
                                tkn = pe.done(mmi)
                                dve.wait(tkn)
                                tev = dve.done(dve.raw.tensor_copy(
                                    kb_[:, S * h2 + 512 * s:S * h2 + 512 * (s + 1)],
                                    ps_kv[:, :]))
                                bank["tok"] = tev
                            for half in range(2):
                                pe.wait(bank["tok"])
                                for i in range(2):
                                    sub = 2 * half + i
                                    for c in range(4):
                                        mmi = pe.raw.matmul(
                                            ps_kv[:, 256 * i:256 * (i + 1)],
                                            bufr[:, 512 * c + 128 * sub:512 * c + 128 * (sub + 1)],
                                            cm(wv_, 4)[:, c, :],
                                            start=(c == 0 and i == 0), stop=(c == 3 and i == 1))
                                tkn = pe.done(mmi)
                                dve.wait(tkn)
                                tev = dve.done(dve.raw.tensor_copy(
                                    vb_[:, 256 * (4 * s + 2 * half):256 * (4 * s + 2 * half + 2)],
                                    ps_kv[:, :]))
                                bank["tok"] = tev
                            t_rfree[s % 2] = bank["tok"]
                            kv_ready.setdefault(gg, []).append(bank["tok"])
                        return fn
                    for s in range(8):
                        cls.append(mk_stile(s))
                    return cls

                def norm_closures(gg):
                    kb_ = kngs[gg % 2]
                    cls = []

                    def mk_seg(sseg):
                        def fn():
                            tzk = None
                            for h2 in range(2):
                                ksl = h2
                                dve.wait(state.get(("ksqrd", ksl)))
                                tsq = dve.done(dve.raw.tensor_tensor(
                                    ksq[:, 512 * ksl:512 * (ksl + 1)],
                                    kb_[:, S * h2 + 512 * sseg:S * h2 + 512 * (sseg + 1)],
                                    kb_[:, S * h2 + 512 * sseg:S * h2 + 512 * (sseg + 1)],
                                    ALU.mult))
                                pe.wait(tsq, bank["tok"])
                                t1 = pe.done(pe.raw.matmul(ps_kv[0:1, 0:512], ones_b[:, :],
                                                           ksq[:, 512 * ksl:512 * (ksl + 1)],
                                                           start=True, stop=True))
                                state[("ksqrd", ksl)] = t1
                                dve.wait(t1)
                                tm = dve.done(dve.raw.tensor_reduce(
                                    scr[:, 1:2], ps_kv[0:1, 0:512], axis=AX.X, op=ALU.max))
                                bank["tok"] = tm
                                dve.wait(tm)
                                tzk = dve.done(dve.raw.tensor_tensor(
                                    scr[:, 8 + sseg:9 + sseg], scr[:, 8 + sseg:9 + sseg],
                                    scr[:, 1:2], ALU.max))
                            shift_ready[(gg, "seg", sseg)] = tzk
                        return fn

                    def seg_init():
                        t0 = dve.done(dve.raw.memset(scr[:, 8:16], 0.0))
                        shift_ready[(gg, "seginit")] = t0
                    cls.append(seg_init)
                    for sseg in range(8):
                        cls.append(mk_seg(sseg))

                    def prefixes():
                        tpv = dve.done(dve.raw.tensor_tensor(scr[:, 0:1], scr[:, 8:9],
                                                             scr[:, 9:10], ALU.max))
                        for tp in range(1, 4):
                            dve.wait(tpv)
                            ta = dve.done(dve.raw.tensor_tensor(
                                scr[:, tp:tp + 1], scr[:, tp - 1:tp],
                                scr[:, 8 + 2 * tp:9 + 2 * tp], ALU.max))
                            dve.wait(ta)
                            tpv = dve.done(dve.raw.tensor_tensor(
                                scr[:, tp:tp + 1], scr[:, tp:tp + 1],
                                scr[:, 9 + 2 * tp:10 + 2 * tp], ALU.max))
                        dve.wait(tpv)
                        tkk = dve.done(dve.raw.tensor_tensor(scr[:, 0:4], scr[:, 0:4],
                                                             scr[:, 4:8], ALU.add))
                        act.wait(tkk)
                        t1 = act.done(act.raw.activation(out=scr[:, 8:12], in_=scr[:, 0:4],
                                                         func=AF.Ln, bias=eps_rms_t[:, :],
                                                         scale=1.0))
                        act.wait(t1)
                        t2 = act.done(act.raw.activation(out=scr[:, 8:12], in_=scr[:, 8:12],
                                                         func=AF.Exp, bias=0.0, scale=0.5))
                        dve.wait(t2)
                        tnk = dve.done(dve.raw.tensor_single_scalar(scr[:, 8:12], scr[:, 8:12],
                                                                    -1.0, ALU.mult))
                        shift_ready[(gg, "nk")] = tnk
                    cls.append(prefixes)

                    def mk_shift(h2):
                        def fn():
                            H = 2 * gg + h2
                            qs.wait(state["nqrow_rd"])
                            t_qld = qs.done(qs.raw.dma_start(out=nqrow[:, :],
                                                             in_=sqq_d[H:H + 1, :]))
                            dve.wait(t_qld, shift_ready[(gg, "nk")], t_shift_dma.get(H - 1),
                                     state["nqrow_rd"])
                            tsl = None
                            for tp in range(4):
                                tsl = dve.done(dve.raw.tensor_scalar(
                                    out=nqrow2[:, 256 * tp:256 * (tp + 1)],
                                    in0=nqrow[:, 256 * tp:256 * (tp + 1)],
                                    scalar1=scr[:, 8 + tp:9 + tp], scalar2=float(50.0 / SCALE),
                                    op0=ALU.mult, op1=ALU.add))
                            state["nqrow_rd"] = tsl
                            qg.wait(tsl)
                            t_shift_dma[H] = qg.done(qg.raw.dma_start(
                                out=qph[64:65, NT * H:NT * (H + 1)], in_=nqrow2[:, :]))
                        return fn
                    cls.append(mk_shift(0))
                    cls.append(mk_shift(1))
                    return cls

                def emit_spv(pend, par, nkb, Hloc, gg, first):
                    slot, st0, t_e, t_m = pend
                    vb_ = vbufs[gg % 2]
                    pe.wait(t_e, t_m)
                    if first:
                        pe.wait(t_sm_free[par], t_o_free[par])
                    for i in range(2):
                        kb = 2 * st0 + i
                        pe.raw.matmul(ps_sms[par][:, :], ones_s[:, :],
                                      pts6[slot][:, 256 * i:256 * (i + 1)],
                                      start=(kb == 0), stop=(kb == nkb - 1))
                        mmi = pe.raw.matmul(
                            ps_os[par][:, :],
                            vb_[:, 256 * kb + 128 * Hloc:256 * kb + 128 * (Hloc + 1)],
                            pts6[slot][:, 256 * i:256 * (i + 1)],
                            start=(kb == 0), stop=(kb == nkb - 1))
                    t_pv = pe.done(mmi)
                    t_pt_free[slot] = t_pv
                    return t_pv

                def flush_epilogue(e):
                    par = e["par"]
                    pe.wait(e["t_r"], bank["tok"])
                    t_bb = pe.done(pe.raw.matmul(ps_kv[:, 0:256], ones_f[:, :],
                                                 sr_at[:, 512 + 256 * par:768 + 256 * par],
                                                 start=True, stop=True))
                    dve.wait(t_bb)
                    t_rbe = dve.done(dve.raw.tensor_copy(
                        rb2_s[:, 256 * par:256 * (par + 1)], ps_kv[:, 0:256]))
                    bank["tok"] = t_rbe
                    dve.wait(t_rbe, t_ot_dma[par])
                    t_on = dve.done(dve.raw.tensor_tensor(
                        onrms[par][:, :], ps_os[par][:, :],
                        rb2_s[:, 256 * par:256 * (par + 1)], ALU.mult))
                    t_o_free[par] = t_on
                    qg.wait(t_on)
                    t_ot_dma[par] = qg.done(qg.raw.dma_start(
                        out=ot_d[128 * e["H"]:128 * (e["H"] + 1),
                                 256 * e["tp"]:256 * (e["tp"] + 1)],
                        in_=onrms[par][:, :]))

                def emit_attention(gg, inject):
                    kb_ = kngs[gg % 2]
                    inj = list(inject)
                    ii = 0

                    def do_inject():
                        nonlocal ii
                        if ii < len(inj):
                            inj[ii]()
                            ii += 1
                    for tp in range(4):
                        nkb = 8 * (tp + 1)
                        for h2 in range(2):
                            H = 2 * gg + h2
                            qrhs_n = qn[:, NT * H + 256 * tp:NT * H + 256 * (tp + 1)]
                            qrhs_p = qph[0:65, NT * H + 256 * tp:NT * H + 256 * (tp + 1)]
                            pe.wait(t_shift_dma[H], *kv_ready[gg])
                            ntile = nkb // 2
                            pq = []
                            par = state["epar"] % 2
                            first_spv = True
                            for st in range(ntile):
                                sb = state["sti"] % 3
                                state["sti"] += 1
                                Sps = Sbanks[sb]
                                pe.wait(t_Sfree[sb])
                                for i in range(2):
                                    kb = 2 * st + i
                                    colo = 256 * i
                                    pe.raw.matmul(
                                        Sps[:, colo:colo + 256],
                                        kb_[:, S * h2 + 128 * kb:S * h2 + 128 * (kb + 1)],
                                        qrhs_n, start=(i == 0), stop=False)
                                    mmi = pe.raw.matmul(
                                        Sps[:, colo:colo + 256],
                                        kpefull[0:65, 128 * kb:128 * (kb + 1)],
                                        qrhs_p, start=False, stop=(i == 1))
                                t_S = pe.done(mmi)
                                if len(pq) >= 2:
                                    emit_spv(pq.pop(0), par, nkb, h2, gg, first_spv)
                                    first_spv = False
                                if st == 2 and state["epi"] is not None:
                                    flush_epilogue(state["epi"])
                                    state["epi"] = None
                                slot = state["pti"] % 6
                                state["pti"] += 1
                                act.wait(t_S, t_pt_free[slot])
                                t_e = act.done(act.raw.activation(out=pts6[slot][:, :],
                                                                  in_=Sps[:, :], func=AF.Exp,
                                                                  bias=0.0, scale=SCALE))
                                t_Sfree[sb] = t_e
                                t_m = None
                                for i in range(2):
                                    kb = 2 * st + i
                                    j = kb - (nkb - 8)
                                    if j >= 0:
                                        dve.wait(t_e)
                                        t_m = dve.done(dve.raw.tensor_tensor(
                                            pts6[slot][:, 256 * i:256 * (i + 1)],
                                            pts6[slot][:, 256 * i:256 * (i + 1)],
                                            masks[:, 256 * j:256 * (j + 1)], ALU.mult))
                                pq.append((slot, st, t_e, t_m))
                            for pend in pq:
                                t_pv = emit_spv(pend, par, nkb, h2, gg, first_spv)
                                first_spv = False
                            pq = []
                            dve.wait(t_pv)
                            t_sev = dve.done(dve.raw.tensor_copy(
                                sr_at[:, 256 * par:256 * (par + 1)], ps_sms[par][:, :]))
                            t_sm_free[par] = t_sev
                            act.wait(t_sev)
                            t_l = act.done(act.raw.activation(
                                out=sr_at[:, 512 + 256 * par:768 + 256 * par],
                                in_=sr_at[:, 256 * par:256 * (par + 1)], func=AF.Ln,
                                bias=0.0, scale=1.0))
                            act.wait(t_l)
                            t_r = act.done(act.raw.activation(
                                out=sr_at[:, 512 + 256 * par:768 + 256 * par],
                                in_=sr_at[:, 512 + 256 * par:768 + 256 * par], func=AF.Exp,
                                bias=ln2n30[:, :], scale=-1.0))
                            if state["epi"] is not None:
                                flush_epilogue(state["epi"])
                            state["epi"] = dict(par=par, t_r=t_r, H=H, tp=tp)
                            state["epar"] += 1
                            do_inject()
                            do_inject()
                            do_inject()
                    while ii < len(inj):
                        do_inject()

                for fn in kvb_closures(0) + norm_closures(0):
                    fn()
                for gg in range(NG):
                    nxt = kvb_closures(gg + 1) + norm_closures(gg + 1) if gg + 1 < NG else []
                    emit_attention(gg, nxt)
                if state["epi"] is not None:
                    flush_epilogue(state["epi"])
                    state["epi"] = None
                em.barrier()
            em.barrier()
        if DBG:
            qg.done(qg.raw.dma_start(out=dbg["d_shift"], in_=qph[64:65, :]))
            t_ = qg.done(qg.raw.dma_start(out=dbg["d_sqq"], in_=sqq_d))
            qg.wait(t_)
            em.barrier()
        if DBG:
            t_ = qg.done(qg.raw.dma_start(out=dbg["d_ot"], in_=ot_d))
            qg.wait(t_)
            em.barrier()

        NTH = NT // 2
        eg_scope = ctx.enter_context(ExitStack())
        ln2 = eg_scope.enter_context(nc.sbuf_tensor("ln2", [128, NKC * NT], BF16))
        with ExitStack() as ph:
            P = ph.enter_context
            ots = P(nc.sbuf_tensor("ots", [128, NKC * NT], BF16))
            attn = P(nc.sbuf_tensor("attn", [128, NKC * NTH], F32))
            xow = P(nc.sbuf_tensor("xow", [128, NKC * NTH], F32))
            wos0 = P(nc.sbuf_tensor("wos0", [128, NKC * 128], BF16))
            wos1 = P(nc.sbuf_tensor("wos1", [128, NKC * 128], BF16))
            bfc = P(nc.sbuf_tensor("bfc", [128, 1024], BF16))
            sqc = P(nc.sbuf_tensor("sqc", [128, 1024], BF16))
            mb2 = P(nc.sbuf_tensor("mb2", [128, 512], F32))
            rb3 = P(nc.sbuf_tensor("rb3", [128, 512], F32))
            ps_e0 = P(nc.psum_tensor("ps_e0", [128, 512], F32))
            ps_e1 = P(nc.psum_tensor("ps_e1", [128, 512], F32))
            ps_f1 = P(nc.psum_tensor("ps_f1", [1, 512], F32))
            ps_f2 = P(nc.psum_tensor("ps_f2", [1, 512], F32))
            t_ot = qs.done(qs.raw.dma_start(out=cm(ots, NKC),
                                            in_=ot_d.rearrange("(c p) n -> p c n", p=128)))
            t_x1dma = None
            for hf in range(2):
                t_xo = qs.done(qs.raw.dma_start(
                    out=cm(xow, NKC),
                    in_=x_own[:, NTH * hf:NTH * (hf + 1)].rearrange("(c p) n -> p c n", p=128)))
                t_wdma = [None, None]
                t_wfree = [None, None]
                t_eslot = [None, None]

                def wo_dma(m):
                    bw = [wos0, wos1][m % 2]
                    qg.wait(t_wfree[m % 2])
                    t_wdma[m % 2] = qg.done(qg.raw.dma_start(
                        out=cm(bw, NKC),
                        in_=woT[:, 128 * m:128 * (m + 1)].rearrange("(c p) n -> p c n",
                                                                    p=128)))
                wo_dma(0)
                for m in range(NKC):
                    bw = [wos0, wos1][m % 2]
                    t_dcur = t_wdma[m % 2]
                    if m + 1 < NKC:
                        wo_dma(m + 1)
                    pst = [ps_e0, ps_e1][m % 2]
                    pe.wait(t_dcur, t_ot, t_eslot[m % 2])
                    for c in range(NKC):
                        mmi = pe.raw.matmul(pst[:, :], bw[:, 128 * c:128 * (c + 1)],
                                            ots[:, NT * c + NTH * hf:NT * c + NTH * (hf + 1)],
                                            start=(c == 0), stop=(c == NKC - 1))
                    tq = pe.done(mmi)
                    t_wfree[m % 2] = tq
                    dve.wait(tq)
                    tev = dve.done(dve.raw.tensor_copy(attn[:, NTH * m:NTH * (m + 1)],
                                                       pst[:, :]))
                    t_eslot[m % 2] = tev
                em.barrier()
                av = cm(attn, NKC)
                t_rs = ln_stats(av, NTH, NKC, HID, eps_ln_t, bfc, sqc, ps_f1, ps_f2)
                dve.wait(t_xo)
                tn, _ = ln_norm(av, NTH, NKC, mb2, rb3, ps_e0, ps_e1, t_rs,
                                out_view=cm(xow, NKC), add_view=cm(xow, NKC))
                qs.wait(tn)
                t_x1dma = qs.done(qs.raw.dma_start(
                    out=x1_d[:, NTH * hf:NTH * (hf + 1)].rearrange("(c p) n -> p c n", p=128),
                    in_=cm(xow, NKC)))
                t_rs2 = ln_stats(cm(xow, NKC), NTH, NKC, HID, eps_ln_t, bfc, sqc, ps_f1, ps_f2)
                dve.wait(t_x1dma)
                tn2, _ = ln_norm(cm(xow, NKC), NTH, NKC, mb2, rb3, ps_e0, ps_e1, t_rs2,
                                 out_view=cm(ln2, NKC)[:, :, NTH * hf:NTH * (hf + 1)])
                em.barrier()
        if DBG:
            qg.done(qg.raw.dma_start(out=dbg["d_x1"].rearrange("(c p) n -> p c n", p=128),
                                      in_=x1_d.rearrange("(c p) n -> p c n", p=128)))
            t_ = qg.done(qg.raw.dma_start(
                out=dbg["d_ln2"].rearrange("(c p) n -> p c n", p=128), in_=cm(ln2, NKC)))
            qg.wait(t_)
            em.barrier()

        NTH = NT // 2
        t_out_final = None
        with ExitStack() as ph:
            P = ph.enter_context
            h2 = P(nc.sbuf_tensor("h2", [128, 40 * NTH], BF16))
            yb = P(nc.sbuf_tensor("yb", [128, NKC * NTH], F32))
            ws0 = P(nc.sbuf_tensor("ws0", [128, NKC * 128], F32))
            ws1 = P(nc.sbuf_tensor("ws1", [128, NKC * 128], F32))
            ta1 = P(nc.sbuf_tensor("ta1", [128, NKC * 128], BF16))
            tern = P(nc.sbuf_tensor("tern", [128, NKC * 128], BF16))
            wp0 = P(nc.sbuf_tensor("wp0", [128, 10 * 128], F32))
            wp1 = P(nc.sbuf_tensor("wp1", [128, 10 * 128], F32))
            tpa1 = P(nc.sbuf_tensor("tpa1", [128, 10 * 128], BF16))
            ternp = P(nc.sbuf_tensor("ternp", [128, 10 * 128], BF16))
            hfp = P(nc.sbuf_tensor("hfp", [128, NTH], F32))
            x1h = P(nc.sbuf_tensor("x1h", [128, NKC * NTH], F32))
            bfc2 = P(nc.sbuf_tensor("bfc2", [128, 1024], BF16))
            sqc2 = P(nc.sbuf_tensor("sqc2", [128, 1024], BF16))
            mb4 = P(nc.sbuf_tensor("mb4", [128, 512], F32))
            rb4 = P(nc.sbuf_tensor("rb4", [128, 512], F32))
            ps_g0 = P(nc.psum_tensor("ps_g0", [128, 512], F32))
            ps_g1 = P(nc.psum_tensor("ps_g1", [128, 512], F32))
            ps_h1 = P(nc.psum_tensor("ps_h1", [1, 512], F32))
            ps_h2 = P(nc.psum_tensor("ps_h2", [1, 512], F32))
            for hf in range(2):
                t_wfree = [None, None]
                t_h2w = None
                t_fcmm = None
                t_fctt = None
                t_gslot = [None, None]
                t_wdma_fc = [None, None]

                def fc_dma(m):
                    bw = [ws0, ws1][m % 2]
                    qs.wait(t_wfree[m % 2])
                    t_wdma_fc[m % 2] = qs.done(qs.raw.dma_start(
                        out=cm(bw, NKC),
                        in_=wfcT[:, 128 * m:128 * (m + 1)].rearrange("(c p) n -> p c n",
                                                                     p=128)))
                fc_dma(0)
                for m in range(40):
                    bw = [ws0, ws1][m % 2]
                    t_d = t_wdma_fc[m % 2]
                    if m + 1 < 40:
                        fc_dma(m + 1)
                    act.wait(t_d, t_fcmm, t_fctt)
                    tsa = act.done(act.raw.activation(out=ta1[:, :], in_=bw[:, :], func=AF.Sign,
                                                      bias=thr_fc[:, 0:1], scale=1.0))
                    tsb = act.done(act.raw.activation(out=tern[:, :], in_=bw[:, :], func=AF.Sign,
                                                      bias=thr_fc[:, 1:2], scale=1.0))
                    t_wfree[m % 2] = tsb
                    dve.wait(tsa, tsb)
                    tt_ = dve.done(dve.raw.tensor_tensor(tern[:, :], tern[:, :], ta1[:, :],
                                                         ALU.add))
                    t_fctt = tt_
                    pst = [ps_g0, ps_g1][m % 2]
                    pe.wait(tt_, t_gslot[m % 2])
                    for c in range(NKC):
                        mmi = pe.raw.matmul(pst[:, 0:NTH], tern[:, 128 * c:128 * (c + 1)],
                                            ln2[:, NT * c + NTH * hf:NT * c + NTH * (hf + 1)],
                                            start=(c == 0), stop=(c == NKC - 1))
                    th = pe.done(mmi)
                    t_fcmm = th
                    dve.wait(th)
                    tcp = dve.done(dve.raw.tensor_copy(hfp[:, :], pst[:, 0:NTH]))
                    t_gslot[m % 2] = tcp
                    dve.wait(tcp)
                    t_h2w = dve.done(dve.raw.scalar_tensor_tensor(
                        h2[:, NTH * m:NTH * (m + 1)], hfp[:, :], 0.0, hfp[:, :],
                        ALU.max, ALU.mult))
                t_wfree = [None, None]
                t_yev = None
                t_yslot = [None, None]
                t_pjmm = None
                t_pjtt = None
                t_wdma_pj = [None, None]
                pj_seq = [(o, kh) for o in range(NKC) for kh in range(4)]

                def pj_dma(i):
                    o_, kh_ = pj_seq[i]
                    bw = [wp0, wp1][i % 2]
                    qs.wait(t_wfree[i % 2])
                    t_wdma_pj[i % 2] = qs.done(qs.raw.dma_start(
                        out=cm(bw, 10),
                        in_=wpjT[1280 * kh_:1280 * (kh_ + 1), 128 * o_:128 * (o_ + 1)]
                        .rearrange("(c p) n -> p c n", p=128)))
                pj_dma(0)
                pji = 0
                for o in range(NKC):
                    pst = [ps_g0, ps_g1][o % 2]
                    for kh in range(4):
                        bw = [wp0, wp1][pji % 2]
                        t_d = t_wdma_pj[pji % 2]
                        if pji + 1 < len(pj_seq):
                            pj_dma(pji + 1)
                        pji += 1
                        act.wait(t_d, t_pjmm, t_pjtt)
                        tsa = act.done(act.raw.activation(out=tpa1[:, :], in_=bw[:, :],
                                                          func=AF.Sign, bias=thr_pj[:, 0:1],
                                                          scale=1.0))
                        tsb = act.done(act.raw.activation(out=ternp[:, :], in_=bw[:, :],
                                                          func=AF.Sign, bias=thr_pj[:, 1:2],
                                                          scale=1.0))
                        t_wfree[(pji - 1) % 2] = tsb
                        dve.wait(tsa, tsb)
                        tt_ = dve.done(dve.raw.tensor_tensor(ternp[:, :], ternp[:, :],
                                                             tpa1[:, :], ALU.add))
                        t_pjtt = tt_
                        pe.wait(tt_, t_h2w)
                        if kh == 0:
                            pe.wait(t_yslot[o % 2])
                        for c in range(10):
                            cc = 10 * kh + c
                            mmi = pe.raw.matmul(pst[:, 0:NTH], ternp[:, 128 * c:128 * (c + 1)],
                                                h2[:, NTH * cc:NTH * (cc + 1)],
                                                start=(cc == 0), stop=(cc == 39))
                        ty = pe.done(mmi)
                        t_pjmm = ty
                    dve.wait(ty)
                    t_yev = dve.done(dve.raw.tensor_single_scalar(
                        cm(yb, NKC)[:, o, :], pst[:, 0:NTH], cy_b[:, 0:1], ALU.mult))
                    t_yslot[o % 2] = t_yev
                em.barrier()
                if DBG and hf == 0:
                    qg.done(qg.raw.dma_start(
                        out=dbg["d_h2"].rearrange("(c p) n -> p c n", p=128),
                        in_=cm(h2, 40)))
                    t_ = qg.done(qg.raw.dma_start(
                        out=dbg["d_y"].rearrange("(c p) n -> p c n", p=128),
                        in_=cm(yb, NKC)))
                    qg.wait(t_)
                    em.barrier()
                t_x1 = qs.done(qs.raw.dma_start(
                    out=cm(x1h, NKC),
                    in_=x1_d[:, NTH * hf:NTH * (hf + 1)].rearrange("(c p) n -> p c n", p=128)))
                yv = cm(yb, NKC)
                t_rs = ln_stats(yv, NTH, NKC, HID, eps_ln_t, bfc2, sqc2, ps_h1, ps_h2)
                dve.wait(t_x1)
                tn, _ = ln_norm(yv, NTH, NKC, mb4, rb4, ps_g0, ps_g1, t_rs,
                                out_view=cm(x1h, NKC), add_view=cm(x1h, NKC))
                qs.wait(tn)
                t_out_final = qs.done(qs.raw.dma_start(
                    out=outT[:, NTH * hf:NTH * (hf + 1)].rearrange("(c p) n -> p c n", p=128),
                    in_=cm(x1h, NKC)))
                em.barrier()
        qs.wait(t_out_final)
        em.barrier()
    return nc, dbg


def _prep(inputs):
    x = np.ascontiguousarray(inputs["L0_attn_hidden_states"])
    Wq = inputs["L0_attn_Wq"]
    Wkva = inputs["L0_attn_Wkva"]
    kvaw = inputs["L0_attn_kv_a_norm_weight"]
    Wkvb = inputs["L0_attn_Wkvb"]
    Wo = inputs["L0_attn_Wo"]
    cos = inputs["L0_attn_cos"]
    sin = inputs["L0_attn_sin"]
    wfc = inputs["L0_ffn_w_fc"]
    wpj = inputs["L0_ffn_w_proj"]

    f32 = np.float32
    WqT = np.ascontiguousarray(Wq.T.astype(f32, copy=False))
    wqn = np.ascontiguousarray(WqT.reshape(HID, NH, DQ)[:, :, :DN].reshape(HID, NH * DN))
    wqp = np.ascontiguousarray(WqT.reshape(HID, NH, DQ)[:, :, DN:].reshape(HID, NH * DR))
    WkvaT = np.ascontiguousarray(Wkva.T.astype(f32, copy=False))
    wkc = np.ascontiguousarray(WkvaT[:, :LORA])
    wkp = np.ascontiguousarray(WkvaT[:, LORA:])
    WkvbT = np.ascontiguousarray(Wkvb.T.astype(f32, copy=False))
    wkbk = np.ascontiguousarray(WkvbT.reshape(LORA, NH, DN + DV)[:, :, :DN]
                                .reshape(LORA, NH * DN))
    wkbv = np.ascontiguousarray(WkvbT.reshape(LORA, NH, DN + DV)[:, :, DN:]
                                .reshape(LORA, NH * DV))
    woT = np.ascontiguousarray(Wo.T.astype(f32, copy=False))
    wfcT = np.ascontiguousarray(wfc.T.astype(f32, copy=False))
    wpjT = np.ascontiguousarray(wpj.T.astype(f32, copy=False))
    kvw = np.ascontiguousarray(np.asarray(kvaw).reshape(4, 128).T.astype(f32, copy=False))
    cosr = np.ascontiguousarray(np.asarray(cos).astype(f32, copy=False)).reshape(1, DR)
    sinr = np.ascontiguousarray(np.asarray(sin).astype(f32, copy=False)).reshape(1, DR)

    in_maps = []
    for c in range(8):
        beta, r = c // 4, c % 4
        xb = np.asarray(x[beta]).T.astype(f32, copy=False)
        idx = (np.arange(S) // 4) * 4 + (np.arange(S) + r) % 4
        xrot = np.ascontiguousarray(xb[:, idx])
        xown = np.ascontiguousarray(xb[:, r::4])
        ki = np.arange(128)
        A = 4 * (ki // 4) + (ki + r) % 4 - r
        cmeta_v = np.zeros((128, 8), f32)
        cmeta_v[:, 0] = A.astype(f32)
        in_maps.append({
            "xT": xrot, "x_own": xown, "cmeta": cmeta_v,
            "wqn": wqn, "wqp": wqp, "wkc": wkc, "wkp": wkp,
            "wkbk": wkbk, "wkbv": wkbv, "woT": woT,
            "wfcT": wfcT, "wpjT": wpjT, "kvw": kvw,
            "cosr": cosr, "sinr": sinr,
        })
    return in_maps


def kernel(**inputs):
    in_maps = _prep(inputs)
    nc, _ = build()
    res = run_bass_kernel_spmd(nc, in_maps, core_ids=list(range(8)))
    out = np.empty((B, S, HID), np.float32)
    for c in range(8):
        beta, r = c // 4, c % 4
        out[beta, r::4, :] = res.results[c]["outT"].T
    return out


if __name__ == "__main__":
    nc, _ = build()
    print("built:", nc.n_instructions(), "instructions")


# version 11
# speedup vs baseline: 1.2096x; 1.0449x over previous
import numpy as np
from contextlib import ExitStack

import concourse.bass as bass
import concourse.mybir as mybir
from concourse.bass_utils import run_bass_kernel_spmd

F32 = mybir.dt.float32
BF16 = mybir.dt.bfloat16
AF = mybir.ActivationFunctionType
ALU = mybir.AluOpType
AX = mybir.AxisListType

B, S, HID = 2, 4096, 2048
NH, DN, DR, DQ, DV, LORA = 16, 128, 64, 192, 128, 512
SCALE = float(DQ) ** -0.5
EPS_LN = 1e-5
EPS_RMS = 1e-6
HF = 5120
NT = S // 4
NKC = HID // 128
DBG = False


class EW:
    def __init__(self, raw, sem, name, dma=False):
        self.raw = raw
        self.sem = sem
        self.n = 0
        self.name = name
        self.step = 16 if dma else 1
        self.waited = {}

    def wait(self, *toks):
        for t in toks:
            if t is None:
                continue
            sem, n = t
            k = sem.name
            if self.waited.get(k, 0) >= n:
                continue
            self.raw.wait_ge(sem, n)
            self.waited[k] = n

    def done(self, inst):
        self.n += self.step
        inst.then_inc(self.sem, self.step)
        return (self.sem, self.n)

    def tok(self):
        return (self.sem, self.n) if self.n else None


class Em:
    def __init__(self, nc, ctx):
        self.nc = nc
        E = ctx.enter_context
        self.pe = EW(nc.tensor, E(nc.semaphore("s_pe")), "pe")
        self.act = EW(nc.scalar, E(nc.semaphore("s_act")), "act")
        self.dve = EW(nc.vector, E(nc.semaphore("s_dve")), "dve")
        self.gpc = EW(nc.gpsimd, E(nc.semaphore("s_gpc")), "gpc")
        self.qs = EW(nc.sync, E(nc.semaphore("s_qs")), "qs", dma=True)
        self.qg = EW(nc.gpsimd, E(nc.semaphore("s_qg")), "qg", dma=True)
        self.qa = EW(nc.scalar, E(nc.semaphore("s_qa")), "qa", dma=True)
        self.all = [self.pe, self.act, self.dve, self.gpc, self.qs, self.qg, self.qa]

    def barrier(self):
        toks = [e.tok() for e in self.all]
        for e in self.all:
            e.wait(*toks)


def cm(t, c):
    return t[:, :].rearrange("p (c n) -> p c n", c=c)


def build():
    nc = bass.Bass("TRN2", target_bir_lowering=False)

    def din(name, shape, dt=F32):
        return nc.dram_tensor(name, shape, dt, kind="ExternalInput").ap()

    xT = din("xT", [HID, S])
    x_own = din("x_own", [HID, NT])
    cmeta = din("cmeta", [128, 8])
    wqn = din("wqn", [HID, NH * DN])
    wqp = din("wqp", [HID, NH * DR])
    wkc = din("wkc", [HID, LORA])
    wkp = din("wkp", [HID, DR])
    wkbk = din("wkbk", [LORA, NH * DN])
    wkbv = din("wkbv", [LORA, NH * DV])
    woT = din("woT", [NH * DV, HID])
    wfcT = din("wfcT", [HID, HF])
    wpjT = din("wpjT", [HF, HID])
    kvw = din("kvw", [128, 4])
    cosr = din("cosr", [1, DR])
    sinr = din("sinr", [1, DR])

    outT = nc.dram_tensor("outT", [HID, NT], F32, kind="ExternalOutput").ap()

    rms_d = nc.dram_tensor("rms_d", [LORA, S], BF16, kind="Internal").ap()
    ot_d = nc.dram_tensor("ot_d", [NH * DV, NT], BF16, kind="Internal").ap()
    x1_d = nc.dram_tensor("x1_d", [HID, NT], F32, kind="Internal").ap()
    sqq_d = nc.dram_tensor("sqq_d", [NH, NT], F32, kind="Internal").ap()

    dbg = {}
    if DBG:
        def dout(name, shape, dt=F32):
            dbg[name] = nc.dram_tensor(name, shape, dt, kind="ExternalOutput").ap()
            return dbg[name]
        dout("d_ln1own", [HID, NT], BF16)
        dout("d_rms", [LORA, S], BF16)
        dout("d_kpe", [DR, S], BF16)
        dout("d_qn", [NH * DN, NT], BF16)
        dout("d_qp", [8 * 128, NT], BF16)
        dout("d_kn0", [2 * DN, S], BF16)
        dout("d_shift", [1, NH * NT], BF16)
        dout("d_sqq", [NH, NT], F32)
        dout("d_ot", [NH * DV, NT], BF16)
        dout("d_x1", [HID, NT], F32)
        dout("d_am", [1, 4], F32)
        dout("d_ln2", [HID, NT], BF16)
        dout("d_h2", [HF, NT // 2], BF16)
        dout("d_y", [HID, NT // 2], F32)

    ctx = ExitStack()
    with ctx:
        em = Em(nc, ctx)
        pe, act, dve, gpc, qs, qg, qa = em.pe, em.act, em.dve, em.gpc, em.qs, em.qg, em.qa
        E = ctx.enter_context

        ones_b = E(nc.sbuf_tensor("ones_b", [128, 1], BF16))
        ones_s = E(nc.sbuf_tensor("ones_s", [128, 1], BF16))
        ln2n30 = E(nc.sbuf_tensor("ln2n30", [1, 1], F32))
        ones_f = E(nc.sbuf_tensor("ones_f", [1, 128], F32))
        onesc_f = E(nc.sbuf_tensor("onesc_f", [128, 1], F32))
        eps_ln_t = E(nc.sbuf_tensor("eps_ln", [1, 1], F32))
        eps_rms_t = E(nc.sbuf_tensor("eps_rms", [1, 1], F32))
        cmeta_t = E(nc.sbuf_tensor("cmeta_t", [128, 8], F32))
        kvw_t = E(nc.sbuf_tensor("kvw_t", [128, 4], F32))
        cosb = E(nc.sbuf_tensor("cosb", [128, DR], F32))
        spb = E(nc.sbuf_tensor("spb", [128, DR], F32))
        nmax = E(nc.sbuf_tensor("nmax", [1, 4], F32))
        scr = E(nc.sbuf_tensor("scr", [1, 16], F32))
        amv = E(nc.sbuf_tensor("amv", [1, 4], F32))
        thr_fc = E(nc.sbuf_tensor("thr_fc", [128, 2], F32))
        thr_pj = E(nc.sbuf_tensor("thr_pj", [128, 2], F32))
        cy_b = E(nc.sbuf_tensor("cy_b", [128, 1], F32))
        srow = E(nc.sbuf_tensor("srow", [1, 4 * 512], F32))
        SM, SQ, SVAR, SRS = 0, 1, 2, 3

        def row(i, w=512):
            return srow[:, 512 * i:512 * i + w]

        def bc_scalar(dst_col128, src_row1, ps):
            t1 = pe.done(pe.raw.matmul(ps[:, 0:1], ones_f[:, :], src_row1, start=True, stop=True))
            dve.wait(t1)
            return dve.done(dve.raw.tensor_copy(dst_col128, ps[:, 0:1]))

        def ln_stats(src_view, T, chunks, n_dims, eps_t, bfc, sqc, ps1, ps2):
            t_mm = None
            t_slot = [None, None]
            for c in range(chunks):
                sl = c % 2
                dve.wait(t_slot[sl])
                tc = dve.done(dve.raw.tensor_copy(bfc[:, T * sl:T * (sl + 1)],
                                                  src_view[:, c, :]))
                act.wait(t_slot[sl])
                ts = act.done(act.raw.activation(out=sqc[:, T * sl:T * (sl + 1)],
                                                 in_=src_view[:, c, :],
                                                 func=AF.Square, bias=0.0, scale=1.0))
                pe.wait(tc, ts)
                pe.raw.matmul(ps1[:, 0:T], ones_b[:, :], bfc[:, T * sl:T * (sl + 1)],
                              start=(c == 0), stop=(c == chunks - 1))
                mmi = pe.raw.matmul(ps2[:, 0:T], ones_b[:, :], sqc[:, T * sl:T * (sl + 1)],
                                    start=(c == 0), stop=(c == chunks - 1))
                t_mm = pe.done(mmi)
                t_slot[sl] = t_mm
            dve.wait(t_mm)
            t1 = dve.done(dve.raw.tensor_single_scalar(row(SM, T), ps1[:, 0:T],
                                                       1.0 / n_dims, ALU.mult))
            t2 = dve.done(dve.raw.tensor_single_scalar(row(SQ, T), ps2[:, 0:T],
                                                       1.0 / n_dims, ALU.mult))
            dve.wait(t1)
            t3 = dve.done(dve.raw.tensor_tensor(row(SVAR, T), row(SM, T), row(SM, T), ALU.mult))
            dve.wait(t2, t3)
            t4 = dve.done(dve.raw.tensor_tensor(row(SVAR, T), row(SQ, T), row(SVAR, T),
                                                ALU.subtract))
            act.wait(t4)
            t5 = act.done(act.raw.activation(out=row(SRS, T), in_=row(SVAR, T), func=AF.Ln,
                                             bias=eps_t[:, :], scale=1.0))
            act.wait(t5)
            return act.done(act.raw.activation(out=row(SRS, T), in_=row(SRS, T), func=AF.Exp,
                                               bias=0.0, scale=-0.5))

        def ln_norm(src_view, T, chunks, mb_sb, rb_sb, ps_a, ps_b, t_rs, out_view,
                    add_view=None):
            pe.wait(t_rs)
            tbm = pe.done(pe.raw.matmul(ps_a[:, 0:T], ones_f[:, :], row(SM, T),
                                        start=True, stop=True))
            tbr = pe.done(pe.raw.matmul(ps_b[:, 0:T], ones_f[:, :], row(SRS, T),
                                        start=True, stop=True))
            dve.wait(tbm)
            tem = dve.done(dve.raw.tensor_copy(mb_sb[:, 0:T], ps_a[:, 0:T]))
            dve.wait(tbr)
            ter = dve.done(dve.raw.tensor_copy(rb_sb[:, 0:T], ps_b[:, 0:T]))
            mb_b = mb_sb[:, 0:T].rearrange("p (o n) -> p o n", o=1).broadcast_to([128, chunks, T])
            rb_b = rb_sb[:, 0:T].rearrange("p (o n) -> p o n", o=1).broadcast_to([128, chunks, T])
            dve.wait(tem)
            tn1 = dve.done(dve.raw.tensor_tensor(src_view, src_view, mb_b, ALU.subtract))
            dve.wait(tn1, ter)
            if add_view is None:
                tn2 = dve.done(dve.raw.tensor_tensor(out_view, src_view, rb_b, ALU.mult))
            else:
                tn2a = dve.done(dve.raw.tensor_tensor(src_view, src_view, rb_b, ALU.mult))
                dve.wait(tn2a)
                tn2 = dve.done(dve.raw.tensor_tensor(out_view, src_view, add_view, ALU.add))
            return tn2, (tbm, tbr)

        t_cm = qs.done(qs.raw.dma_start(out=cmeta_t[:, :], in_=cmeta))
        qs.done(qs.raw.dma_start(out=kvw_t[:, :], in_=kvw))
        dve.done(dve.raw.memset(ones_b[:, :], 1.0))
        dve.done(dve.raw.memset(ones_s[:, :], float(2.0 ** -30)))
        dve.done(dve.raw.memset(ln2n30[:, :], float(-30.0 * np.log(2.0))))
        t_onef = dve.done(dve.raw.memset(ones_f[:, :], 1.0))
        dve.done(dve.raw.memset(onesc_f[:, :], 1.0))
        dve.done(dve.raw.memset(eps_ln_t[:, :], EPS_LN))
        t_ep2 = dve.done(dve.raw.memset(eps_rms_t[:, :], EPS_RMS))

        with ExitStack() as bd_scope:
            masks = bd_scope.enter_context(nc.sbuf_tensor("masks", [128, 8 * 256], BF16))
            kpefull = bd_scope.enter_context(nc.sbuf_tensor("kpefull", [65, S], BF16))

            with ExitStack() as ph:
                cs_row = ph.enter_context(nc.sbuf_tensor("cs_row", [1, 2 * DR], F32))
                iq = ph.enter_context(nc.sbuf_tensor("iq", [128, 256], F32))
                iqa = ph.enter_context(nc.sbuf_tensor("iqa", [128, 256], F32))
                ps_b0 = ph.enter_context(nc.psum_tensor("ps_b0", [128, 512], F32))
                t_cos = qs.done(qs.raw.dma_start(out=cs_row[:, 0:DR], in_=cosr))
                t_sin = qs.done(qs.raw.dma_start(out=cs_row[:, DR:2 * DR], in_=sinr))
                dve.wait(t_sin)
                t_neg = dve.done(dve.raw.tensor_single_scalar(
                    cs_row[:, DR:DR + 32], cs_row[:, DR:DR + 32], -1.0, ALU.mult))
                pe.wait(t_cos, t_onef)
                t1 = pe.done(pe.raw.matmul(ps_b0[:, 0:DR], ones_f[:, :], cs_row[:, 0:DR],
                                           start=True, stop=True))
                dve.wait(t1)
                t_cb = dve.done(dve.raw.tensor_copy(cosb[:, :], ps_b0[:, 0:DR]))
                pe.wait(t_cb, t_neg)
                t2 = pe.done(pe.raw.matmul(ps_b0[:, 0:DR], ones_f[:, :], cs_row[:, DR:2 * DR],
                                           start=True, stop=True))
                dve.wait(t2)
                t_sb = dve.done(dve.raw.tensor_copy(spb[:, :], ps_b0[:, 0:DR]))
                t_io = gpc.done(gpc.raw.iota(iq[:, :], pattern=[[4, 256]], base=0,
                                             channel_multiplier=0,
                                             allow_small_or_imprecise_dtypes=True))
                dve.wait(t_io, t_cm)
                t_iqa = dve.done(dve.raw.tensor_single_scalar(iqa[:, :], iq[:, :],
                                                              cmeta_t[:, 0:1], ALU.subtract))
                dve.wait(t_iqa)
                for j in range(8):
                    mj = dve.raw.tensor_single_scalar(masks[:, 256 * j:256 * (j + 1)],
                                                      iqa[:, :], float(128 * j), ALU.is_ge)
                dve.done(mj)
                em.barrier()

            NB = 512
            NBLK = S // NB
            with ExitStack() as bc_scope:
                ln1own = bc_scope.enter_context(nc.sbuf_tensor("ln1own", [128, NKC * NT], BF16))

                with ExitStack() as ph:
                    P = ph.enter_context
                    wkc_t = P(nc.sbuf_tensor("wkc_t", [128, NKC * 512], BF16))
                    weffk = P(nc.sbuf_tensor("weffk", [128, NKC * DR], BF16))
                    t_wkc = qg.done(qg.raw.dma_start(
                        out=cm(wkc_t, NKC), in_=wkc.rearrange("(c p) n -> p c n", p=128)))
                    with ExitStack() as ph2:
                        wkpf = ph2.enter_context(nc.sbuf_tensor("wkpf", [128, NKC * DR], F32))
                        wktmp = ph2.enter_context(nc.sbuf_tensor("wktmp", [128, NKC * DR], F32))
                        t_wkp = qs.done(qs.raw.dma_start(
                            out=cm(wkpf, NKC), in_=wkp.rearrange("(c p) n -> p c n", p=128)))
                        dve.wait(t_wkp, t_sb)
                        cb3 = cosb[:, :].rearrange("p (o d) -> p o d", o=1) \
                            .broadcast_to([128, NKC, DR])
                        sb4 = spb[:, :].rearrange("p (o t d) -> p o t d", o=1, t=2) \
                            .broadcast_to([128, NKC, 2, 32])
                        t1 = dve.done(dve.raw.tensor_tensor(cm(wktmp, NKC), cm(wkpf, NKC),
                                                            cb3, ALU.mult))
                        wperm = wkpf[:, :].rearrange("p (c t d) -> p c t d",
                                                     c=NKC, t=2)[:, :, 1::-1, :]
                        weffk4 = weffk[:, :].rearrange("p (c t d) -> p c t d", c=NKC, t=2)
                        t2 = dve.done(dve.raw.tensor_tensor(weffk4, wperm, sb4, ALU.mult))
                        dve.wait(t1, t2)
                        t_wk = dve.done(dve.raw.tensor_tensor(weffk[:, :], weffk[:, :],
                                                              wktmp[:, :], ALU.add))
                    xblk0 = P(nc.sbuf_tensor("xblk0", [128, NKC * NB], F32))
                    xblk1 = P(nc.sbuf_tensor("xblk1", [128, NKC * NB], F32))
                    xbf = P(nc.sbuf_tensor("xbf", [128, NKC * NB], BF16))
                    xsq = P(nc.sbuf_tensor("xsq", [128, NKC * NB], BF16))
                    mb_s = P(nc.sbuf_tensor("mb_s", [128, NB], F32))
                    rb_s = P(nc.sbuf_tensor("rb_s", [128, NB], F32))
                    ln1b = P(nc.sbuf_tensor("ln1b", [128, NKC * NB], BF16))
                    cf32 = P(nc.sbuf_tensor("cf32", [128, 4 * NB], F32))
                    csq = P(nc.sbuf_tensor("csq", [128, 4 * NB], BF16))
                    rr_s = rb_s
                    rmsb = P(nc.sbuf_tensor("rmsb", [128, 4 * NB], BF16))
                    ps_s1 = P(nc.psum_tensor("ps_s1", [1, 512], F32))
                    ps_s2 = P(nc.psum_tensor("ps_s2", [1, 512], F32))
                    ps_c0 = P(nc.psum_tensor("ps_c0", [128, 512], F32))
                    ps_c1 = P(nc.psum_tensor("ps_c1", [128, 512], F32))
                    ps_c2 = P(nc.psum_tensor("ps_c2", [128, 512], F32))
                    ps_c3 = P(nc.psum_tensor("ps_c3", [128, 512], F32))
                    ps_kp = P(nc.psum_tensor("ps_kp", [64, 512], F32))
                    cps = [ps_c0, ps_c1, ps_c2, ps_c3]

                    xb_ = [xblk0, xblk1]
                    t_ck = [t_wk, t_wk]
                    t_scr_free = None
                    t_rmsn = None
                    t_rms_dma = None
                    t_kpe_ev = None
                    t_ps_s_rd = None
                    t_r2_prev = None
                    t_ps_c_rd = None
                    t_xd = [None, None]

                    def x_dma(b):
                        slb = xb_[b % 2]
                        qs.wait(t_ck[b % 2])
                        t_xd[b % 2] = qs.done(qs.raw.dma_start(
                            out=cm(slb, NKC),
                            in_=xT[:, NB * b:NB * (b + 1)].rearrange("(c p) n -> p c n",
                                                                     p=128)))
                    x_dma(0)
                    for b in range(NBLK):
                        sl = xb_[b % 2]
                        slv = cm(sl, NKC)
                        t_x = t_xd[b % 2]
                        if b + 1 < NBLK:
                            x_dma(b + 1)
                        dve.wait(t_x, t_scr_free)
                        tc = dve.done(dve.raw.tensor_copy(xbf[:, :], sl[:, :]))
                        act.wait(t_x, t_scr_free)
                        ts = act.done(act.raw.activation(out=xsq[:, :], in_=sl[:, :],
                                                         func=AF.Square, bias=0.0, scale=1.0))
                        pe.wait(tc, ts, t_ps_s_rd)
                        for c in range(NKC):
                            pe.raw.matmul(ps_s1[:, 0:NB], ones_b[:, :],
                                          xbf[:, NB * c:NB * (c + 1)],
                                          start=(c == 0), stop=(c == NKC - 1))
                            mmi = pe.raw.matmul(ps_s2[:, 0:NB], ones_b[:, :],
                                                xsq[:, NB * c:NB * (c + 1)],
                                                start=(c == 0), stop=(c == NKC - 1))
                        t_mm = pe.done(mmi)
                        dve.wait(t_mm)
                        t1 = dve.done(dve.raw.tensor_single_scalar(row(SM, NB), ps_s1[:, 0:NB],
                                                                   1.0 / HID, ALU.mult))
                        t2 = dve.done(dve.raw.tensor_single_scalar(row(SQ, NB), ps_s2[:, 0:NB],
                                                                   1.0 / HID, ALU.mult))
                        dve.wait(t1)
                        t3 = dve.done(dve.raw.tensor_tensor(row(SVAR, NB), row(SM, NB),
                                                            row(SM, NB), ALU.mult))
                        dve.wait(t2, t3)
                        t4 = dve.done(dve.raw.tensor_tensor(row(SVAR, NB), row(SQ, NB),
                                                            row(SVAR, NB), ALU.subtract))
                        act.wait(t4)
                        t5 = act.done(act.raw.activation(out=row(SRS, NB), in_=row(SVAR, NB),
                                                         func=AF.Ln, bias=eps_ln_t[:, :], scale=1.0))
                        act.wait(t5)
                        t6 = act.done(act.raw.activation(out=row(SRS, NB), in_=row(SRS, NB),
                                                         func=AF.Exp, bias=0.0, scale=-0.5))
                        pe.wait(t6, t_ps_c_rd)
                        tbm = pe.done(pe.raw.matmul(ps_c0[:, 0:NB], ones_f[:, :], row(SM, NB),
                                                    start=True, stop=True))
                        tbr = pe.done(pe.raw.matmul(ps_c1[:, 0:NB], ones_f[:, :], row(SRS, NB),
                                                    start=True, stop=True))
                        dve.wait(tbm)
                        tem = dve.done(dve.raw.tensor_copy(mb_s[:, :], ps_c0[:, 0:NB]))
                        dve.wait(tbr)
                        ter = dve.done(dve.raw.tensor_copy(rb_s[:, :], ps_c1[:, 0:NB]))
                        mb_b = mb_s[:, :].rearrange("p (o n) -> p o n", o=1) \
                            .broadcast_to([128, NKC, NB])
                        rb_b = rb_s[:, :].rearrange("p (o n) -> p o n", o=1) \
                            .broadcast_to([128, NKC, NB])
                        dve.wait(tem)
                        tn1 = dve.done(dve.raw.tensor_tensor(slv, slv, mb_b, ALU.subtract))
                        dve.wait(tn1, ter, t_ck[b % 2])
                        tln = dve.done(dve.raw.tensor_tensor(cm(ln1b, NKC), slv, rb_b, ALU.mult))
                        dve.wait(tln)
                        t_rp = dve.done(dve.raw.tensor_copy(
                            cm(ln1own, NKC)[:, :, (NB // 4) * b:(NB // 4) * (b + 1)],
                            cm(ln1b, NKC)[:, :, 0:NB:4]))
                        pe.wait(tln, tem, ter, t_wkc, t_wk, t_rmsn, t_kpe_ev)
                        for c in range(NKC):
                            for m in range(4):
                                pe.raw.matmul(cps[m][:, 0:NB],
                                              wkc_t[:, 512 * c + 128 * m:512 * c + 128 * (m + 1)],
                                              ln1b[:, NB * c:NB * (c + 1)],
                                              start=(c == 0), stop=(c == NKC - 1))
                            mmi = pe.raw.matmul(ps_kp[:, 0:NB], weffk[:, DR * c:DR * (c + 1)],
                                                ln1b[:, NB * c:NB * (c + 1)],
                                                start=(c == 0), stop=(c == NKC - 1))
                        t_ckv = pe.done(mmi)
                        t_ck[b % 2] = t_ckv
                        dve.wait(t_ckv)
                        for m in range(4):
                            cpi = dve.raw.tensor_copy(cf32[:, NB * m:NB * (m + 1)], cps[m][:, 0:NB])
                        dve.done(cpi)
                        t_cf = dve.done(dve.raw.tensor_copy(kpefull[0:64, NB * b:NB * (b + 1)],
                                                            ps_kp[:, 0:NB]))
                        t_kpe_ev = t_cf
                        act.wait(t_cf, t_r2_prev)
                        t_csq = act.done(act.raw.activation(out=csq[:, :], in_=cf32[:, :],
                                                            func=AF.Square, bias=0.0, scale=1.0))
                        pe.wait(t_csq)
                        for m in range(4):
                            mmi = pe.raw.matmul(ps_s1[:, 0:NB], ones_b[:, :],
                                                csq[:, NB * m:NB * (m + 1)],
                                                start=(m == 0), stop=(m == 3))
                        t_r2 = pe.done(mmi)
                        t_r2_prev = t_r2
                        act.wait(t_r2, t_ep2)
                        t_lr = act.done(act.raw.activation(out=row(SQ, NB), in_=ps_s1[:, 0:NB],
                                                           func=AF.Ln, bias=eps_rms_t[:, :],
                                                           scale=1.0 / LORA))
                        t_ps_s_rd = t_lr
                        act.wait(t_lr)
                        t_er = act.done(act.raw.activation(out=row(SQ, NB), in_=row(SQ, NB),
                                                           func=AF.Exp, bias=0.0, scale=-0.5))
                        pe.wait(t_er)
                        t_br2 = pe.done(pe.raw.matmul(ps_c0[:, 0:NB], ones_f[:, :], row(SQ, NB),
                                                      start=True, stop=True))
                        dve.wait(t_br2)
                        t_err = dve.done(dve.raw.tensor_copy(rr_s[:, :], ps_c0[:, 0:NB]))
                        t_ps_c_rd = t_err
                        dve.wait(t_err, t_rms_dma)
                        for m in range(4):
                            sti = dve.raw.scalar_tensor_tensor(
                                rmsb[:, NB * m:NB * (m + 1)], cf32[:, NB * m:NB * (m + 1)],
                                kvw_t[:, m:m + 1], rr_s[:, :], ALU.mult, ALU.mult)
                        t_rmsn = dve.done(sti)
                        qs.wait(t_rmsn)
                        t_rms_dma = qs.done(qs.raw.dma_start(
                            out=rms_d[:, NB * b:NB * (b + 1)].rearrange("(m p) n -> p m n", p=128),
                            in_=cm(rmsb, 4)))
                    t_kones = dve.done(dve.raw.memset(kpefull[64:65, :], 1.0))
                    em.barrier()
                if DBG:
                    qg.done(qg.raw.dma_start(
                        out=dbg["d_ln1own"].rearrange("(c p) n -> p c n", p=128),
                        in_=cm(ln1own, NKC)))
                    qg.done(qg.raw.dma_start(out=dbg["d_kpe"], in_=kpefull[0:64, :]))
                    t_ = qg.done(qg.raw.dma_start(out=dbg["d_rms"], in_=rms_d))
                    qg.wait(t_)
                    em.barrier()

                qn = bd_scope.enter_context(nc.sbuf_tensor("qn", [128, NH * NT], BF16,
                                                           side="right"))
                qph = bd_scope.enter_context(nc.sbuf_tensor("qph", [65, NH * NT], BF16,
                                                            side="right"))
                with ExitStack() as ph:
                    P = ph.enter_context
                    wtam0 = P(nc.sbuf_tensor("wtam0", [128, 4096], F32))
                    wtam1 = P(nc.sbuf_tensor("wtam1", [128, 4096], F32))
                    racc = P(nc.sbuf_tensor("racc", [128, 2], F32))
                    rtmp = P(nc.sbuf_tensor("rtmp", [128, 1], F32))
                    t_amz = dve.done(dve.raw.memset(racc[:, :], 0.0))
                    am_tiles = []
                    for wi, w_ap in enumerate([wfcT, wpjT]):
                        flat = w_ap.rearrange("a b -> (a b)").rearrange(
                            "(t p n) -> t p n", p=128, n=4096)
                        for ti in range(flat.shape[0]):
                            am_tiles.append((wi, flat[ti]))
                    am_state = {"i": 0, "slot_tok": [None, None], "acc": t_amz}

                    def am_step(k=1):
                        for _ in range(k):
                            i = am_state["i"]
                            if i >= len(am_tiles):
                                return
                            wi, tile = am_tiles[i]
                            sl = i % 2
                            bufw = [wtam0, wtam1][sl]
                            qa.wait(am_state["slot_tok"][sl])
                            td = qa.done(qa.raw.dma_start(out=bufw[:, :], in_=tile))
                            dve.wait(td, am_state["acc"])
                            tr = dve.done(dve.raw.tensor_reduce(
                                rtmp[:, :], bufw[:, :], axis=AX.X, op=ALU.add,
                                apply_absolute_value=True))
                            dve.wait(tr)
                            ta = dve.done(dve.raw.tensor_tensor(
                                racc[:, wi:wi + 1], racc[:, wi:wi + 1], rtmp[:, :], ALU.add))
                            am_state["acc"] = ta
                            am_state["slot_tok"][sl] = ta
                            am_state["i"] = i + 1

                    weffq = P(nc.sbuf_tensor("weffq", [128, 2 * NKC * 128], BF16))
                    wqs0 = P(nc.sbuf_tensor("wqs0", [128, NKC * 128], BF16))
                    wqs1 = P(nc.sbuf_tensor("wqs1", [128, NKC * 128], BF16))
                    wqpf0 = P(nc.sbuf_tensor("wqpf0", [128, NKC * 128], F32))
                    wqpf1 = P(nc.sbuf_tensor("wqpf1", [128, NKC * 128], F32))
                    wqtmp = P(nc.sbuf_tensor("wqtmp", [128, NKC * 128], F32))
                    weffpf = P(nc.sbuf_tensor("weffpf", [128, NKC * 128], F32))
                    ps_q0 = P(nc.psum_tensor("ps_q0", [128, 512], F32))
                    ps_q1 = P(nc.psum_tensor("ps_q1", [128, 512], F32))

                    t_dma = [None, None]
                    t_cons = [None, None]
                    t_weff_cons = None
                    t_weffrd = [None, None]
                    t_qslot = [None, None]

                    def qp_dma(p):
                        buf_ = [wqpf0, wqpf1][p % 2]
                        qs.wait(t_cons[p % 2])
                        t_dma[p % 2] = qs.done(qs.raw.dma_start(
                            out=cm(buf_, NKC),
                            in_=wqp[:, 128 * p:128 * (p + 1)]
                            .rearrange("(c pp) n -> pp c n", pp=128)))
                    qp_dma(0)
                    for p in range(8):
                        am_step(2)
                        buf = [wqpf0, wqpf1][p % 2]
                        t_dcur = t_dma[p % 2]
                        if p + 1 < 8:
                            qp_dma(p + 1)
                        dve.wait(t_dcur, t_weffrd[p % 2])
                        w4 = buf[:, :].rearrange("pp (c h t d) -> pp c h t d", c=NKC, h=2, t=2)
                        cb4 = cosb[:, :].rearrange("pp (o t d) -> pp o t d", o=1, t=2) \
                            .rearrange("pp o t d -> pp o () t d") \
                            .broadcast_to([128, NKC, 2, 2, 32])
                        sb4 = spb[:, :].rearrange("pp (o t d) -> pp o t d", o=1, t=2) \
                            .rearrange("pp o t d -> pp o () t d") \
                            .broadcast_to([128, NKC, 2, 2, 32])
                        ta = dve.done(dve.raw.tensor_tensor(
                            wqtmp[:, :].rearrange("pp (c h t d) -> pp c h t d", c=NKC, h=2, t=2),
                            w4, cb4, ALU.mult))
                        tb = dve.done(dve.raw.tensor_tensor(
                            weffpf[:, :].rearrange("pp (c h t d) -> pp c h t d", c=NKC, h=2, t=2),
                            w4[:, :, :, 1::-1, :], sb4, ALU.mult))
                        t_cons[p % 2] = tb
                        dve.wait(ta, tb)
                        WQW = NKC * 128
                        tcq = dve.done(dve.raw.tensor_tensor(
                            weffq[:, WQW * (p % 2):WQW * (p % 2 + 1)], weffpf[:, :],
                            wqtmp[:, :], ALU.add))
                        for hh in range(2):
                            h = 2 * p + hh
                            for tt in range(2):
                                pst = [ps_q0, ps_q1][tt]
                                pe.wait(tcq, t_qslot[tt])
                                for c in range(NKC):
                                    mmi = pe.raw.matmul(
                                        pst[0:64, :],
                                        weffq[:, WQW * (p % 2) + 128 * c + 64 * hh:
                                              WQW * (p % 2) + 128 * c + 64 * (hh + 1)],
                                        ln1own[:, NT * c + 512 * tt:NT * c + 512 * (tt + 1)],
                                        start=(c == 0), stop=(c == NKC - 1))
                                tq = pe.done(mmi)
                                if hh == 1 and tt == 1:
                                    t_weff_cons = tq
                                    t_weffrd[p % 2] = tq
                                dve.wait(tq)
                                tev = dve.done(dve.raw.tensor_copy(
                                    qph[0:64, NT * h + 512 * tt:NT * h + 512 * (tt + 1)],
                                    pst[0:64, :]))
                                t_qslot[tt] = tev
                    t_dma = [None, None]
                    t_cons = [None, None]

                    def qn_dma(h):
                        buf_ = [wqs0, wqs1][h % 2]
                        qg.wait(t_cons[h % 2])
                        t_dma[h % 2] = qg.done(qg.raw.dma_start(
                            out=cm(buf_, NKC),
                            in_=wqn[:, 128 * h:128 * (h + 1)]
                            .rearrange("(c p) n -> p c n", p=128)))
                    qn_dma(0)
                    for h in range(NH):
                        am_step(2)
                        buf = [wqs0, wqs1][h % 2]
                        t_dcur = t_dma[h % 2]
                        if h + 1 < NH:
                            qn_dma(h + 1)
                        for tt in range(2):
                            pst = [ps_q0, ps_q1][tt]
                            pe.wait(t_dcur, t_qslot[tt])
                            for c in range(NKC):
                                mmi = pe.raw.matmul(
                                    pst[:, :], buf[:, 128 * c:128 * (c + 1)],
                                    ln1own[:, NT * c + 512 * tt:NT * c + 512 * (tt + 1)],
                                    start=(c == 0), stop=(c == NKC - 1))
                            tq = pe.done(mmi)
                            if tt == 1:
                                t_cons[h % 2] = tq
                            dve.wait(tq)
                            tev = dve.done(dve.raw.tensor_copy(
                                qn[:, NT * h + 512 * tt:NT * h + 512 * (tt + 1)], pst[:, :]))
                            t_qslot[tt] = tev
                    with ExitStack() as ph2:
                        sqsc2 = ph2.enter_context(nc.sbuf_tensor("sqsc2", [128, NT], BF16))
                        qrow = ph2.enter_context(nc.sbuf_tensor("qrow", [1, NT], F32))
                        ps_nq = ph2.enter_context(nc.psum_tensor("ps_nq", [1, 512], F32))
                        t_qdma = None
                        t_rd = None
                        t_psrd = None
                        for H in range(NH):
                            dve.wait(t_rd)
                            tq1 = dve.done(dve.raw.tensor_tensor(
                                sqsc2[:, :], qn[:, NT * H:NT * (H + 1)],
                                qn[:, NT * H:NT * (H + 1)], ALU.mult))
                            for tt in range(2):
                                pe.wait(tq1, t_rd, t_psrd)
                                tm1 = pe.done(pe.raw.matmul(
                                    ps_nq[:, :], ones_b[:, :],
                                    sqsc2[:, 512 * tt:512 * (tt + 1)], start=True, stop=True))
                                dve.wait(tm1, t_qdma)
                                te1 = dve.done(dve.raw.tensor_copy(
                                    qrow[:, 512 * tt:512 * (tt + 1)], ps_nq[:, :]))
                                t_psrd = te1
                            tq2 = dve.done(dve.raw.tensor_tensor(
                                sqsc2[0:64, 0:NT], qph[0:64, NT * H:NT * (H + 1)],
                                qph[0:64, NT * H:NT * (H + 1)], ALU.mult))
                            for tt in range(2):
                                pe.wait(tq2, t_psrd)
                                tm3 = pe.done(pe.raw.matmul(
                                    ps_nq[:, :], ones_b[0:64, :],
                                    sqsc2[0:64, 512 * tt:512 * (tt + 1)], start=True, stop=True))
                                t_rd = tm3
                                dve.wait(tm3)
                                te3 = dve.done(dve.raw.tensor_tensor(
                                    qrow[:, 512 * tt:512 * (tt + 1)],
                                    qrow[:, 512 * tt:512 * (tt + 1)], ps_nq[:, :], ALU.add))
                                t_psrd = te3
                            act.wait(te3)
                            ta1 = act.done(act.raw.activation(out=qrow[:, :], in_=qrow[:, :],
                                                              func=AF.Ln, bias=eps_rms_t[:, :],
                                                              scale=1.0))
                            act.wait(ta1)
                            ta2 = act.done(act.raw.activation(out=qrow[:, :], in_=qrow[:, :],
                                                              func=AF.Exp, bias=0.0, scale=0.5))
                            qs.wait(ta2)
                            t_qdma = qs.done(qs.raw.dma_start(out=sqq_d[H:H + 1, :],
                                                              in_=qrow[:, :]))
                            t_rd = ta2

                    em.barrier()
                    ps_am = ph.enter_context(nc.psum_tensor("ps_am", [128, 512], F32))
                    am_step(100)
                    t_z = am_state["acc"]
                    for wi in range(2):
                        pe.wait(t_z)
                        tmm = pe.done(pe.raw.matmul(ps_am[0:1, 0:1], racc[:, wi:wi + 1],
                                                    onesc_f[:, :], start=True, stop=True))
                        dve.wait(tmm)
                        t1 = dve.done(dve.raw.tensor_single_scalar(
                            amv[:, wi:wi + 1], ps_am[0:1, 0:1], 1.0 / (HID * HF), ALU.mult))
                        dve.wait(t1)
                        t2 = dve.done(dve.raw.tensor_single_scalar(
                            amv[:, wi:wi + 1], amv[:, wi:wi + 1], 1e-5, ALU.max))
                        pe.wait(t2)
                    t1 = dve.done(dve.raw.tensor_tensor(amv[:, 2:3], amv[:, 0:1], amv[:, 0:1],
                                                        ALU.mult))
                    dve.wait(t1)
                    t2 = dve.done(dve.raw.tensor_tensor(amv[:, 2:3], amv[:, 2:3], amv[:, 1:2],
                                                        ALU.mult))
                    dve.wait(t2)
                    t3 = dve.done(dve.raw.tensor_single_scalar(amv[:, 2:3], amv[:, 2:3],
                                                               0.125, ALU.mult))
                    for (dst, col, src_, sc) in [(thr_fc, 0, 0, -0.7), (thr_fc, 1, 0, 0.7),
                                                 (thr_pj, 0, 1, -0.7), (thr_pj, 1, 1, 0.7)]:
                        dve.wait(t3)
                        tsc = dve.done(dve.raw.tensor_single_scalar(
                            scr[:, 2:3], amv[:, src_:src_ + 1], sc, ALU.mult))
                        pe.wait(tsc)
                        tmm = pe.done(pe.raw.matmul(ps_am[:, 0:1], ones_f[:, :], scr[:, 2:3],
                                                    start=True, stop=True))
                        dve.wait(tmm)
                        tev = dve.done(dve.raw.tensor_copy(dst[:, col:col + 1], ps_am[:, 0:1]))
                        pe.wait(tev)
                    pe.wait(t3)
                    tmm = pe.done(pe.raw.matmul(ps_am[:, 0:1], ones_f[:, :], amv[:, 2:3],
                                                start=True, stop=True))
                    dve.wait(tmm)
                    tev = dve.done(dve.raw.tensor_copy(cy_b[:, :], ps_am[:, 0:1]))
                    pe.wait(tev)
                em.barrier()
            if DBG:
                qg.done(qg.raw.dma_start(out=dbg["d_qn"].rearrange("(h p) n -> p h n", p=128),
                                          in_=cm(qn, NH)))
                t_ = qg.done(qg.raw.dma_start(
                    out=dbg["d_qp"].rearrange("(h pp) n -> pp h n", pp=64),
                    in_=qph[0:64, :].rearrange("pp (h n) -> pp h n", h=NH)))
                qg.wait(t_)
                em.barrier()

            with ExitStack() as ph:
                P = ph.enter_context
                kq = P(nc.sbuf_tensor("kq", [64, 512], BF16))
                ps_n = P(nc.psum_tensor("ps_n", [1, 512], F32))
                tsg = None
                for sseg in range(8):
                    tsq = dve.done(dve.raw.tensor_tensor(
                        kq[:, :], kpefull[0:64, 512 * sseg:512 * (sseg + 1)],
                        kpefull[0:64, 512 * sseg:512 * (sseg + 1)], ALU.mult))
                    pe.wait(tsq, tsg)
                    t1 = pe.done(pe.raw.matmul(ps_n[:, :], ones_b[0:64, :], kq[:, :],
                                               start=True, stop=True))
                    dve.wait(t1)
                    tsg = dve.done(dve.raw.tensor_reduce(scr[:, 8 + sseg:9 + sseg],
                                                         ps_n[:, :], axis=AX.X, op=ALU.max))
                dve.wait(tsg)
                t1 = dve.done(dve.raw.tensor_tensor(scr[:, 4:5], scr[:, 8:9], scr[:, 9:10],
                                                    ALU.max))
                dve.wait(t1)
                for tp in range(1, 4):
                    ta = dve.done(dve.raw.tensor_tensor(scr[:, 4 + tp:5 + tp],
                                                        scr[:, 3 + tp:4 + tp],
                                                        scr[:, 8 + 2 * tp:9 + 2 * tp], ALU.max))
                    dve.wait(ta)
                    tb = dve.done(dve.raw.tensor_tensor(scr[:, 4 + tp:5 + tp],
                                                        scr[:, 4 + tp:5 + tp],
                                                        scr[:, 9 + 2 * tp:10 + 2 * tp], ALU.max))
                    dve.wait(tb)
            em.barrier()

            em.barrier()
            em.barrier()
            if DBG:
                t_ = qg.done(qg.raw.dma_start(out=dbg["d_am"], in_=amv[0:1, :]))
                qg.wait(t_)

            NG = 8
            with ExitStack() as ph:
                P = ph.enter_context
                kngs = [P(nc.sbuf_tensor("kngA", [128, 2 * S], BF16)),
                        P(nc.sbuf_tensor("kngB", [128, 2 * S], BF16))]
                vbufs = [P(nc.sbuf_tensor("vbufA", [128, 32 * 256], BF16)),
                         P(nc.sbuf_tensor("vbufB", [128, 32 * 256], BF16))]
                wkbks = [P(nc.sbuf_tensor("wkbk0", [128, 4 * 256], BF16)),
                         P(nc.sbuf_tensor("wkbk1", [128, 4 * 256], BF16))]
                wkbvs = [P(nc.sbuf_tensor("wkbv0", [128, 4 * 256], BF16)),
                         P(nc.sbuf_tensor("wkbv1", [128, 4 * 256], BF16))]
                rmst0 = P(nc.sbuf_tensor("rmst0", [128, 4 * 512], BF16))
                rmst1 = P(nc.sbuf_tensor("rmst1", [128, 4 * 512], BF16))
                pts6 = [P(nc.sbuf_tensor(f"pt{i}", [128, 512], BF16)) for i in range(6)]
                ksq = P(nc.sbuf_tensor("ksq", [128, 1024], BF16))
                nqrow = P(nc.sbuf_tensor("nqrow", [1, NT], F32))
                nqrow2 = P(nc.sbuf_tensor("nqrow2", [1, NT], F32))
                sr_at = P(nc.sbuf_tensor("sr_at", [1, 1024], F32))
                rb2_s = P(nc.sbuf_tensor("rb2_s", [128, 512], F32))
                onrm0 = P(nc.sbuf_tensor("onrm0", [128, 256], BF16))
                onrm1 = P(nc.sbuf_tensor("onrm1", [128, 256], BF16))
                ps_SA = P(nc.psum_tensor("ps_SA", [128, 512], F32))
                ps_SB = P(nc.psum_tensor("ps_SB", [128, 512], F32))
                ps_SC = P(nc.psum_tensor("ps_SC", [128, 512], F32))
                ps_o0 = P(nc.psum_tensor("ps_o0", [128, 256], F32))
                ps_o1 = P(nc.psum_tensor("ps_o1", [128, 256], F32))
                ps_sm0 = P(nc.psum_tensor("ps_sm0", [1, 256], F32))
                ps_sm1 = P(nc.psum_tensor("ps_sm1", [1, 256], F32))
                ps_kv = P(nc.psum_tensor("ps_kv", [128, 512], F32))

                Sbanks = [ps_SA, ps_SB, ps_SC]
                ps_os = [ps_o0, ps_o1]
                ps_sms = [ps_sm0, ps_sm1]
                onrms = [onrm0, onrm1]
                bank = {"tok": None}
                t_rfree = [None, None]
                t_wkfree = [None, None]
                kv_ready = {}
                shift_ready = {}
                t_shift_dma = {}
                t_sm_free = [None, None]
                t_o_free = [None, None]
                t_pt_free = [None] * 6
                t_Sfree = [None, None, None]
                t_ot_dma = [None, None]
                state = {"pti": 0, "sti": 0, "epar": 0, "epi": None,
                         "ksq2_rd": None, "nqrow_rd": None}

                def kvb_closures(gg):
                    t_rdma = [None, None]
                    kb_ = kngs[gg % 2]
                    vb_ = vbufs[gg % 2]
                    wk_ = wkbks[gg % 2]
                    wv_ = wkbvs[gg % 2]
                    cls = []

                    def dma_w():
                        qg.wait(t_wkfree[gg % 2])
                        t1 = qg.done(qg.raw.dma_start(
                            out=cm(wk_, 4),
                            in_=wkbk[:, 256 * gg:256 * (gg + 1)]
                            .rearrange("(c p) n -> p c n", p=128)))
                        t2 = qg.done(qg.raw.dma_start(
                            out=cm(wv_, 4),
                            in_=wkbv[:, 256 * gg:256 * (gg + 1)]
                            .rearrange("(c p) n -> p c n", p=128)))
                        kv_ready.setdefault(gg, []).extend([t1, t2])
                    cls.append(dma_w)

                    def rms_dma(s):
                        bufr_ = [rmst0, rmst1][s % 2]
                        qs.wait(t_rfree[s % 2])
                        t_rdma[s % 2] = qs.done(qs.raw.dma_start(
                            out=cm(bufr_, 4),
                            in_=rms_d[:, 512 * s:512 * (s + 1)]
                            .rearrange("(c p) n -> p c n", p=128)))

                    def mk_stile(s):
                        def fn():
                            bufr = [rmst0, rmst1][s % 2]
                            if s == 0:
                                rms_dma(0)
                            t_r = t_rdma[s % 2]
                            if s + 1 < 8:
                                rms_dma(s + 1)
                            tw = kv_ready[gg][:2]
                            for h2 in range(2):
                                pe.wait(t_r, bank["tok"], *tw)
                                for c in range(4):
                                    mmi = pe.raw.matmul(
                                        ps_kv[:, :],
                                        cm(wk_, 4)[:, c, 128 * h2:128 * (h2 + 1)],
                                        bufr[:, 512 * c:512 * (c + 1)],
                                        start=(c == 0), stop=(c == 3))
                                tkn = pe.done(mmi)
                                dve.wait(tkn)
                                tev = dve.done(dve.raw.tensor_copy(
                                    kb_[:, S * h2 + 512 * s:S * h2 + 512 * (s + 1)],
                                    ps_kv[:, :]))
                                bank["tok"] = tev
                            for half in range(2):
                                pe.wait(bank["tok"])
                                for i in range(2):
                                    sub = 2 * half + i
                                    for c in range(4):
                                        mmi = pe.raw.matmul(
                                            ps_kv[:, 256 * i:256 * (i + 1)],
                                            bufr[:, 512 * c + 128 * sub:512 * c + 128 * (sub + 1)],
                                            cm(wv_, 4)[:, c, :],
                                            start=(c == 0 and i == 0), stop=(c == 3 and i == 1))
                                tkn = pe.done(mmi)
                                dve.wait(tkn)
                                tev = dve.done(dve.raw.tensor_copy(
                                    vb_[:, 256 * (4 * s + 2 * half):256 * (4 * s + 2 * half + 2)],
                                    ps_kv[:, :]))
                                bank["tok"] = tev
                            t_rfree[s % 2] = bank["tok"]
                            kv_ready.setdefault(gg, []).append(bank["tok"])
                        return fn
                    for s in range(8):
                        cls.append(mk_stile(s))
                    return cls

                def norm_closures(gg):
                    kb_ = kngs[gg % 2]
                    cls = []

                    def mk_seg(sseg):
                        def fn():
                            tzk = None
                            for h2 in range(2):
                                ksl = h2
                                dve.wait(state.get(("ksqrd", ksl)))
                                tsq = dve.done(dve.raw.tensor_tensor(
                                    ksq[:, 512 * ksl:512 * (ksl + 1)],
                                    kb_[:, S * h2 + 512 * sseg:S * h2 + 512 * (sseg + 1)],
                                    kb_[:, S * h2 + 512 * sseg:S * h2 + 512 * (sseg + 1)],
                                    ALU.mult))
                                pe.wait(tsq, bank["tok"])
                                t1 = pe.done(pe.raw.matmul(ps_kv[0:1, 0:512], ones_b[:, :],
                                                           ksq[:, 512 * ksl:512 * (ksl + 1)],
                                                           start=True, stop=True))
                                state[("ksqrd", ksl)] = t1
                                dve.wait(t1)
                                tm = dve.done(dve.raw.tensor_reduce(
                                    scr[:, 1:2], ps_kv[0:1, 0:512], axis=AX.X, op=ALU.max))
                                bank["tok"] = tm
                                dve.wait(tm)
                                tzk = dve.done(dve.raw.tensor_tensor(
                                    scr[:, 8 + sseg:9 + sseg], scr[:, 8 + sseg:9 + sseg],
                                    scr[:, 1:2], ALU.max))
                            shift_ready[(gg, "seg", sseg)] = tzk
                        return fn

                    def seg_init():
                        t0 = dve.done(dve.raw.memset(scr[:, 8:16], 0.0))
                        shift_ready[(gg, "seginit")] = t0
                    cls.append(seg_init)
                    for sseg in range(8):
                        cls.append(mk_seg(sseg))

                    def prefixes():
                        tpv = dve.done(dve.raw.tensor_tensor(scr[:, 0:1], scr[:, 8:9],
                                                             scr[:, 9:10], ALU.max))
                        for tp in range(1, 4):
                            dve.wait(tpv)
                            ta = dve.done(dve.raw.tensor_tensor(
                                scr[:, tp:tp + 1], scr[:, tp - 1:tp],
                                scr[:, 8 + 2 * tp:9 + 2 * tp], ALU.max))
                            dve.wait(ta)
                            tpv = dve.done(dve.raw.tensor_tensor(
                                scr[:, tp:tp + 1], scr[:, tp:tp + 1],
                                scr[:, 9 + 2 * tp:10 + 2 * tp], ALU.max))
                        dve.wait(tpv)
                        tkk = dve.done(dve.raw.tensor_tensor(scr[:, 0:4], scr[:, 0:4],
                                                             scr[:, 4:8], ALU.add))
                        act.wait(tkk)
                        t1 = act.done(act.raw.activation(out=scr[:, 8:12], in_=scr[:, 0:4],
                                                         func=AF.Ln, bias=eps_rms_t[:, :],
                                                         scale=1.0))
                        act.wait(t1)
                        t2 = act.done(act.raw.activation(out=scr[:, 8:12], in_=scr[:, 8:12],
                                                         func=AF.Exp, bias=0.0, scale=0.5))
                        dve.wait(t2)
                        tnk = dve.done(dve.raw.tensor_single_scalar(scr[:, 8:12], scr[:, 8:12],
                                                                    -1.0, ALU.mult))
                        shift_ready[(gg, "nk")] = tnk
                    cls.append(prefixes)

                    def mk_shift(h2):
                        def fn():
                            H = 2 * gg + h2
                            qs.wait(state["nqrow_rd"])
                            t_qld = qs.done(qs.raw.dma_start(out=nqrow[:, :],
                                                             in_=sqq_d[H:H + 1, :]))
                            dve.wait(t_qld, shift_ready[(gg, "nk")], t_shift_dma.get(H - 1),
                                     state["nqrow_rd"])
                            tsl = None
                            for tp in range(4):
                                tsl = dve.done(dve.raw.tensor_scalar(
                                    out=nqrow2[:, 256 * tp:256 * (tp + 1)],
                                    in0=nqrow[:, 256 * tp:256 * (tp + 1)],
                                    scalar1=scr[:, 8 + tp:9 + tp], scalar2=float(50.0 / SCALE),
                                    op0=ALU.mult, op1=ALU.add))
                            state["nqrow_rd"] = tsl
                            qg.wait(tsl)
                            t_shift_dma[H] = qg.done(qg.raw.dma_start(
                                out=qph[64:65, NT * H:NT * (H + 1)], in_=nqrow2[:, :]))
                        return fn
                    cls.append(mk_shift(0))
                    cls.append(mk_shift(1))
                    return cls

                def emit_spv(pend, par, nkb, Hloc, gg, first):
                    slot, st0, t_e, t_m = pend
                    vb_ = vbufs[gg % 2]
                    pe.wait(t_e, t_m)
                    if first:
                        pe.wait(t_sm_free[par], t_o_free[par])
                    for i in range(2):
                        kb = 2 * st0 + i
                        pe.raw.matmul(ps_sms[par][:, :], ones_s[:, :],
                                      pts6[slot][:, 256 * i:256 * (i + 1)],
                                      start=(kb == 0), stop=(kb == nkb - 1))
                        mmi = pe.raw.matmul(
                            ps_os[par][:, :],
                            vb_[:, 256 * kb + 128 * Hloc:256 * kb + 128 * (Hloc + 1)],
                            pts6[slot][:, 256 * i:256 * (i + 1)],
                            start=(kb == 0), stop=(kb == nkb - 1))
                    t_pv = pe.done(mmi)
                    t_pt_free[slot] = t_pv
                    return t_pv

                def flush_epilogue(e):
                    par = e["par"]
                    pe.wait(e["t_r"], bank["tok"])
                    t_bb = pe.done(pe.raw.matmul(ps_kv[:, 0:256], ones_f[:, :],
                                                 sr_at[:, 512 + 256 * par:768 + 256 * par],
                                                 start=True, stop=True))
                    dve.wait(t_bb)
                    t_rbe = dve.done(dve.raw.tensor_copy(
                        rb2_s[:, 256 * par:256 * (par + 1)], ps_kv[:, 0:256]))
                    bank["tok"] = t_rbe
                    dve.wait(t_rbe, t_ot_dma[par])
                    t_on = dve.done(dve.raw.tensor_tensor(
                        onrms[par][:, :], ps_os[par][:, :],
                        rb2_s[:, 256 * par:256 * (par + 1)], ALU.mult))
                    t_o_free[par] = t_on
                    qg.wait(t_on)
                    t_ot_dma[par] = qg.done(qg.raw.dma_start(
                        out=ot_d[128 * e["H"]:128 * (e["H"] + 1),
                                 256 * e["tp"]:256 * (e["tp"] + 1)],
                        in_=onrms[par][:, :]))

                def emit_attention(gg, inject):
                    kb_ = kngs[gg % 2]
                    inj = list(inject)
                    ii = 0

                    def do_inject():
                        nonlocal ii
                        if ii < len(inj):
                            inj[ii]()
                            ii += 1
                    for tp in range(4):
                        nkb = 8 * (tp + 1)
                        for h2 in range(2):
                            H = 2 * gg + h2
                            qrhs_n = qn[:, NT * H + 256 * tp:NT * H + 256 * (tp + 1)]
                            qrhs_p = qph[0:65, NT * H + 256 * tp:NT * H + 256 * (tp + 1)]
                            pe.wait(t_shift_dma[H], *kv_ready[gg])
                            ntile = nkb // 2
                            pq = []
                            par = state["epar"] % 2
                            first_spv = True
                            for st in range(ntile):
                                sb = state["sti"] % 3
                                state["sti"] += 1
                                Sps = Sbanks[sb]
                                pe.wait(t_Sfree[sb])
                                for i in range(2):
                                    kb = 2 * st + i
                                    colo = 256 * i
                                    pe.raw.matmul(
                                        Sps[:, colo:colo + 256],
                                        kb_[:, S * h2 + 128 * kb:S * h2 + 128 * (kb + 1)],
                                        qrhs_n, start=(i == 0), stop=False)
                                    mmi = pe.raw.matmul(
                                        Sps[:, colo:colo + 256],
                                        kpefull[0:65, 128 * kb:128 * (kb + 1)],
                                        qrhs_p, start=False, stop=(i == 1))
                                t_S = pe.done(mmi)
                                if len(pq) >= 2:
                                    emit_spv(pq.pop(0), par, nkb, h2, gg, first_spv)
                                    first_spv = False
                                if st == 2 and state["epi"] is not None:
                                    flush_epilogue(state["epi"])
                                    state["epi"] = None
                                slot = state["pti"] % 6
                                state["pti"] += 1
                                act.wait(t_S, t_pt_free[slot])
                                t_e = act.done(act.raw.activation(out=pts6[slot][:, :],
                                                                  in_=Sps[:, :], func=AF.Exp,
                                                                  bias=0.0, scale=SCALE))
                                t_Sfree[sb] = t_e
                                t_m = None
                                for i in range(2):
                                    kb = 2 * st + i
                                    j = kb - (nkb - 8)
                                    if j >= 0:
                                        dve.wait(t_e)
                                        t_m = dve.done(dve.raw.tensor_tensor(
                                            pts6[slot][:, 256 * i:256 * (i + 1)],
                                            pts6[slot][:, 256 * i:256 * (i + 1)],
                                            masks[:, 256 * j:256 * (j + 1)], ALU.mult))
                                pq.append((slot, st, t_e, t_m))
                            for pend in pq:
                                t_pv = emit_spv(pend, par, nkb, h2, gg, first_spv)
                                first_spv = False
                            pq = []
                            dve.wait(t_pv)
                            t_sev = dve.done(dve.raw.tensor_copy(
                                sr_at[:, 256 * par:256 * (par + 1)], ps_sms[par][:, :]))
                            t_sm_free[par] = t_sev
                            act.wait(t_sev)
                            t_l = act.done(act.raw.activation(
                                out=sr_at[:, 512 + 256 * par:768 + 256 * par],
                                in_=sr_at[:, 256 * par:256 * (par + 1)], func=AF.Ln,
                                bias=0.0, scale=1.0))
                            act.wait(t_l)
                            t_r = act.done(act.raw.activation(
                                out=sr_at[:, 512 + 256 * par:768 + 256 * par],
                                in_=sr_at[:, 512 + 256 * par:768 + 256 * par], func=AF.Exp,
                                bias=ln2n30[:, :], scale=-1.0))
                            if state["epi"] is not None:
                                flush_epilogue(state["epi"])
                            state["epi"] = dict(par=par, t_r=t_r, H=H, tp=tp)
                            state["epar"] += 1
                            do_inject()
                            do_inject()
                            do_inject()
                    while ii < len(inj):
                        do_inject()

                for fn in kvb_closures(0) + norm_closures(0):
                    fn()
                for gg in range(NG):
                    nxt = kvb_closures(gg + 1) + norm_closures(gg + 1) if gg + 1 < NG else []
                    emit_attention(gg, nxt)
                if state["epi"] is not None:
                    flush_epilogue(state["epi"])
                    state["epi"] = None
                em.barrier()
            em.barrier()
        if DBG:
            qg.done(qg.raw.dma_start(out=dbg["d_shift"], in_=qph[64:65, :]))
            t_ = qg.done(qg.raw.dma_start(out=dbg["d_sqq"], in_=sqq_d))
            qg.wait(t_)
            em.barrier()
        if DBG:
            t_ = qg.done(qg.raw.dma_start(out=dbg["d_ot"], in_=ot_d))
            qg.wait(t_)
            em.barrier()

        NTH = NT // 2
        eg_scope = ctx.enter_context(ExitStack())
        ln2 = eg_scope.enter_context(nc.sbuf_tensor("ln2", [128, NKC * NT], BF16))
        with ExitStack() as ph:
            P = ph.enter_context
            ots = P(nc.sbuf_tensor("ots", [128, NKC * NT], BF16))
            attn = P(nc.sbuf_tensor("attn", [128, NKC * NTH], F32))
            xow = P(nc.sbuf_tensor("xow", [128, NKC * NTH], F32))
            wos0 = P(nc.sbuf_tensor("wos0", [128, NKC * 128], BF16))
            wos1 = P(nc.sbuf_tensor("wos1", [128, NKC * 128], BF16))
            bfc = P(nc.sbuf_tensor("bfc", [128, 1024], BF16))
            sqc = P(nc.sbuf_tensor("sqc", [128, 1024], BF16))
            mb2 = P(nc.sbuf_tensor("mb2", [128, 512], F32))
            rb3 = P(nc.sbuf_tensor("rb3", [128, 512], F32))
            ps_e0 = P(nc.psum_tensor("ps_e0", [128, 512], F32))
            ps_e1 = P(nc.psum_tensor("ps_e1", [128, 512], F32))
            ps_f1 = P(nc.psum_tensor("ps_f1", [1, 512], F32))
            ps_f2 = P(nc.psum_tensor("ps_f2", [1, 512], F32))
            t_ot = qs.done(qs.raw.dma_start(out=cm(ots, NKC),
                                            in_=ot_d.rearrange("(c p) n -> p c n", p=128)))
            t_x1dma = None
            for hf in range(2):
                t_xo = qs.done(qs.raw.dma_start(
                    out=cm(xow, NKC),
                    in_=x_own[:, NTH * hf:NTH * (hf + 1)].rearrange("(c p) n -> p c n", p=128)))
                t_wdma = [None, None]
                t_wfree = [None, None]
                t_eslot = [None, None]

                def wo_dma(m):
                    bw = [wos0, wos1][m % 2]
                    qq_ = qg
                    qq_.wait(t_wfree[m % 2])
                    t_wdma[m % 2] = qq_.done(qq_.raw.dma_start(
                        out=cm(bw, NKC),
                        in_=woT[:, 128 * m:128 * (m + 1)].rearrange("(c p) n -> p c n",
                                                                    p=128)))
                wo_dma(0)
                for m in range(NKC):
                    bw = [wos0, wos1][m % 2]
                    t_dcur = t_wdma[m % 2]
                    if m + 1 < NKC:
                        wo_dma(m + 1)
                    pst = [ps_e0, ps_e1][m % 2]
                    pe.wait(t_dcur, t_ot, t_eslot[m % 2])
                    for c in range(NKC):
                        mmi = pe.raw.matmul(pst[:, :], bw[:, 128 * c:128 * (c + 1)],
                                            ots[:, NT * c + NTH * hf:NT * c + NTH * (hf + 1)],
                                            start=(c == 0), stop=(c == NKC - 1))
                    tq = pe.done(mmi)
                    t_wfree[m % 2] = tq
                    dve.wait(tq)
                    tev = dve.done(dve.raw.tensor_copy(attn[:, NTH * m:NTH * (m + 1)],
                                                       pst[:, :]))
                    t_eslot[m % 2] = tev
                em.barrier()
                av = cm(attn, NKC)
                t_rs = ln_stats(av, NTH, NKC, HID, eps_ln_t, bfc, sqc, ps_f1, ps_f2)
                dve.wait(t_xo)
                tn, _ = ln_norm(av, NTH, NKC, mb2, rb3, ps_e0, ps_e1, t_rs,
                                out_view=cm(xow, NKC), add_view=cm(xow, NKC))
                qs.wait(tn)
                t_x1dma = qs.done(qs.raw.dma_start(
                    out=x1_d[:, NTH * hf:NTH * (hf + 1)].rearrange("(c p) n -> p c n", p=128),
                    in_=cm(xow, NKC)))
                t_rs2 = ln_stats(cm(xow, NKC), NTH, NKC, HID, eps_ln_t, bfc, sqc, ps_f1, ps_f2)
                dve.wait(t_x1dma)
                tn2, _ = ln_norm(cm(xow, NKC), NTH, NKC, mb2, rb3, ps_e0, ps_e1, t_rs2,
                                 out_view=cm(ln2, NKC)[:, :, NTH * hf:NTH * (hf + 1)])
                em.barrier()
        if DBG:
            qg.done(qg.raw.dma_start(out=dbg["d_x1"].rearrange("(c p) n -> p c n", p=128),
                                      in_=x1_d.rearrange("(c p) n -> p c n", p=128)))
            t_ = qg.done(qg.raw.dma_start(
                out=dbg["d_ln2"].rearrange("(c p) n -> p c n", p=128), in_=cm(ln2, NKC)))
            qg.wait(t_)
            em.barrier()

        NTH = NT // 2
        t_out_final = None
        with ExitStack() as ph:
            P = ph.enter_context
            hfp = P(nc.sbuf_tensor("hfp", [128, NTH], F32))
            ps_gb = [P(nc.psum_tensor(f"ps_g{i}", [128, 512], F32)) for i in range(4)]
            ps_h1 = P(nc.psum_tensor("ps_h1", [1, 512], F32))
            ps_h2 = P(nc.psum_tensor("ps_h2", [1, 512], F32))
            ph2 = ph.enter_context(ExitStack())
            h2 = ph2.enter_context(nc.sbuf_tensor("h2", [128, 40 * NT], BF16))
            with ExitStack() as phf:
                Pf = phf.enter_context
                ws0 = Pf(nc.sbuf_tensor("ws0", [128, NKC * 128], F32))
                ws1 = Pf(nc.sbuf_tensor("ws1", [128, NKC * 128], F32))
                ta1 = Pf(nc.sbuf_tensor("ta1", [128, NKC * 128], BF16))
                tern = Pf(nc.sbuf_tensor("tern", [128, 2 * NKC * 128], BF16))
                t_wfree = [None, None]
                t_fctt = None
                t_gslot = [None, None, None, None]
                t_wdma_fc = [None, None]
                t_terns = [None, None]

                def fc_dma(m):
                    bw = [ws0, ws1][m % 2]
                    qq_ = [qs, qa][m % 2]
                    qq_.wait(t_wfree[m % 2])
                    t_wdma_fc[m % 2] = qq_.done(qq_.raw.dma_start(
                        out=cm(bw, NKC),
                        in_=wfcT[:, 128 * m:128 * (m + 1)].rearrange("(c p) n -> p c n",
                                                                     p=128)))
                fc_dma(0)
                for m in range(40):
                    bw = [ws0, ws1][m % 2]
                    t_d = t_wdma_fc[m % 2]
                    if m + 1 < 40:
                        fc_dma(m + 1)
                    TW = NKC * 128
                    tsl_ = m % 2
                    act.wait(t_d, t_terns[tsl_], t_fctt)
                    tsa = act.done(act.raw.activation(out=ta1[:, :],
                                                      in_=bw[:, :], func=AF.Sign,
                                                      bias=thr_fc[:, 0:1], scale=1.0))
                    tsb = act.done(act.raw.activation(out=tern[:, TW * tsl_:TW * (tsl_ + 1)],
                                                      in_=bw[:, :], func=AF.Sign,
                                                      bias=thr_fc[:, 1:2], scale=1.0))
                    t_wfree[m % 2] = tsb
                    dve.wait(tsa, tsb)
                    tt_ = dve.done(dve.raw.tensor_tensor(
                        tern[:, TW * tsl_:TW * (tsl_ + 1)],
                        tern[:, TW * tsl_:TW * (tsl_ + 1)],
                        ta1[:, :], ALU.add))
                    t_fctt = tt_
                    for hfi in range(2):
                        bk = 2 * (m % 2) + hfi
                        pst = ps_gb[bk]
                        pe.wait(tt_, t_gslot[bk])
                        for c in range(NKC):
                            mmi = pe.raw.matmul(
                                pst[:, 0:NTH],
                                tern[:, TW * tsl_ + 128 * c:TW * tsl_ + 128 * (c + 1)],
                                ln2[:, NT * c + NTH * hfi:NT * c + NTH * (hfi + 1)],
                                start=(c == 0), stop=(c == NKC - 1))
                        th = pe.done(mmi)
                        dve.wait(th)
                        tcp = dve.done(dve.raw.tensor_copy(hfp[:, :], pst[:, 0:NTH]))
                        t_gslot[bk] = tcp
                        dve.wait(tcp)
                        dve.done(dve.raw.scalar_tensor_tensor(
                            h2[:, NT * m + NTH * hfi:NT * m + NTH * (hfi + 1)],
                            hfp[:, :], 0.0, hfp[:, :], ALU.max, ALU.mult))
                    t_terns[tsl_] = th
                em.barrier()
            yb = P(nc.sbuf_tensor("yb", [128, NKC * NT], F32, side="right"))
            with ExitStack() as php:
                Pp = php.enter_context
                wp0 = Pp(nc.sbuf_tensor("wp0", [128, 10 * 128], F32))
                wp1 = Pp(nc.sbuf_tensor("wp1", [128, 10 * 128], F32))
                tpa1 = Pp(nc.sbuf_tensor("tpa1", [128, 10 * 128], BF16))
                ternp = Pp(nc.sbuf_tensor("ternp", [128, 2 * 10 * 128], BF16))
                t_wfree = [None, None]
                t_yslot = [None, None, None, None]
                t_pjtt = None
                t_wdma_pj = [None, None]
                t_ternp = [None, None]
                pj_seq = [(o, kh) for o in range(NKC) for kh in range(4)]

                def pj_dma(i):
                    o_, kh_ = pj_seq[i]
                    bw = [wp0, wp1][i % 2]
                    qq_ = [qs, qa][i % 2]
                    qq_.wait(t_wfree[i % 2])
                    t_wdma_pj[i % 2] = qq_.done(qq_.raw.dma_start(
                        out=cm(bw, 10),
                        in_=wpjT[1280 * kh_:1280 * (kh_ + 1), 128 * o_:128 * (o_ + 1)]
                        .rearrange("(c p) n -> p c n", p=128)))
                pj_dma(0)
                pji = 0
                for o in range(NKC):
                    for kh in range(4):
                        bw = [wp0, wp1][pji % 2]
                        t_d = t_wdma_pj[pji % 2]
                        if pji + 1 < len(pj_seq):
                            pj_dma(pji + 1)
                        pji += 1
                        PW = 10 * 128
                        psl = (pji - 1) % 2
                        act.wait(t_d, t_ternp[psl], t_pjtt)
                        tsa = act.done(act.raw.activation(out=tpa1[:, :],
                                                          in_=bw[:, :],
                                                          func=AF.Sign, bias=thr_pj[:, 0:1],
                                                          scale=1.0))
                        tsb = act.done(act.raw.activation(out=ternp[:, PW * psl:PW * (psl + 1)],
                                                          in_=bw[:, :],
                                                          func=AF.Sign, bias=thr_pj[:, 1:2],
                                                          scale=1.0))
                        t_wfree[psl] = tsb
                        dve.wait(tsa, tsb)
                        tt_ = dve.done(dve.raw.tensor_tensor(
                            ternp[:, PW * psl:PW * (psl + 1)],
                            ternp[:, PW * psl:PW * (psl + 1)],
                            tpa1[:, :], ALU.add))
                        t_pjtt = tt_
                        for hfi in range(2):
                            bk = 2 * (o % 2) + hfi
                            pst = ps_gb[bk]
                            pe.wait(tt_)
                            if kh == 0:
                                pe.wait(t_yslot[bk])
                            for c in range(10):
                                cc = 10 * kh + c
                                mmi = pe.raw.matmul(
                                    pst[:, 0:NTH],
                                    ternp[:, PW * psl + 128 * c:PW * psl + 128 * (c + 1)],
                                    h2[:, NT * cc + NTH * hfi:NT * cc + NTH * (hfi + 1)],
                                    start=(cc == 0), stop=(cc == 39))
                            ty = pe.done(mmi)
                        t_ternp[psl] = ty
                    for hfi in range(2):
                        bk = 2 * (o % 2) + hfi
                        dve.wait(ty)
                        t_yev = dve.done(dve.raw.tensor_single_scalar(
                            cm(yb, NKC)[:, o, NTH * hfi:NTH * (hfi + 1)],
                            ps_gb[bk][:, 0:NTH], cy_b[:, 0:1], ALU.mult))
                        t_yslot[bk] = t_yev
                em.barrier()
            if DBG:
                qg.done(qg.raw.dma_start(
                    out=dbg["d_h2"].rearrange("(c p) n -> p c n", p=128),
                    in_=cm(h2, 40)[:, :, 0:NTH]))
                t_ = qg.done(qg.raw.dma_start(
                    out=dbg["d_y"].rearrange("(c p) n -> p c n", p=128),
                    in_=cm(yb, NKC)[:, :, 0:NTH]))
                qg.wait(t_)
                em.barrier()
            ph2.close()
            with ExitStack() as phe:
                Pe = phe.enter_context
                x1h = Pe(nc.sbuf_tensor("x1h", [128, NKC * NTH], F32))
                bfc2 = Pe(nc.sbuf_tensor("bfc2", [128, 1024], BF16))
                sqc2 = Pe(nc.sbuf_tensor("sqc2", [128, 1024], BF16))
                mb4 = Pe(nc.sbuf_tensor("mb4", [128, 512], F32))
                rb4 = Pe(nc.sbuf_tensor("rb4", [128, 512], F32))
                for hf in range(2):
                    t_x1 = qs.done(qs.raw.dma_start(
                        out=cm(x1h, NKC),
                        in_=x1_d[:, NTH * hf:NTH * (hf + 1)].rearrange("(c p) n -> p c n",
                                                                       p=128)))
                    yv = cm(yb, NKC)[:, :, NTH * hf:NTH * (hf + 1)]
                    t_rs = ln_stats(yv, NTH, NKC, HID, eps_ln_t, bfc2, sqc2, ps_h1, ps_h2)
                    dve.wait(t_x1)
                    tn, _ = ln_norm(yv, NTH, NKC, mb4, rb4, ps_gb[0], ps_gb[1], t_rs,
                                    out_view=cm(x1h, NKC), add_view=cm(x1h, NKC))
                    qs.wait(tn)
                    t_out_final = qs.done(qs.raw.dma_start(
                        out=outT[:, NTH * hf:NTH * (hf + 1)].rearrange("(c p) n -> p c n",
                                                                       p=128),
                        in_=cm(x1h, NKC)))
                    em.barrier()
        qs.wait(t_out_final)
        em.barrier()
    return nc, dbg


def _prep(inputs):
    x = np.ascontiguousarray(inputs["L0_attn_hidden_states"])
    Wq = inputs["L0_attn_Wq"]
    Wkva = inputs["L0_attn_Wkva"]
    kvaw = inputs["L0_attn_kv_a_norm_weight"]
    Wkvb = inputs["L0_attn_Wkvb"]
    Wo = inputs["L0_attn_Wo"]
    cos = inputs["L0_attn_cos"]
    sin = inputs["L0_attn_sin"]
    wfc = inputs["L0_ffn_w_fc"]
    wpj = inputs["L0_ffn_w_proj"]

    f32 = np.float32
    WqT = np.ascontiguousarray(Wq.T.astype(f32, copy=False))
    wqn = np.ascontiguousarray(WqT.reshape(HID, NH, DQ)[:, :, :DN].reshape(HID, NH * DN))
    wqp = np.ascontiguousarray(WqT.reshape(HID, NH, DQ)[:, :, DN:].reshape(HID, NH * DR))
    WkvaT = np.ascontiguousarray(Wkva.T.astype(f32, copy=False))
    wkc = np.ascontiguousarray(WkvaT[:, :LORA])
    wkp = np.ascontiguousarray(WkvaT[:, LORA:])
    WkvbT = np.ascontiguousarray(Wkvb.T.astype(f32, copy=False))
    wkbk = np.ascontiguousarray(WkvbT.reshape(LORA, NH, DN + DV)[:, :, :DN]
                                .reshape(LORA, NH * DN))
    wkbv = np.ascontiguousarray(WkvbT.reshape(LORA, NH, DN + DV)[:, :, DN:]
                                .reshape(LORA, NH * DV))
    woT = np.ascontiguousarray(Wo.T.astype(f32, copy=False))
    wfcT = np.ascontiguousarray(wfc.T.astype(f32, copy=False))
    wpjT = np.ascontiguousarray(wpj.T.astype(f32, copy=False))
    kvw = np.ascontiguousarray(np.asarray(kvaw).reshape(4, 128).T.astype(f32, copy=False))
    cosr = np.ascontiguousarray(np.asarray(cos).astype(f32, copy=False)).reshape(1, DR)
    sinr = np.ascontiguousarray(np.asarray(sin).astype(f32, copy=False)).reshape(1, DR)

    in_maps = []
    for c in range(8):
        beta, r = c // 4, c % 4
        xb = np.asarray(x[beta]).T.astype(f32, copy=False)
        idx = (np.arange(S) // 4) * 4 + (np.arange(S) + r) % 4
        xrot = np.ascontiguousarray(xb[:, idx])
        xown = np.ascontiguousarray(xb[:, r::4])
        ki = np.arange(128)
        A = 4 * (ki // 4) + (ki + r) % 4 - r
        cmeta_v = np.zeros((128, 8), f32)
        cmeta_v[:, 0] = A.astype(f32)
        in_maps.append({
            "xT": xrot, "x_own": xown, "cmeta": cmeta_v,
            "wqn": wqn, "wqp": wqp, "wkc": wkc, "wkp": wkp,
            "wkbk": wkbk, "wkbv": wkbv, "woT": woT,
            "wfcT": wfcT, "wpjT": wpjT, "kvw": kvw,
            "cosr": cosr, "sinr": sinr,
        })
    return in_maps


def kernel(**inputs):
    in_maps = _prep(inputs)
    nc, _ = build()
    res = run_bass_kernel_spmd(nc, in_maps, core_ids=list(range(8)))
    out = np.empty((B, S, HID), np.float32)
    for c in range(8):
        beta, r = c // 4, c % 4
        out[beta, r::4, :] = res.results[c]["outT"].T
    return out


if __name__ == "__main__":
    nc, _ = build()
    print("built:", nc.n_instructions(), "instructions")


# version 12
# speedup vs baseline: 1.4588x; 1.2060x over previous
import numpy as np
from contextlib import ExitStack

import concourse.bass as bass
import concourse.mybir as mybir
from concourse.bass_utils import run_bass_kernel_spmd

F32 = mybir.dt.float32
BF16 = mybir.dt.bfloat16
AF = mybir.ActivationFunctionType
ALU = mybir.AluOpType
AX = mybir.AxisListType

B, S, HID = 2, 4096, 2048
NH, DN, DR, DQ, DV, LORA = 16, 128, 64, 192, 128, 512
SCALE = float(DQ) ** -0.5
EPS_LN = 1e-5
EPS_RMS = 1e-6
HF = 5120
NT = S // 4
NKC = HID // 128
DBG = False


class EW:
    def __init__(self, raw, sem, name, dma=False):
        self.raw = raw
        self.sem = sem
        self.n = 0
        self.name = name
        self.step = 16 if dma else 1
        self.waited = {}

    def wait(self, *toks):
        for t in toks:
            if t is None:
                continue
            sem, n = t
            k = sem.name
            if self.waited.get(k, 0) >= n:
                continue
            self.raw.wait_ge(sem, n)
            self.waited[k] = n

    def done(self, inst):
        self.n += self.step
        inst.then_inc(self.sem, self.step)
        return (self.sem, self.n)

    def tok(self):
        return (self.sem, self.n) if self.n else None


class Em:
    def __init__(self, nc, ctx):
        self.nc = nc
        E = ctx.enter_context
        self.pe = EW(nc.tensor, E(nc.semaphore("s_pe")), "pe")
        self.act = EW(nc.scalar, E(nc.semaphore("s_act")), "act")
        self.dve = EW(nc.vector, E(nc.semaphore("s_dve")), "dve")
        self.gpc = EW(nc.gpsimd, E(nc.semaphore("s_gpc")), "gpc")
        self.qs = EW(nc.sync, E(nc.semaphore("s_qs")), "qs", dma=True)
        self.qg = EW(nc.gpsimd, E(nc.semaphore("s_qg")), "qg", dma=True)
        self.qa = EW(nc.scalar, E(nc.semaphore("s_qa")), "qa", dma=True)
        self.all = [self.pe, self.act, self.dve, self.gpc, self.qs, self.qg, self.qa]

    def barrier(self):
        toks = [e.tok() for e in self.all]
        for e in self.all:
            e.wait(*toks)


def cm(t, c):
    return t[:, :].rearrange("p (c n) -> p c n", c=c)


def build():
    nc = bass.Bass("TRN2", target_bir_lowering=False)

    def din(name, shape, dt=F32):
        return nc.dram_tensor(name, shape, dt, kind="ExternalInput").ap()

    xT = din("xT", [HID, S])
    x_own = din("x_own", [HID, NT])
    cmeta = din("cmeta", [128, 8])
    wqn = din("wqn", [HID, NH * DN])
    wqp = din("wqp", [HID, NH * DR])
    wkc = din("wkc", [HID, LORA])
    wkp = din("wkp", [HID, DR])
    wkbk = din("wkbk", [LORA, NH * DN])
    wkbv = din("wkbv", [LORA, NH * DV])
    woT = din("woT", [NH * DV, HID])
    wfcT = din("wfcT", [HID, HF])
    wpjT = din("wpjT", [HF, HID])
    kvw = din("kvw", [128, 4])
    cosr = din("cosr", [1, DR])
    sinr = din("sinr", [1, DR])

    outT = nc.dram_tensor("outT", [HID, NT], F32, kind="ExternalOutput").ap()

    rms_d = nc.dram_tensor("rms_d", [LORA, S], BF16, kind="Internal").ap()
    ot_d = nc.dram_tensor("ot_d", [NH * DV, NT], BF16, kind="Internal").ap()
    x1_d = nc.dram_tensor("x1_d", [HID, NT], F32, kind="Internal").ap()
    sqq_d = nc.dram_tensor("sqq_d", [NH, NT], F32, kind="Internal").ap()

    dbg = {}
    if DBG:
        def dout(name, shape, dt=F32):
            dbg[name] = nc.dram_tensor(name, shape, dt, kind="ExternalOutput").ap()
            return dbg[name]
        dout("d_ln1own", [HID, NT], BF16)
        dout("d_rms", [LORA, S], BF16)
        dout("d_kpe", [DR, S], BF16)
        dout("d_qn", [NH * DN, NT], BF16)
        dout("d_qp", [8 * 128, NT], BF16)
        dout("d_kn0", [2 * DN, S], BF16)
        dout("d_shift", [1, NH * NT], BF16)
        dout("d_sqq", [NH, NT], F32)
        dout("d_ot", [NH * DV, NT], BF16)
        dout("d_x1", [HID, NT], F32)
        dout("d_am", [1, 4], F32)
        dout("d_ln2", [HID, NT], BF16)
        dout("d_h2", [HF, NT // 2], BF16)
        dout("d_y", [HID, NT // 2], F32)

    ctx = ExitStack()
    with ctx:
        em = Em(nc, ctx)
        pe, act, dve, gpc, qs, qg, qa = em.pe, em.act, em.dve, em.gpc, em.qs, em.qg, em.qa
        E = ctx.enter_context

        ones_b = E(nc.sbuf_tensor("ones_b", [128, 1], BF16))
        ones_s = E(nc.sbuf_tensor("ones_s", [128, 1], BF16))
        ln2n30 = E(nc.sbuf_tensor("ln2n30", [1, 1], F32))
        ones_f = E(nc.sbuf_tensor("ones_f", [1, 128], F32))
        onesc_f = E(nc.sbuf_tensor("onesc_f", [128, 1], F32))
        eps_ln_t = E(nc.sbuf_tensor("eps_ln", [1, 1], F32))
        eps_rms_t = E(nc.sbuf_tensor("eps_rms", [1, 1], F32))
        cmeta_t = E(nc.sbuf_tensor("cmeta_t", [128, 8], F32))
        kvw_t = E(nc.sbuf_tensor("kvw_t", [128, 4], F32))
        cosb = E(nc.sbuf_tensor("cosb", [128, DR], F32))
        spb = E(nc.sbuf_tensor("spb", [128, DR], F32))
        nmax = E(nc.sbuf_tensor("nmax", [1, 4], F32))
        scr = E(nc.sbuf_tensor("scr", [1, 16], F32))
        amv = E(nc.sbuf_tensor("amv", [1, 4], F32))
        thr_fc = E(nc.sbuf_tensor("thr_fc", [128, 2], F32))
        thr_pj = E(nc.sbuf_tensor("thr_pj", [128, 2], F32))
        cy_b = E(nc.sbuf_tensor("cy_b", [128, 1], F32))
        srow = E(nc.sbuf_tensor("srow", [1, 4 * 512], F32))
        SM, SQ, SVAR, SRS = 0, 1, 2, 3

        def row(i, w=512):
            return srow[:, 512 * i:512 * i + w]

        def bc_scalar(dst_col128, src_row1, ps):
            t1 = pe.done(pe.raw.matmul(ps[:, 0:1], ones_f[:, :], src_row1, start=True, stop=True))
            dve.wait(t1)
            return dve.done(dve.raw.tensor_copy(dst_col128, ps[:, 0:1]))

        def ln_stats(src_view, T, chunks, n_dims, eps_t, bfc, sqc, ps1, ps2):
            t_mm = None
            t_slot = [None, None]
            for c in range(chunks):
                sl = c % 2
                dve.wait(t_slot[sl])
                tc = dve.done(dve.raw.tensor_copy(bfc[:, T * sl:T * (sl + 1)],
                                                  src_view[:, c, :]))
                act.wait(t_slot[sl])
                ts = act.done(act.raw.activation(out=sqc[:, T * sl:T * (sl + 1)],
                                                 in_=src_view[:, c, :],
                                                 func=AF.Square, bias=0.0, scale=1.0))
                pe.wait(tc, ts)
                pe.raw.matmul(ps1[:, 0:T], ones_b[:, :], bfc[:, T * sl:T * (sl + 1)],
                              start=(c == 0), stop=(c == chunks - 1))
                mmi = pe.raw.matmul(ps2[:, 0:T], ones_b[:, :], sqc[:, T * sl:T * (sl + 1)],
                                    start=(c == 0), stop=(c == chunks - 1))
                t_mm = pe.done(mmi)
                t_slot[sl] = t_mm
            dve.wait(t_mm)
            t1 = dve.done(dve.raw.tensor_single_scalar(row(SM, T), ps1[:, 0:T],
                                                       1.0 / n_dims, ALU.mult))
            t2 = dve.done(dve.raw.tensor_single_scalar(row(SQ, T), ps2[:, 0:T],
                                                       1.0 / n_dims, ALU.mult))
            dve.wait(t1)
            t3 = dve.done(dve.raw.tensor_tensor(row(SVAR, T), row(SM, T), row(SM, T), ALU.mult))
            dve.wait(t2, t3)
            t4 = dve.done(dve.raw.tensor_tensor(row(SVAR, T), row(SQ, T), row(SVAR, T),
                                                ALU.subtract))
            act.wait(t4)
            t5 = act.done(act.raw.activation(out=row(SRS, T), in_=row(SVAR, T), func=AF.Ln,
                                             bias=eps_t[:, :], scale=1.0))
            act.wait(t5)
            return act.done(act.raw.activation(out=row(SRS, T), in_=row(SRS, T), func=AF.Exp,
                                               bias=0.0, scale=-0.5))

        def ln_norm(src_view, T, chunks, mb_sb, rb_sb, ps_a, ps_b, t_rs, out_view,
                    add_view=None):
            pe.wait(t_rs)
            tbm = pe.done(pe.raw.matmul(ps_a[:, 0:T], ones_f[:, :], row(SM, T),
                                        start=True, stop=True))
            tbr = pe.done(pe.raw.matmul(ps_b[:, 0:T], ones_f[:, :], row(SRS, T),
                                        start=True, stop=True))
            dve.wait(tbm)
            tem = dve.done(dve.raw.tensor_copy(mb_sb[:, 0:T], ps_a[:, 0:T]))
            dve.wait(tbr)
            ter = dve.done(dve.raw.tensor_copy(rb_sb[:, 0:T], ps_b[:, 0:T]))
            mb_b = mb_sb[:, 0:T].rearrange("p (o n) -> p o n", o=1).broadcast_to([128, chunks, T])
            rb_b = rb_sb[:, 0:T].rearrange("p (o n) -> p o n", o=1).broadcast_to([128, chunks, T])
            dve.wait(tem)
            tn1 = dve.done(dve.raw.tensor_tensor(src_view, src_view, mb_b, ALU.subtract))
            dve.wait(tn1, ter)
            if add_view is None:
                tn2 = dve.done(dve.raw.tensor_tensor(out_view, src_view, rb_b, ALU.mult))
            else:
                tn2a = dve.done(dve.raw.tensor_tensor(src_view, src_view, rb_b, ALU.mult))
                dve.wait(tn2a)
                tn2 = dve.done(dve.raw.tensor_tensor(out_view, src_view, add_view, ALU.add))
            return tn2, (tbm, tbr)

        t_cm = qs.done(qs.raw.dma_start(out=cmeta_t[:, :], in_=cmeta))
        qs.done(qs.raw.dma_start(out=kvw_t[:, :], in_=kvw))
        dve.done(dve.raw.memset(ones_b[:, :], 1.0))
        dve.done(dve.raw.memset(ones_s[:, :], float(2.0 ** -30)))
        dve.done(dve.raw.memset(ln2n30[:, :], float(-30.0 * np.log(2.0))))
        t_onef = dve.done(dve.raw.memset(ones_f[:, :], 1.0))
        dve.done(dve.raw.memset(onesc_f[:, :], 1.0))
        dve.done(dve.raw.memset(eps_ln_t[:, :], EPS_LN))
        t_ep2 = dve.done(dve.raw.memset(eps_rms_t[:, :], EPS_RMS))

        with ExitStack() as bd_scope:
            masks = bd_scope.enter_context(nc.sbuf_tensor("masks", [128, 8 * 256], BF16))
            kpefull = bd_scope.enter_context(nc.sbuf_tensor("kpefull", [65, S], BF16))

            with ExitStack() as ph:
                cs_row = ph.enter_context(nc.sbuf_tensor("cs_row", [1, 2 * DR], F32))
                iq = ph.enter_context(nc.sbuf_tensor("iq", [128, 256], F32))
                iqa = ph.enter_context(nc.sbuf_tensor("iqa", [128, 256], F32))
                ps_b0 = ph.enter_context(nc.psum_tensor("ps_b0", [128, 512], F32))
                t_cos = qs.done(qs.raw.dma_start(out=cs_row[:, 0:DR], in_=cosr))
                t_sin = qs.done(qs.raw.dma_start(out=cs_row[:, DR:2 * DR], in_=sinr))
                dve.wait(t_sin)
                t_neg = dve.done(dve.raw.tensor_single_scalar(
                    cs_row[:, DR:DR + 32], cs_row[:, DR:DR + 32], -1.0, ALU.mult))
                pe.wait(t_cos, t_onef)
                t1 = pe.done(pe.raw.matmul(ps_b0[:, 0:DR], ones_f[:, :], cs_row[:, 0:DR],
                                           start=True, stop=True))
                dve.wait(t1)
                t_cb = dve.done(dve.raw.tensor_copy(cosb[:, :], ps_b0[:, 0:DR]))
                pe.wait(t_cb, t_neg)
                t2 = pe.done(pe.raw.matmul(ps_b0[:, 0:DR], ones_f[:, :], cs_row[:, DR:2 * DR],
                                           start=True, stop=True))
                dve.wait(t2)
                t_sb = dve.done(dve.raw.tensor_copy(spb[:, :], ps_b0[:, 0:DR]))
                t_io = gpc.done(gpc.raw.iota(iq[:, :], pattern=[[4, 256]], base=0,
                                             channel_multiplier=0,
                                             allow_small_or_imprecise_dtypes=True))
                dve.wait(t_io, t_cm)
                t_iqa = dve.done(dve.raw.tensor_single_scalar(iqa[:, :], iq[:, :],
                                                              cmeta_t[:, 0:1], ALU.subtract))
                dve.wait(t_iqa)
                for j in range(8):
                    mj = dve.raw.tensor_single_scalar(masks[:, 256 * j:256 * (j + 1)],
                                                      iqa[:, :], float(128 * j), ALU.is_ge)
                dve.done(mj)
                em.barrier()

            NB = 512
            NBLK = S // NB
            with ExitStack() as bc_scope:
                ln1own = bc_scope.enter_context(nc.sbuf_tensor("ln1own", [128, NKC * NT], BF16))

                with ExitStack() as ph:
                    P = ph.enter_context
                    wkc_t = P(nc.sbuf_tensor("wkc_t", [128, NKC * 512], BF16))
                    weffk = P(nc.sbuf_tensor("weffk", [128, NKC * DR], BF16))
                    t_wkc = qg.done(qg.raw.dma_start(
                        out=cm(wkc_t, NKC), in_=wkc.rearrange("(c p) n -> p c n", p=128)))
                    with ExitStack() as ph2:
                        wkpf = ph2.enter_context(nc.sbuf_tensor("wkpf", [128, NKC * DR], F32))
                        wktmp = ph2.enter_context(nc.sbuf_tensor("wktmp", [128, NKC * DR], F32))
                        t_wkp = qs.done(qs.raw.dma_start(
                            out=cm(wkpf, NKC), in_=wkp.rearrange("(c p) n -> p c n", p=128)))
                        dve.wait(t_wkp, t_sb)
                        cb3 = cosb[:, :].rearrange("p (o d) -> p o d", o=1) \
                            .broadcast_to([128, NKC, DR])
                        sb4 = spb[:, :].rearrange("p (o t d) -> p o t d", o=1, t=2) \
                            .broadcast_to([128, NKC, 2, 32])
                        t1 = dve.done(dve.raw.tensor_tensor(cm(wktmp, NKC), cm(wkpf, NKC),
                                                            cb3, ALU.mult))
                        wperm = wkpf[:, :].rearrange("p (c t d) -> p c t d",
                                                     c=NKC, t=2)[:, :, 1::-1, :]
                        weffk4 = weffk[:, :].rearrange("p (c t d) -> p c t d", c=NKC, t=2)
                        t2 = dve.done(dve.raw.tensor_tensor(weffk4, wperm, sb4, ALU.mult))
                        dve.wait(t1, t2)
                        t_wk = dve.done(dve.raw.tensor_tensor(weffk[:, :], weffk[:, :],
                                                              wktmp[:, :], ALU.add))
                    xblk0 = P(nc.sbuf_tensor("xblk0", [128, NKC * NB], F32))
                    xblk1 = P(nc.sbuf_tensor("xblk1", [128, NKC * NB], F32))
                    xbf = P(nc.sbuf_tensor("xbf", [128, NKC * NB], BF16))
                    xsq = P(nc.sbuf_tensor("xsq", [128, NKC * NB], BF16))
                    mb_s = P(nc.sbuf_tensor("mb_s", [128, NB], F32))
                    rb_s = P(nc.sbuf_tensor("rb_s", [128, NB], F32))
                    ln1b = P(nc.sbuf_tensor("ln1b", [128, NKC * NB], BF16))
                    cf32 = P(nc.sbuf_tensor("cf32", [128, 4 * NB], F32))
                    csq = P(nc.sbuf_tensor("csq", [128, 4 * NB], BF16))
                    rr_s = rb_s
                    rmsb = P(nc.sbuf_tensor("rmsb", [128, 4 * NB], BF16))
                    ps_s1 = P(nc.psum_tensor("ps_s1", [1, 512], F32))
                    ps_s2 = P(nc.psum_tensor("ps_s2", [1, 512], F32))
                    ps_c0 = P(nc.psum_tensor("ps_c0", [128, 512], F32))
                    ps_c1 = P(nc.psum_tensor("ps_c1", [128, 512], F32))
                    ps_c2 = P(nc.psum_tensor("ps_c2", [128, 512], F32))
                    ps_c3 = P(nc.psum_tensor("ps_c3", [128, 512], F32))
                    ps_kp = P(nc.psum_tensor("ps_kp", [64, 512], F32))
                    cps = [ps_c0, ps_c1, ps_c2, ps_c3]

                    xb_ = [xblk0, xblk1]
                    t_ck = [t_wk, t_wk]
                    t_scr_free = None
                    t_rmsn = None
                    t_rms_dma = None
                    t_kpe_ev = None
                    t_ps_s_rd = None
                    t_r2_prev = None
                    t_ps_c_rd = None
                    t_xd = [None, None]

                    def x_dma(b):
                        slb = xb_[b % 2]
                        qs.wait(t_ck[b % 2])
                        t_xd[b % 2] = qs.done(qs.raw.dma_start(
                            out=cm(slb, NKC),
                            in_=xT[:, NB * b:NB * (b + 1)].rearrange("(c p) n -> p c n",
                                                                     p=128)))
                    x_dma(0)
                    for b in range(NBLK):
                        sl = xb_[b % 2]
                        slv = cm(sl, NKC)
                        t_x = t_xd[b % 2]
                        if b + 1 < NBLK:
                            x_dma(b + 1)
                        dve.wait(t_x, t_scr_free)
                        tc = dve.done(dve.raw.tensor_copy(xbf[:, :], sl[:, :]))
                        act.wait(t_x, t_scr_free)
                        ts = act.done(act.raw.activation(out=xsq[:, :], in_=sl[:, :],
                                                         func=AF.Square, bias=0.0, scale=1.0))
                        pe.wait(tc, ts, t_ps_s_rd)
                        for c in range(NKC):
                            pe.raw.matmul(ps_s1[:, 0:NB], ones_b[:, :],
                                          xbf[:, NB * c:NB * (c + 1)],
                                          start=(c == 0), stop=(c == NKC - 1))
                            mmi = pe.raw.matmul(ps_s2[:, 0:NB], ones_b[:, :],
                                                xsq[:, NB * c:NB * (c + 1)],
                                                start=(c == 0), stop=(c == NKC - 1))
                        t_mm = pe.done(mmi)
                        dve.wait(t_mm)
                        t1 = dve.done(dve.raw.tensor_single_scalar(row(SM, NB), ps_s1[:, 0:NB],
                                                                   1.0 / HID, ALU.mult))
                        t2 = dve.done(dve.raw.tensor_single_scalar(row(SQ, NB), ps_s2[:, 0:NB],
                                                                   1.0 / HID, ALU.mult))
                        dve.wait(t1)
                        t3 = dve.done(dve.raw.tensor_tensor(row(SVAR, NB), row(SM, NB),
                                                            row(SM, NB), ALU.mult))
                        dve.wait(t2, t3)
                        t4 = dve.done(dve.raw.tensor_tensor(row(SVAR, NB), row(SQ, NB),
                                                            row(SVAR, NB), ALU.subtract))
                        act.wait(t4)
                        t5 = act.done(act.raw.activation(out=row(SRS, NB), in_=row(SVAR, NB),
                                                         func=AF.Ln, bias=eps_ln_t[:, :], scale=1.0))
                        act.wait(t5)
                        t6 = act.done(act.raw.activation(out=row(SRS, NB), in_=row(SRS, NB),
                                                         func=AF.Exp, bias=0.0, scale=-0.5))
                        pe.wait(t6, t_ps_c_rd)
                        tbm = pe.done(pe.raw.matmul(ps_c0[:, 0:NB], ones_f[:, :], row(SM, NB),
                                                    start=True, stop=True))
                        tbr = pe.done(pe.raw.matmul(ps_c1[:, 0:NB], ones_f[:, :], row(SRS, NB),
                                                    start=True, stop=True))
                        dve.wait(tbm)
                        tem = dve.done(dve.raw.tensor_copy(mb_s[:, :], ps_c0[:, 0:NB]))
                        dve.wait(tbr)
                        ter = dve.done(dve.raw.tensor_copy(rb_s[:, :], ps_c1[:, 0:NB]))
                        mb_b = mb_s[:, :].rearrange("p (o n) -> p o n", o=1) \
                            .broadcast_to([128, NKC, NB])
                        rb_b = rb_s[:, :].rearrange("p (o n) -> p o n", o=1) \
                            .broadcast_to([128, NKC, NB])
                        dve.wait(tem)
                        tn1 = dve.done(dve.raw.tensor_tensor(slv, slv, mb_b, ALU.subtract))
                        dve.wait(tn1, ter, t_ck[b % 2])
                        tln = dve.done(dve.raw.tensor_tensor(cm(ln1b, NKC), slv, rb_b, ALU.mult))
                        dve.wait(tln)
                        t_rp = dve.done(dve.raw.tensor_copy(
                            cm(ln1own, NKC)[:, :, (NB // 4) * b:(NB // 4) * (b + 1)],
                            cm(ln1b, NKC)[:, :, 0:NB:4]))
                        pe.wait(tln, tem, ter, t_wkc, t_wk, t_rmsn, t_kpe_ev)
                        for c in range(NKC):
                            for m in range(4):
                                pe.raw.matmul(cps[m][:, 0:NB],
                                              wkc_t[:, 512 * c + 128 * m:512 * c + 128 * (m + 1)],
                                              ln1b[:, NB * c:NB * (c + 1)],
                                              start=(c == 0), stop=(c == NKC - 1))
                            mmi = pe.raw.matmul(ps_kp[:, 0:NB], weffk[:, DR * c:DR * (c + 1)],
                                                ln1b[:, NB * c:NB * (c + 1)],
                                                start=(c == 0), stop=(c == NKC - 1))
                        t_ckv = pe.done(mmi)
                        t_ck[b % 2] = t_ckv
                        dve.wait(t_ckv)
                        for m in range(4):
                            cpi = dve.raw.tensor_copy(cf32[:, NB * m:NB * (m + 1)], cps[m][:, 0:NB])
                        dve.done(cpi)
                        t_cf = dve.done(dve.raw.tensor_copy(kpefull[0:64, NB * b:NB * (b + 1)],
                                                            ps_kp[:, 0:NB]))
                        t_kpe_ev = t_cf
                        act.wait(t_cf, t_r2_prev)
                        t_csq = act.done(act.raw.activation(out=csq[:, :], in_=cf32[:, :],
                                                            func=AF.Square, bias=0.0, scale=1.0))
                        pe.wait(t_csq)
                        for m in range(4):
                            mmi = pe.raw.matmul(ps_s1[:, 0:NB], ones_b[:, :],
                                                csq[:, NB * m:NB * (m + 1)],
                                                start=(m == 0), stop=(m == 3))
                        t_r2 = pe.done(mmi)
                        t_r2_prev = t_r2
                        act.wait(t_r2, t_ep2)
                        t_lr = act.done(act.raw.activation(out=row(SQ, NB), in_=ps_s1[:, 0:NB],
                                                           func=AF.Ln, bias=eps_rms_t[:, :],
                                                           scale=1.0 / LORA))
                        t_ps_s_rd = t_lr
                        act.wait(t_lr)
                        t_er = act.done(act.raw.activation(out=row(SQ, NB), in_=row(SQ, NB),
                                                           func=AF.Exp, bias=0.0, scale=-0.5))
                        pe.wait(t_er)
                        t_br2 = pe.done(pe.raw.matmul(ps_c0[:, 0:NB], ones_f[:, :], row(SQ, NB),
                                                      start=True, stop=True))
                        dve.wait(t_br2)
                        t_err = dve.done(dve.raw.tensor_copy(rr_s[:, :], ps_c0[:, 0:NB]))
                        t_ps_c_rd = t_err
                        dve.wait(t_err, t_rms_dma)
                        for m in range(4):
                            sti = dve.raw.scalar_tensor_tensor(
                                rmsb[:, NB * m:NB * (m + 1)], cf32[:, NB * m:NB * (m + 1)],
                                kvw_t[:, m:m + 1], rr_s[:, :], ALU.mult, ALU.mult)
                        t_rmsn = dve.done(sti)
                        qs.wait(t_rmsn)
                        t_rms_dma = qs.done(qs.raw.dma_start(
                            out=rms_d[:, NB * b:NB * (b + 1)].rearrange("(m p) n -> p m n", p=128),
                            in_=cm(rmsb, 4)))
                    t_kones = dve.done(dve.raw.memset(kpefull[64:65, :], 1.0))
                    em.barrier()
                if DBG:
                    qg.done(qg.raw.dma_start(
                        out=dbg["d_ln1own"].rearrange("(c p) n -> p c n", p=128),
                        in_=cm(ln1own, NKC)))
                    qg.done(qg.raw.dma_start(out=dbg["d_kpe"], in_=kpefull[0:64, :]))
                    t_ = qg.done(qg.raw.dma_start(out=dbg["d_rms"], in_=rms_d))
                    qg.wait(t_)
                    em.barrier()

                qn = bd_scope.enter_context(nc.sbuf_tensor("qn", [128, NH * NT], BF16,
                                                           side="right"))
                qph = bd_scope.enter_context(nc.sbuf_tensor("qph", [65, NH * NT], BF16,
                                                            side="right"))
                with ExitStack() as ph:
                    P = ph.enter_context
                    wtam0 = P(nc.sbuf_tensor("wtam0", [128, 4096], F32))
                    wtam1 = P(nc.sbuf_tensor("wtam1", [128, 4096], F32))
                    racc = P(nc.sbuf_tensor("racc", [128, 2], F32))
                    rtmp = P(nc.sbuf_tensor("rtmp", [128, 1], F32))
                    t_amz = dve.done(dve.raw.memset(racc[:, :], 0.0))
                    am_tiles = []
                    for wi, w_ap in enumerate([wfcT, wpjT]):
                        flat = w_ap.rearrange("a b -> (a b)").rearrange(
                            "(t p n) -> t p n", p=128, n=4096)
                        for ti in range(flat.shape[0]):
                            am_tiles.append((wi, flat[ti]))
                    am_state = {"i": 0, "slot_tok": [None, None], "acc": t_amz}

                    def am_step(k=1):
                        for _ in range(k):
                            i = am_state["i"]
                            if i >= len(am_tiles):
                                return
                            wi, tile = am_tiles[i]
                            sl = i % 2
                            bufw = [wtam0, wtam1][sl]
                            qa.wait(am_state["slot_tok"][sl])
                            td = qa.done(qa.raw.dma_start(out=bufw[:, :], in_=tile))
                            dve.wait(td, am_state["acc"])
                            tr = dve.done(dve.raw.tensor_reduce(
                                rtmp[:, :], bufw[:, :], axis=AX.X, op=ALU.add,
                                apply_absolute_value=True))
                            dve.wait(tr)
                            ta = dve.done(dve.raw.tensor_tensor(
                                racc[:, wi:wi + 1], racc[:, wi:wi + 1], rtmp[:, :], ALU.add))
                            am_state["acc"] = ta
                            am_state["slot_tok"][sl] = ta
                            am_state["i"] = i + 1

                    weffq = P(nc.sbuf_tensor("weffq", [128, 2 * NKC * 128], BF16))
                    wqs0 = P(nc.sbuf_tensor("wqs0", [128, NKC * 128], BF16))
                    wqs1 = P(nc.sbuf_tensor("wqs1", [128, NKC * 128], BF16))
                    wqpf0 = P(nc.sbuf_tensor("wqpf0", [128, NKC * 128], F32))
                    wqpf1 = P(nc.sbuf_tensor("wqpf1", [128, NKC * 128], F32))
                    wqtmp = P(nc.sbuf_tensor("wqtmp", [128, NKC * 128], F32))
                    weffpf = P(nc.sbuf_tensor("weffpf", [128, NKC * 128], F32))
                    ps_q0 = P(nc.psum_tensor("ps_q0", [128, 512], F32))
                    ps_q1 = P(nc.psum_tensor("ps_q1", [128, 512], F32))

                    t_dma = [None, None]
                    t_cons = [None, None]
                    t_weff_cons = None
                    t_weffrd = [None, None]
                    t_qslot = [None, None]

                    def qp_dma(p):
                        buf_ = [wqpf0, wqpf1][p % 2]
                        qs.wait(t_cons[p % 2])
                        t_dma[p % 2] = qs.done(qs.raw.dma_start(
                            out=cm(buf_, NKC),
                            in_=wqp[:, 128 * p:128 * (p + 1)]
                            .rearrange("(c pp) n -> pp c n", pp=128)))
                    qp_dma(0)
                    for p in range(8):
                        am_step(2)
                        buf = [wqpf0, wqpf1][p % 2]
                        t_dcur = t_dma[p % 2]
                        if p + 1 < 8:
                            qp_dma(p + 1)
                        dve.wait(t_dcur, t_weffrd[p % 2])
                        w4 = buf[:, :].rearrange("pp (c h t d) -> pp c h t d", c=NKC, h=2, t=2)
                        cb4 = cosb[:, :].rearrange("pp (o t d) -> pp o t d", o=1, t=2) \
                            .rearrange("pp o t d -> pp o () t d") \
                            .broadcast_to([128, NKC, 2, 2, 32])
                        sb4 = spb[:, :].rearrange("pp (o t d) -> pp o t d", o=1, t=2) \
                            .rearrange("pp o t d -> pp o () t d") \
                            .broadcast_to([128, NKC, 2, 2, 32])
                        ta = dve.done(dve.raw.tensor_tensor(
                            wqtmp[:, :].rearrange("pp (c h t d) -> pp c h t d", c=NKC, h=2, t=2),
                            w4, cb4, ALU.mult))
                        tb = dve.done(dve.raw.tensor_tensor(
                            weffpf[:, :].rearrange("pp (c h t d) -> pp c h t d", c=NKC, h=2, t=2),
                            w4[:, :, :, 1::-1, :], sb4, ALU.mult))
                        t_cons[p % 2] = tb
                        dve.wait(ta, tb)
                        WQW = NKC * 128
                        tcq = dve.done(dve.raw.tensor_tensor(
                            weffq[:, WQW * (p % 2):WQW * (p % 2 + 1)], weffpf[:, :],
                            wqtmp[:, :], ALU.add))
                        for hh in range(2):
                            h = 2 * p + hh
                            for tt in range(2):
                                pst = [ps_q0, ps_q1][tt]
                                pe.wait(tcq, t_qslot[tt])
                                for c in range(NKC):
                                    mmi = pe.raw.matmul(
                                        pst[0:64, :],
                                        weffq[:, WQW * (p % 2) + 128 * c + 64 * hh:
                                              WQW * (p % 2) + 128 * c + 64 * (hh + 1)],
                                        ln1own[:, NT * c + 512 * tt:NT * c + 512 * (tt + 1)],
                                        start=(c == 0), stop=(c == NKC - 1))
                                tq = pe.done(mmi)
                                if hh == 1 and tt == 1:
                                    t_weff_cons = tq
                                    t_weffrd[p % 2] = tq
                                dve.wait(tq)
                                tev = dve.done(dve.raw.tensor_copy(
                                    qph[0:64, NT * h + 512 * tt:NT * h + 512 * (tt + 1)],
                                    pst[0:64, :]))
                                t_qslot[tt] = tev
                    t_dma = [None, None]
                    t_cons = [None, None]

                    def qn_dma(h):
                        buf_ = [wqs0, wqs1][h % 2]
                        qg.wait(t_cons[h % 2])
                        t_dma[h % 2] = qg.done(qg.raw.dma_start(
                            out=cm(buf_, NKC),
                            in_=wqn[:, 128 * h:128 * (h + 1)]
                            .rearrange("(c p) n -> p c n", p=128)))
                    qn_dma(0)
                    for h in range(NH):
                        am_step(2)
                        buf = [wqs0, wqs1][h % 2]
                        t_dcur = t_dma[h % 2]
                        if h + 1 < NH:
                            qn_dma(h + 1)
                        for tt in range(2):
                            pst = [ps_q0, ps_q1][tt]
                            pe.wait(t_dcur, t_qslot[tt])
                            for c in range(NKC):
                                mmi = pe.raw.matmul(
                                    pst[:, :], buf[:, 128 * c:128 * (c + 1)],
                                    ln1own[:, NT * c + 512 * tt:NT * c + 512 * (tt + 1)],
                                    start=(c == 0), stop=(c == NKC - 1))
                            tq = pe.done(mmi)
                            if tt == 1:
                                t_cons[h % 2] = tq
                            dve.wait(tq)
                            tev = dve.done(dve.raw.tensor_copy(
                                qn[:, NT * h + 512 * tt:NT * h + 512 * (tt + 1)], pst[:, :]))
                            t_qslot[tt] = tev
                    with ExitStack() as ph2:
                        sqsc2 = ph2.enter_context(nc.sbuf_tensor("sqsc2", [128, NT], BF16))
                        qrow = ph2.enter_context(nc.sbuf_tensor("qrow", [1, NT], F32))
                        ps_nq = ph2.enter_context(nc.psum_tensor("ps_nq", [1, 512], F32))
                        t_qdma = None
                        t_rd = None
                        t_psrd = None
                        for H in range(NH):
                            dve.wait(t_rd)
                            tq1 = dve.done(dve.raw.tensor_tensor(
                                sqsc2[:, :], qn[:, NT * H:NT * (H + 1)],
                                qn[:, NT * H:NT * (H + 1)], ALU.mult))
                            for tt in range(2):
                                pe.wait(tq1, t_rd, t_psrd)
                                tm1 = pe.done(pe.raw.matmul(
                                    ps_nq[:, :], ones_b[:, :],
                                    sqsc2[:, 512 * tt:512 * (tt + 1)], start=True, stop=True))
                                dve.wait(tm1, t_qdma)
                                te1 = dve.done(dve.raw.tensor_copy(
                                    qrow[:, 512 * tt:512 * (tt + 1)], ps_nq[:, :]))
                                t_psrd = te1
                            tq2 = dve.done(dve.raw.tensor_tensor(
                                sqsc2[0:64, 0:NT], qph[0:64, NT * H:NT * (H + 1)],
                                qph[0:64, NT * H:NT * (H + 1)], ALU.mult))
                            for tt in range(2):
                                pe.wait(tq2, t_psrd)
                                tm3 = pe.done(pe.raw.matmul(
                                    ps_nq[:, :], ones_b[0:64, :],
                                    sqsc2[0:64, 512 * tt:512 * (tt + 1)], start=True, stop=True))
                                t_rd = tm3
                                dve.wait(tm3)
                                te3 = dve.done(dve.raw.tensor_tensor(
                                    qrow[:, 512 * tt:512 * (tt + 1)],
                                    qrow[:, 512 * tt:512 * (tt + 1)], ps_nq[:, :], ALU.add))
                                t_psrd = te3
                            act.wait(te3)
                            ta1 = act.done(act.raw.activation(out=qrow[:, :], in_=qrow[:, :],
                                                              func=AF.Ln, bias=eps_rms_t[:, :],
                                                              scale=1.0))
                            act.wait(ta1)
                            ta2 = act.done(act.raw.activation(out=qrow[:, :], in_=qrow[:, :],
                                                              func=AF.Exp, bias=0.0, scale=0.5))
                            qs.wait(ta2)
                            t_qdma = qs.done(qs.raw.dma_start(out=sqq_d[H:H + 1, :],
                                                              in_=qrow[:, :]))
                            t_rd = ta2

                    em.barrier()
                    ps_am = ph.enter_context(nc.psum_tensor("ps_am", [128, 512], F32))
                    am_step(100)
                    t_z = am_state["acc"]
                    for wi in range(2):
                        pe.wait(t_z)
                        tmm = pe.done(pe.raw.matmul(ps_am[0:1, 0:1], racc[:, wi:wi + 1],
                                                    onesc_f[:, :], start=True, stop=True))
                        dve.wait(tmm)
                        t1 = dve.done(dve.raw.tensor_single_scalar(
                            amv[:, wi:wi + 1], ps_am[0:1, 0:1], 1.0 / (HID * HF), ALU.mult))
                        dve.wait(t1)
                        t2 = dve.done(dve.raw.tensor_single_scalar(
                            amv[:, wi:wi + 1], amv[:, wi:wi + 1], 1e-5, ALU.max))
                        pe.wait(t2)
                    t1 = dve.done(dve.raw.tensor_tensor(amv[:, 2:3], amv[:, 0:1], amv[:, 0:1],
                                                        ALU.mult))
                    dve.wait(t1)
                    t2 = dve.done(dve.raw.tensor_tensor(amv[:, 2:3], amv[:, 2:3], amv[:, 1:2],
                                                        ALU.mult))
                    dve.wait(t2)
                    t3 = dve.done(dve.raw.tensor_single_scalar(amv[:, 2:3], amv[:, 2:3],
                                                               0.125, ALU.mult))
                    for (dst, col, src_, sc) in [(thr_fc, 0, 0, -0.7), (thr_fc, 1, 0, 0.7),
                                                 (thr_pj, 0, 1, -0.7), (thr_pj, 1, 1, 0.7)]:
                        dve.wait(t3)
                        tsc = dve.done(dve.raw.tensor_single_scalar(
                            scr[:, 2:3], amv[:, src_:src_ + 1], sc, ALU.mult))
                        pe.wait(tsc)
                        tmm = pe.done(pe.raw.matmul(ps_am[:, 0:1], ones_f[:, :], scr[:, 2:3],
                                                    start=True, stop=True))
                        dve.wait(tmm)
                        tev = dve.done(dve.raw.tensor_copy(dst[:, col:col + 1], ps_am[:, 0:1]))
                        pe.wait(tev)
                    pe.wait(t3)
                    tmm = pe.done(pe.raw.matmul(ps_am[:, 0:1], ones_f[:, :], amv[:, 2:3],
                                                start=True, stop=True))
                    dve.wait(tmm)
                    tev = dve.done(dve.raw.tensor_copy(cy_b[:, :], ps_am[:, 0:1]))
                    pe.wait(tev)
                em.barrier()
            if DBG:
                qg.done(qg.raw.dma_start(out=dbg["d_qn"].rearrange("(h p) n -> p h n", p=128),
                                          in_=cm(qn, NH)))
                t_ = qg.done(qg.raw.dma_start(
                    out=dbg["d_qp"].rearrange("(h pp) n -> pp h n", pp=64),
                    in_=qph[0:64, :].rearrange("pp (h n) -> pp h n", h=NH)))
                qg.wait(t_)
                em.barrier()

            with ExitStack() as ph:
                P = ph.enter_context
                kq = P(nc.sbuf_tensor("kq", [64, 512], BF16))
                ps_n = P(nc.psum_tensor("ps_n", [1, 512], F32))
                tsg = None
                for sseg in range(8):
                    tsq = dve.done(dve.raw.tensor_tensor(
                        kq[:, :], kpefull[0:64, 512 * sseg:512 * (sseg + 1)],
                        kpefull[0:64, 512 * sseg:512 * (sseg + 1)], ALU.mult))
                    pe.wait(tsq, tsg)
                    t1 = pe.done(pe.raw.matmul(ps_n[:, :], ones_b[0:64, :], kq[:, :],
                                               start=True, stop=True))
                    dve.wait(t1)
                    tsg = dve.done(dve.raw.tensor_reduce(scr[:, 8 + sseg:9 + sseg],
                                                         ps_n[:, :], axis=AX.X, op=ALU.max))
                dve.wait(tsg)
                t1 = dve.done(dve.raw.tensor_tensor(scr[:, 4:5], scr[:, 8:9], scr[:, 9:10],
                                                    ALU.max))
                dve.wait(t1)
                for tp in range(1, 4):
                    ta = dve.done(dve.raw.tensor_tensor(scr[:, 4 + tp:5 + tp],
                                                        scr[:, 3 + tp:4 + tp],
                                                        scr[:, 8 + 2 * tp:9 + 2 * tp], ALU.max))
                    dve.wait(ta)
                    tb = dve.done(dve.raw.tensor_tensor(scr[:, 4 + tp:5 + tp],
                                                        scr[:, 4 + tp:5 + tp],
                                                        scr[:, 9 + 2 * tp:10 + 2 * tp], ALU.max))
                    dve.wait(tb)
            em.barrier()

            em.barrier()
            em.barrier()
            if DBG:
                t_ = qg.done(qg.raw.dma_start(out=dbg["d_am"], in_=amv[0:1, :]))
                qg.wait(t_)

            NG = 8
            with ExitStack() as ph:
                P = ph.enter_context
                kngs = [P(nc.sbuf_tensor("kngA", [128, 2 * S], BF16)),
                        P(nc.sbuf_tensor("kngB", [128, 2 * S], BF16))]
                vbufs = [P(nc.sbuf_tensor("vbufA", [128, 32 * 256], BF16)),
                         P(nc.sbuf_tensor("vbufB", [128, 32 * 256], BF16))]
                wkbks = [P(nc.sbuf_tensor("wkbk0", [128, 4 * 256], BF16)),
                         P(nc.sbuf_tensor("wkbk1", [128, 4 * 256], BF16))]
                wkbvs = [P(nc.sbuf_tensor("wkbv0", [128, 4 * 256], BF16)),
                         P(nc.sbuf_tensor("wkbv1", [128, 4 * 256], BF16))]
                rmst0 = P(nc.sbuf_tensor("rmst0", [128, 4 * 512], BF16))
                rmst1 = P(nc.sbuf_tensor("rmst1", [128, 4 * 512], BF16))
                pts6 = [P(nc.sbuf_tensor(f"pt{i}", [128, 512], BF16)) for i in range(6)]
                ksq = P(nc.sbuf_tensor("ksq", [128, 1024], BF16))
                nqrow = P(nc.sbuf_tensor("nqrow", [1, NT], F32))
                nqrow2 = P(nc.sbuf_tensor("nqrow2", [1, NT], F32))
                sr_at = P(nc.sbuf_tensor("sr_at", [1, 1024], F32))
                rb2_s = P(nc.sbuf_tensor("rb2_s", [128, 512], F32))
                onrm0 = P(nc.sbuf_tensor("onrm0", [128, 256], BF16))
                onrm1 = P(nc.sbuf_tensor("onrm1", [128, 256], BF16))
                ps_SA = P(nc.psum_tensor("ps_SA", [128, 512], F32))
                ps_SB = P(nc.psum_tensor("ps_SB", [128, 512], F32))
                ps_SC = P(nc.psum_tensor("ps_SC", [128, 512], F32))
                ps_o0 = P(nc.psum_tensor("ps_o0", [128, 256], F32))
                ps_o1 = P(nc.psum_tensor("ps_o1", [128, 256], F32))
                ps_sm0 = P(nc.psum_tensor("ps_sm0", [1, 256], F32))
                ps_sm1 = P(nc.psum_tensor("ps_sm1", [1, 256], F32))
                ps_kv = P(nc.psum_tensor("ps_kv", [128, 512], F32))

                Sbanks = [ps_SA, ps_SB, ps_SC]
                ps_os = [ps_o0, ps_o1]
                ps_sms = [ps_sm0, ps_sm1]
                onrms = [onrm0, onrm1]
                bank = {"tok": None}
                t_rfree = [None, None]
                t_wkfree = [None, None]
                kv_ready = {}
                shift_ready = {}
                t_shift_dma = {}
                t_sm_free = [None, None]
                t_o_free = [None, None]
                t_pt_free = [None] * 6
                t_Sfree = [None, None, None]
                t_ot_dma = [None, None]
                state = {"pti": 0, "sti": 0, "epar": 0, "epi": None,
                         "ksq2_rd": None, "nqrow_rd": None}

                def kvb_closures(gg):
                    t_rdma = [None, None]
                    kb_ = kngs[gg % 2]
                    vb_ = vbufs[gg % 2]
                    wk_ = wkbks[gg % 2]
                    wv_ = wkbvs[gg % 2]
                    cls = []

                    def dma_w():
                        qg.wait(t_wkfree[gg % 2])
                        t1 = qg.done(qg.raw.dma_start(
                            out=cm(wk_, 4),
                            in_=wkbk[:, 256 * gg:256 * (gg + 1)]
                            .rearrange("(c p) n -> p c n", p=128)))
                        t2 = qg.done(qg.raw.dma_start(
                            out=cm(wv_, 4),
                            in_=wkbv[:, 256 * gg:256 * (gg + 1)]
                            .rearrange("(c p) n -> p c n", p=128)))
                        kv_ready.setdefault(gg, []).extend([t1, t2])
                    cls.append(dma_w)

                    def rms_dma(s):
                        bufr_ = [rmst0, rmst1][s % 2]
                        qs.wait(t_rfree[s % 2])
                        t_rdma[s % 2] = qs.done(qs.raw.dma_start(
                            out=cm(bufr_, 4),
                            in_=rms_d[:, 512 * s:512 * (s + 1)]
                            .rearrange("(c p) n -> p c n", p=128)))

                    def mk_stile(s):
                        def fn():
                            bufr = [rmst0, rmst1][s % 2]
                            if s == 0:
                                rms_dma(0)
                            t_r = t_rdma[s % 2]
                            if s + 1 < 8:
                                rms_dma(s + 1)
                            tw = kv_ready[gg][:2]
                            for h2 in range(2):
                                pe.wait(t_r, bank["tok"], *tw)
                                for c in range(4):
                                    mmi = pe.raw.matmul(
                                        ps_kv[:, :],
                                        cm(wk_, 4)[:, c, 128 * h2:128 * (h2 + 1)],
                                        bufr[:, 512 * c:512 * (c + 1)],
                                        start=(c == 0), stop=(c == 3))
                                tkn = pe.done(mmi)
                                dve.wait(tkn)
                                tev = dve.done(dve.raw.tensor_copy(
                                    kb_[:, S * h2 + 512 * s:S * h2 + 512 * (s + 1)],
                                    ps_kv[:, :]))
                                bank["tok"] = tev
                            for half in range(2):
                                pe.wait(bank["tok"])
                                for i in range(2):
                                    sub = 2 * half + i
                                    for c in range(4):
                                        mmi = pe.raw.matmul(
                                            ps_kv[:, 256 * i:256 * (i + 1)],
                                            bufr[:, 512 * c + 128 * sub:512 * c + 128 * (sub + 1)],
                                            cm(wv_, 4)[:, c, :],
                                            start=(c == 0 and i == 0), stop=(c == 3 and i == 1))
                                tkn = pe.done(mmi)
                                dve.wait(tkn)
                                tev = dve.done(dve.raw.tensor_copy(
                                    vb_[:, 256 * (4 * s + 2 * half):256 * (4 * s + 2 * half + 2)],
                                    ps_kv[:, :]))
                                bank["tok"] = tev
                            t_rfree[s % 2] = bank["tok"]
                            kv_ready.setdefault(gg, []).append(bank["tok"])
                        return fn
                    for s in range(8):
                        cls.append(mk_stile(s))
                    return cls

                def norm_closures(gg):
                    kb_ = kngs[gg % 2]
                    cls = []

                    def mk_seg(sseg):
                        def fn():
                            tzk = None
                            for h2 in range(2):
                                ksl = h2
                                dve.wait(state.get(("ksqrd", ksl)))
                                tsq = dve.done(dve.raw.tensor_tensor(
                                    ksq[:, 512 * ksl:512 * (ksl + 1)],
                                    kb_[:, S * h2 + 512 * sseg:S * h2 + 512 * (sseg + 1)],
                                    kb_[:, S * h2 + 512 * sseg:S * h2 + 512 * (sseg + 1)],
                                    ALU.mult))
                                pe.wait(tsq, bank["tok"])
                                t1 = pe.done(pe.raw.matmul(ps_kv[0:1, 0:512], ones_b[:, :],
                                                           ksq[:, 512 * ksl:512 * (ksl + 1)],
                                                           start=True, stop=True))
                                state[("ksqrd", ksl)] = t1
                                dve.wait(t1)
                                tm = dve.done(dve.raw.tensor_reduce(
                                    scr[:, 1:2], ps_kv[0:1, 0:512], axis=AX.X, op=ALU.max))
                                bank["tok"] = tm
                                dve.wait(tm)
                                tzk = dve.done(dve.raw.tensor_tensor(
                                    scr[:, 8 + sseg:9 + sseg], scr[:, 8 + sseg:9 + sseg],
                                    scr[:, 1:2], ALU.max))
                            shift_ready[(gg, "seg", sseg)] = tzk
                        return fn

                    def seg_init():
                        t0 = dve.done(dve.raw.memset(scr[:, 8:16], 0.0))
                        shift_ready[(gg, "seginit")] = t0
                    cls.append(seg_init)
                    for sseg in range(8):
                        cls.append(mk_seg(sseg))

                    def prefixes():
                        tpv = dve.done(dve.raw.tensor_tensor(scr[:, 0:1], scr[:, 8:9],
                                                             scr[:, 9:10], ALU.max))
                        for tp in range(1, 4):
                            dve.wait(tpv)
                            ta = dve.done(dve.raw.tensor_tensor(
                                scr[:, tp:tp + 1], scr[:, tp - 1:tp],
                                scr[:, 8 + 2 * tp:9 + 2 * tp], ALU.max))
                            dve.wait(ta)
                            tpv = dve.done(dve.raw.tensor_tensor(
                                scr[:, tp:tp + 1], scr[:, tp:tp + 1],
                                scr[:, 9 + 2 * tp:10 + 2 * tp], ALU.max))
                        dve.wait(tpv)
                        tkk = dve.done(dve.raw.tensor_tensor(scr[:, 0:4], scr[:, 0:4],
                                                             scr[:, 4:8], ALU.add))
                        act.wait(tkk)
                        t1 = act.done(act.raw.activation(out=scr[:, 8:12], in_=scr[:, 0:4],
                                                         func=AF.Ln, bias=eps_rms_t[:, :],
                                                         scale=1.0))
                        act.wait(t1)
                        t2 = act.done(act.raw.activation(out=scr[:, 8:12], in_=scr[:, 8:12],
                                                         func=AF.Exp, bias=0.0, scale=0.5))
                        dve.wait(t2)
                        tnk = dve.done(dve.raw.tensor_single_scalar(scr[:, 8:12], scr[:, 8:12],
                                                                    -1.0, ALU.mult))
                        shift_ready[(gg, "nk")] = tnk
                    cls.append(prefixes)

                    def mk_shift(h2):
                        def fn():
                            H = 2 * gg + h2
                            qs.wait(state["nqrow_rd"])
                            t_qld = qs.done(qs.raw.dma_start(out=nqrow[:, :],
                                                             in_=sqq_d[H:H + 1, :]))
                            dve.wait(t_qld, shift_ready[(gg, "nk")], t_shift_dma.get(H - 1),
                                     state["nqrow_rd"])
                            tsl = None
                            for tp in range(4):
                                tsl = dve.done(dve.raw.tensor_scalar(
                                    out=nqrow2[:, 256 * tp:256 * (tp + 1)],
                                    in0=nqrow[:, 256 * tp:256 * (tp + 1)],
                                    scalar1=scr[:, 8 + tp:9 + tp], scalar2=float(50.0 / SCALE),
                                    op0=ALU.mult, op1=ALU.add))
                            state["nqrow_rd"] = tsl
                            qg.wait(tsl)
                            t_shift_dma[H] = qg.done(qg.raw.dma_start(
                                out=qph[64:65, NT * H:NT * (H + 1)], in_=nqrow2[:, :]))
                        return fn
                    cls.append(mk_shift(0))
                    cls.append(mk_shift(1))
                    return cls

                def emit_spv(pend, par, nkb, Hloc, gg, first):
                    slot, st0, t_e, t_m = pend
                    vb_ = vbufs[gg % 2]
                    pe.wait(t_e, t_m)
                    if first:
                        pe.wait(t_sm_free[par], t_o_free[par])
                    for i in range(2):
                        kb = 2 * st0 + i
                        pe.raw.matmul(ps_sms[par][:, :], ones_s[:, :],
                                      pts6[slot][:, 256 * i:256 * (i + 1)],
                                      start=(kb == 0), stop=(kb == nkb - 1))
                        mmi = pe.raw.matmul(
                            ps_os[par][:, :],
                            vb_[:, 256 * kb + 128 * Hloc:256 * kb + 128 * (Hloc + 1)],
                            pts6[slot][:, 256 * i:256 * (i + 1)],
                            start=(kb == 0), stop=(kb == nkb - 1))
                    t_pv = pe.done(mmi)
                    t_pt_free[slot] = t_pv
                    return t_pv

                def flush_epilogue(e):
                    par = e["par"]
                    pe.wait(e["t_r"], bank["tok"])
                    t_bb = pe.done(pe.raw.matmul(ps_kv[:, 0:256], ones_f[:, :],
                                                 sr_at[:, 512 + 256 * par:768 + 256 * par],
                                                 start=True, stop=True))
                    dve.wait(t_bb)
                    t_rbe = dve.done(dve.raw.tensor_copy(
                        rb2_s[:, 256 * par:256 * (par + 1)], ps_kv[:, 0:256]))
                    bank["tok"] = t_rbe
                    dve.wait(t_rbe, t_ot_dma[par])
                    t_on = dve.done(dve.raw.tensor_tensor(
                        onrms[par][:, :], ps_os[par][:, :],
                        rb2_s[:, 256 * par:256 * (par + 1)], ALU.mult))
                    t_o_free[par] = t_on
                    qg.wait(t_on)
                    t_ot_dma[par] = qg.done(qg.raw.dma_start(
                        out=ot_d[128 * e["H"]:128 * (e["H"] + 1),
                                 256 * e["tp"]:256 * (e["tp"] + 1)],
                        in_=onrms[par][:, :]))

                def emit_attention(gg, inject):
                    kb_ = kngs[gg % 2]
                    inj = list(inject)
                    ii = 0

                    def do_inject():
                        nonlocal ii
                        if ii < len(inj):
                            inj[ii]()
                            ii += 1
                    for tp in range(4):
                        nkb = 8 * (tp + 1)
                        for h2 in range(2):
                            H = 2 * gg + h2
                            qrhs_n = qn[:, NT * H + 256 * tp:NT * H + 256 * (tp + 1)]
                            qrhs_p = qph[0:65, NT * H + 256 * tp:NT * H + 256 * (tp + 1)]
                            pe.wait(t_shift_dma[H], *kv_ready[gg])
                            ntile = nkb // 2
                            pq = []
                            par = state["epar"] % 2
                            first_spv = True
                            for st in range(ntile):
                                sb = state["sti"] % 3
                                state["sti"] += 1
                                Sps = Sbanks[sb]
                                pe.wait(t_Sfree[sb])
                                for i in range(2):
                                    kb = 2 * st + i
                                    colo = 256 * i
                                    pe.raw.matmul(
                                        Sps[:, colo:colo + 256],
                                        kb_[:, S * h2 + 128 * kb:S * h2 + 128 * (kb + 1)],
                                        qrhs_n, start=(i == 0), stop=False)
                                    mmi = pe.raw.matmul(
                                        Sps[:, colo:colo + 256],
                                        kpefull[0:65, 128 * kb:128 * (kb + 1)],
                                        qrhs_p, start=False, stop=(i == 1))
                                t_S = pe.done(mmi)
                                if len(pq) >= 2:
                                    emit_spv(pq.pop(0), par, nkb, h2, gg, first_spv)
                                    first_spv = False
                                if st == 2 and state["epi"] is not None:
                                    flush_epilogue(state["epi"])
                                    state["epi"] = None
                                slot = state["pti"] % 6
                                state["pti"] += 1
                                act.wait(t_S, t_pt_free[slot])
                                t_e = act.done(act.raw.activation(out=pts6[slot][:, :],
                                                                  in_=Sps[:, :], func=AF.Exp,
                                                                  bias=0.0, scale=SCALE))
                                t_Sfree[sb] = t_e
                                t_m = None
                                for i in range(2):
                                    kb = 2 * st + i
                                    j = kb - (nkb - 8)
                                    if j >= 0:
                                        dve.wait(t_e)
                                        t_m = dve.done(dve.raw.tensor_tensor(
                                            pts6[slot][:, 256 * i:256 * (i + 1)],
                                            pts6[slot][:, 256 * i:256 * (i + 1)],
                                            masks[:, 256 * j:256 * (j + 1)], ALU.mult))
                                pq.append((slot, st, t_e, t_m))
                            for pend in pq:
                                t_pv = emit_spv(pend, par, nkb, h2, gg, first_spv)
                                first_spv = False
                            pq = []
                            dve.wait(t_pv)
                            t_sev = dve.done(dve.raw.tensor_copy(
                                sr_at[:, 256 * par:256 * (par + 1)], ps_sms[par][:, :]))
                            t_sm_free[par] = t_sev
                            act.wait(t_sev)
                            t_l = act.done(act.raw.activation(
                                out=sr_at[:, 512 + 256 * par:768 + 256 * par],
                                in_=sr_at[:, 256 * par:256 * (par + 1)], func=AF.Ln,
                                bias=0.0, scale=1.0))
                            act.wait(t_l)
                            t_r = act.done(act.raw.activation(
                                out=sr_at[:, 512 + 256 * par:768 + 256 * par],
                                in_=sr_at[:, 512 + 256 * par:768 + 256 * par], func=AF.Exp,
                                bias=ln2n30[:, :], scale=-1.0))
                            if state["epi"] is not None:
                                flush_epilogue(state["epi"])
                            state["epi"] = dict(par=par, t_r=t_r, H=H, tp=tp)
                            state["epar"] += 1
                            do_inject()
                            do_inject()
                            do_inject()
                    while ii < len(inj):
                        do_inject()

                for fn in kvb_closures(0) + norm_closures(0):
                    fn()
                for gg in range(NG):
                    nxt = kvb_closures(gg + 1) + norm_closures(gg + 1) if gg + 1 < NG else []
                    emit_attention(gg, nxt)
                if state["epi"] is not None:
                    flush_epilogue(state["epi"])
                    state["epi"] = None
                em.barrier()
            em.barrier()
        if DBG:
            qg.done(qg.raw.dma_start(out=dbg["d_shift"], in_=qph[64:65, :]))
            t_ = qg.done(qg.raw.dma_start(out=dbg["d_sqq"], in_=sqq_d))
            qg.wait(t_)
            em.barrier()
        if DBG:
            t_ = qg.done(qg.raw.dma_start(out=dbg["d_ot"], in_=ot_d))
            qg.wait(t_)
            em.barrier()

        NTH = NT // 2
        eg_scope = ctx.enter_context(ExitStack())
        ln2 = eg_scope.enter_context(nc.sbuf_tensor("ln2", [128, NKC * NT], BF16))
        with ExitStack() as ph:
            P = ph.enter_context
            ots = P(nc.sbuf_tensor("ots", [128, NKC * NT], BF16))
            attn = P(nc.sbuf_tensor("attn", [128, NKC * NT], F32))
            xow = P(nc.sbuf_tensor("xow", [128, NKC * NTH], F32))
            wos0 = P(nc.sbuf_tensor("wos0", [128, NKC * 128], BF16))
            wos1 = P(nc.sbuf_tensor("wos1", [128, NKC * 128], BF16))
            bfc = P(nc.sbuf_tensor("bfc", [128, 1024], BF16))
            sqc = P(nc.sbuf_tensor("sqc", [128, 1024], BF16))
            mb2 = P(nc.sbuf_tensor("mb2", [128, 512], F32))
            rb3 = P(nc.sbuf_tensor("rb3", [128, 512], F32))
            ps_eb = [P(nc.psum_tensor(f"ps_e{i}", [128, 512], F32)) for i in range(4)]
            ps_e0, ps_e1 = ps_eb[0], ps_eb[1]
            ps_f1 = P(nc.psum_tensor("ps_f1", [1, 512], F32))
            ps_f2 = P(nc.psum_tensor("ps_f2", [1, 512], F32))
            t_ot = qs.done(qs.raw.dma_start(out=cm(ots, NKC),
                                            in_=ot_d.rearrange("(c p) n -> p c n", p=128)))
            t_x1dma = None
            t_wdma = [None, None]
            t_wfree = [None, None]
            t_eslot = [None, None, None, None]

            def wo_dma(m):
                bw = [wos0, wos1][m % 2]
                qq_ = qg
                qq_.wait(t_wfree[m % 2])
                t_wdma[m % 2] = qq_.done(qq_.raw.dma_start(
                    out=cm(bw, NKC),
                    in_=woT[:, 128 * m:128 * (m + 1)].rearrange("(c p) n -> p c n",
                                                                p=128)))
            wo_dma(0)
            for m in range(NKC):
                bw = [wos0, wos1][m % 2]
                t_dcur = t_wdma[m % 2]
                if m + 1 < NKC:
                    wo_dma(m + 1)
                for hfi in range(2):
                    bk = 2 * (m % 2) + hfi
                    pst = ps_eb[bk]
                    pe.wait(t_dcur, t_ot, t_eslot[bk])
                    for c in range(NKC):
                        mmi = pe.raw.matmul(pst[:, :], bw[:, 128 * c:128 * (c + 1)],
                                            ots[:, NT * c + NTH * hfi:NT * c + NTH * (hfi + 1)],
                                            start=(c == 0), stop=(c == NKC - 1))
                    tq = pe.done(mmi)
                    dve.wait(tq)
                    tev = dve.done(dve.raw.tensor_copy(
                        attn[:, NT * m + NTH * hfi:NT * m + NTH * (hfi + 1)], pst[:, :]))
                    t_eslot[bk] = tev
                t_wfree[m % 2] = tq
            em.barrier()
            for hf in range(2):
                t_xo = qs.done(qs.raw.dma_start(
                    out=cm(xow, NKC),
                    in_=x_own[:, NTH * hf:NTH * (hf + 1)].rearrange("(c p) n -> p c n", p=128)))
                av = cm(attn, NKC)[:, :, NTH * hf:NTH * (hf + 1)]
                t_rs = ln_stats(av, NTH, NKC, HID, eps_ln_t, bfc, sqc, ps_f1, ps_f2)
                dve.wait(t_xo)
                tn, _ = ln_norm(av, NTH, NKC, mb2, rb3, ps_e0, ps_e1, t_rs,
                                out_view=cm(xow, NKC), add_view=cm(xow, NKC))
                qs.wait(tn)
                t_x1dma = qs.done(qs.raw.dma_start(
                    out=x1_d[:, NTH * hf:NTH * (hf + 1)].rearrange("(c p) n -> p c n", p=128),
                    in_=cm(xow, NKC)))
                t_rs2 = ln_stats(cm(xow, NKC), NTH, NKC, HID, eps_ln_t, bfc, sqc, ps_f1, ps_f2)
                dve.wait(t_x1dma)
                tn2, _ = ln_norm(cm(xow, NKC), NTH, NKC, mb2, rb3, ps_e0, ps_e1, t_rs2,
                                 out_view=cm(ln2, NKC)[:, :, NTH * hf:NTH * (hf + 1)])
                em.barrier()
        if DBG:
            qg.done(qg.raw.dma_start(out=dbg["d_x1"].rearrange("(c p) n -> p c n", p=128),
                                      in_=x1_d.rearrange("(c p) n -> p c n", p=128)))
            t_ = qg.done(qg.raw.dma_start(
                out=dbg["d_ln2"].rearrange("(c p) n -> p c n", p=128), in_=cm(ln2, NKC)))
            qg.wait(t_)
            em.barrier()

        NTH = NT // 2
        t_out_final = None
        with ExitStack() as ph:
            P = ph.enter_context
            hfp = P(nc.sbuf_tensor("hfp", [128, NTH], F32))
            ps_gb = [P(nc.psum_tensor(f"ps_g{i}", [128, 512], F32)) for i in range(4)]
            ps_h1 = P(nc.psum_tensor("ps_h1", [1, 512], F32))
            ps_h2 = P(nc.psum_tensor("ps_h2", [1, 512], F32))
            ph2 = ph.enter_context(ExitStack())
            h2 = ph2.enter_context(nc.sbuf_tensor("h2", [128, 40 * NT], BF16))
            with ExitStack() as phf:
                Pf = phf.enter_context
                ws0 = Pf(nc.sbuf_tensor("ws0", [128, NKC * 128], F32))
                ws1 = Pf(nc.sbuf_tensor("ws1", [128, NKC * 128], F32))
                ta1 = Pf(nc.sbuf_tensor("ta1", [128, NKC * 128], BF16))
                tern = Pf(nc.sbuf_tensor("tern", [128, 2 * NKC * 128], BF16))
                t_wfree = [None, None]
                t_fctt = None
                t_gslot = [None, None, None, None]
                t_wdma_fc = [None, None]
                t_terns = [None, None]

                def fc_dma(m):
                    bw = [ws0, ws1][m % 2]
                    qq_ = [qs, qa][m % 2]
                    qq_.wait(t_wfree[m % 2])
                    t_wdma_fc[m % 2] = qq_.done(qq_.raw.dma_start(
                        out=cm(bw, NKC),
                        in_=wfcT[:, 128 * m:128 * (m + 1)].rearrange("(c p) n -> p c n",
                                                                     p=128)))
                fc_dma(0)
                for m in range(40):
                    bw = [ws0, ws1][m % 2]
                    t_d = t_wdma_fc[m % 2]
                    if m + 1 < 40:
                        fc_dma(m + 1)
                    TW = NKC * 128
                    tsl_ = m % 2
                    act.wait(t_d, t_terns[tsl_], t_fctt)
                    tsa = act.done(act.raw.activation(out=ta1[:, :],
                                                      in_=bw[:, :], func=AF.Sign,
                                                      bias=thr_fc[:, 0:1], scale=1.0))
                    tsb = act.done(act.raw.activation(out=tern[:, TW * tsl_:TW * (tsl_ + 1)],
                                                      in_=bw[:, :], func=AF.Sign,
                                                      bias=thr_fc[:, 1:2], scale=1.0))
                    t_wfree[m % 2] = tsb
                    dve.wait(tsa, tsb)
                    tt_ = dve.done(dve.raw.tensor_tensor(
                        tern[:, TW * tsl_:TW * (tsl_ + 1)],
                        tern[:, TW * tsl_:TW * (tsl_ + 1)],
                        ta1[:, :], ALU.add))
                    t_fctt = tt_
                    for hfi in range(2):
                        bk = 2 * (m % 2) + hfi
                        pst = ps_gb[bk]
                        pe.wait(tt_, t_gslot[bk])
                        for c in range(NKC):
                            mmi = pe.raw.matmul(
                                pst[:, 0:NTH],
                                tern[:, TW * tsl_ + 128 * c:TW * tsl_ + 128 * (c + 1)],
                                ln2[:, NT * c + NTH * hfi:NT * c + NTH * (hfi + 1)],
                                start=(c == 0), stop=(c == NKC - 1))
                        th = pe.done(mmi)
                        dve.wait(th)
                        tcp = dve.done(dve.raw.tensor_copy(hfp[:, :], pst[:, 0:NTH]))
                        t_gslot[bk] = tcp
                        dve.wait(tcp)
                        dve.done(dve.raw.scalar_tensor_tensor(
                            h2[:, NT * m + NTH * hfi:NT * m + NTH * (hfi + 1)],
                            hfp[:, :], 0.0, hfp[:, :], ALU.max, ALU.mult))
                    t_terns[tsl_] = th
                em.barrier()
            yb = P(nc.sbuf_tensor("yb", [128, NKC * NT], F32, side="right"))
            with ExitStack() as php:
                Pp = php.enter_context
                wp0 = Pp(nc.sbuf_tensor("wp0", [128, 10 * 128], F32))
                wp1 = Pp(nc.sbuf_tensor("wp1", [128, 10 * 128], F32))
                tpa1 = Pp(nc.sbuf_tensor("tpa1", [128, 10 * 128], BF16))
                ternp = Pp(nc.sbuf_tensor("ternp", [128, 2 * 10 * 128], BF16))
                t_wfree = [None, None]
                t_yslot = [None, None, None, None]
                t_pjtt = None
                t_wdma_pj = [None, None]
                t_ternp = [None, None]
                pj_seq = [(o, kh) for o in range(NKC) for kh in range(4)]

                def pj_dma(i):
                    o_, kh_ = pj_seq[i]
                    bw = [wp0, wp1][i % 2]
                    qq_ = [qs, qa][i % 2]
                    qq_.wait(t_wfree[i % 2])
                    t_wdma_pj[i % 2] = qq_.done(qq_.raw.dma_start(
                        out=cm(bw, 10),
                        in_=wpjT[1280 * kh_:1280 * (kh_ + 1), 128 * o_:128 * (o_ + 1)]
                        .rearrange("(c p) n -> p c n", p=128)))
                pj_dma(0)
                pji = 0
                for o in range(NKC):
                    for kh in range(4):
                        bw = [wp0, wp1][pji % 2]
                        t_d = t_wdma_pj[pji % 2]
                        if pji + 1 < len(pj_seq):
                            pj_dma(pji + 1)
                        pji += 1
                        PW = 10 * 128
                        psl = (pji - 1) % 2
                        act.wait(t_d, t_ternp[psl], t_pjtt)
                        tsa = act.done(act.raw.activation(out=tpa1[:, :],
                                                          in_=bw[:, :],
                                                          func=AF.Sign, bias=thr_pj[:, 0:1],
                                                          scale=1.0))
                        tsb = act.done(act.raw.activation(out=ternp[:, PW * psl:PW * (psl + 1)],
                                                          in_=bw[:, :],
                                                          func=AF.Sign, bias=thr_pj[:, 1:2],
                                                          scale=1.0))
                        t_wfree[psl] = tsb
                        dve.wait(tsa, tsb)
                        tt_ = dve.done(dve.raw.tensor_tensor(
                            ternp[:, PW * psl:PW * (psl + 1)],
                            ternp[:, PW * psl:PW * (psl + 1)],
                            tpa1[:, :], ALU.add))
                        t_pjtt = tt_
                        for hfi in range(2):
                            bk = 2 * (o % 2) + hfi
                            pst = ps_gb[bk]
                            pe.wait(tt_)
                            if kh == 0:
                                pe.wait(t_yslot[bk])
                            for c in range(10):
                                cc = 10 * kh + c
                                mmi = pe.raw.matmul(
                                    pst[:, 0:NTH],
                                    ternp[:, PW * psl + 128 * c:PW * psl + 128 * (c + 1)],
                                    h2[:, NT * cc + NTH * hfi:NT * cc + NTH * (hfi + 1)],
                                    start=(cc == 0), stop=(cc == 39))
                            ty = pe.done(mmi)
                        t_ternp[psl] = ty
                    for hfi in range(2):
                        bk = 2 * (o % 2) + hfi
                        dve.wait(ty)
                        t_yev = dve.done(dve.raw.tensor_single_scalar(
                            cm(yb, NKC)[:, o, NTH * hfi:NTH * (hfi + 1)],
                            ps_gb[bk][:, 0:NTH], cy_b[:, 0:1], ALU.mult))
                        t_yslot[bk] = t_yev
                em.barrier()
            if DBG:
                qg.done(qg.raw.dma_start(
                    out=dbg["d_h2"].rearrange("(c p) n -> p c n", p=128),
                    in_=cm(h2, 40)[:, :, 0:NTH]))
                t_ = qg.done(qg.raw.dma_start(
                    out=dbg["d_y"].rearrange("(c p) n -> p c n", p=128),
                    in_=cm(yb, NKC)[:, :, 0:NTH]))
                qg.wait(t_)
                em.barrier()
            ph2.close()
            with ExitStack() as phe:
                Pe = phe.enter_context
                x1h = Pe(nc.sbuf_tensor("x1h", [128, NKC * NTH], F32))
                bfc2 = Pe(nc.sbuf_tensor("bfc2", [128, 1024], BF16))
                sqc2 = Pe(nc.sbuf_tensor("sqc2", [128, 1024], BF16))
                mb4 = Pe(nc.sbuf_tensor("mb4", [128, 512], F32))
                rb4 = Pe(nc.sbuf_tensor("rb4", [128, 512], F32))
                for hf in range(2):
                    t_x1 = qs.done(qs.raw.dma_start(
                        out=cm(x1h, NKC),
                        in_=x1_d[:, NTH * hf:NTH * (hf + 1)].rearrange("(c p) n -> p c n",
                                                                       p=128)))
                    yv = cm(yb, NKC)[:, :, NTH * hf:NTH * (hf + 1)]
                    t_rs = ln_stats(yv, NTH, NKC, HID, eps_ln_t, bfc2, sqc2, ps_h1, ps_h2)
                    dve.wait(t_x1)
                    tn, _ = ln_norm(yv, NTH, NKC, mb4, rb4, ps_gb[0], ps_gb[1], t_rs,
                                    out_view=cm(x1h, NKC), add_view=cm(x1h, NKC))
                    qs.wait(tn)
                    t_out_final = qs.done(qs.raw.dma_start(
                        out=outT[:, NTH * hf:NTH * (hf + 1)].rearrange("(c p) n -> p c n",
                                                                       p=128),
                        in_=cm(x1h, NKC)))
                    em.barrier()
        qs.wait(t_out_final)
        em.barrier()
    return nc, dbg


def _prep(inputs):
    x = np.ascontiguousarray(inputs["L0_attn_hidden_states"])
    Wq = inputs["L0_attn_Wq"]
    Wkva = inputs["L0_attn_Wkva"]
    kvaw = inputs["L0_attn_kv_a_norm_weight"]
    Wkvb = inputs["L0_attn_Wkvb"]
    Wo = inputs["L0_attn_Wo"]
    cos = inputs["L0_attn_cos"]
    sin = inputs["L0_attn_sin"]
    wfc = inputs["L0_ffn_w_fc"]
    wpj = inputs["L0_ffn_w_proj"]

    f32 = np.float32
    WqT = np.ascontiguousarray(Wq.T.astype(f32, copy=False))
    wqn = np.ascontiguousarray(WqT.reshape(HID, NH, DQ)[:, :, :DN].reshape(HID, NH * DN))
    wqp = np.ascontiguousarray(WqT.reshape(HID, NH, DQ)[:, :, DN:].reshape(HID, NH * DR))
    WkvaT = np.ascontiguousarray(Wkva.T.astype(f32, copy=False))
    wkc = np.ascontiguousarray(WkvaT[:, :LORA])
    wkp = np.ascontiguousarray(WkvaT[:, LORA:])
    WkvbT = np.ascontiguousarray(Wkvb.T.astype(f32, copy=False))
    wkbk = np.ascontiguousarray(WkvbT.reshape(LORA, NH, DN + DV)[:, :, :DN]
                                .reshape(LORA, NH * DN))
    wkbv = np.ascontiguousarray(WkvbT.reshape(LORA, NH, DN + DV)[:, :, DN:]
                                .reshape(LORA, NH * DV))
    woT = np.ascontiguousarray(Wo.T.astype(f32, copy=False))
    wfcT = np.ascontiguousarray(wfc.T.astype(f32, copy=False))
    wpjT = np.ascontiguousarray(wpj.T.astype(f32, copy=False))
    kvw = np.ascontiguousarray(np.asarray(kvaw).reshape(4, 128).T.astype(f32, copy=False))
    cosr = np.ascontiguousarray(np.asarray(cos).astype(f32, copy=False)).reshape(1, DR)
    sinr = np.ascontiguousarray(np.asarray(sin).astype(f32, copy=False)).reshape(1, DR)

    in_maps = []
    for c in range(8):
        beta, r = c // 4, c % 4
        xb = np.asarray(x[beta]).T.astype(f32, copy=False)
        idx = (np.arange(S) // 4) * 4 + (np.arange(S) + r) % 4
        xrot = np.ascontiguousarray(xb[:, idx])
        xown = np.ascontiguousarray(xb[:, r::4])
        ki = np.arange(128)
        A = 4 * (ki // 4) + (ki + r) % 4 - r
        cmeta_v = np.zeros((128, 8), f32)
        cmeta_v[:, 0] = A.astype(f32)
        in_maps.append({
            "xT": xrot, "x_own": xown, "cmeta": cmeta_v,
            "wqn": wqn, "wqp": wqp, "wkc": wkc, "wkp": wkp,
            "wkbk": wkbk, "wkbv": wkbv, "woT": woT,
            "wfcT": wfcT, "wpjT": wpjT, "kvw": kvw,
            "cosr": cosr, "sinr": sinr,
        })
    return in_maps


def kernel(**inputs):
    in_maps = _prep(inputs)
    nc, _ = build()
    res = run_bass_kernel_spmd(nc, in_maps, core_ids=list(range(8)))
    out = np.empty((B, S, HID), np.float32)
    for c in range(8):
        beta, r = c // 4, c % 4
        out[beta, r::4, :] = res.results[c]["outT"].T
    return out


if __name__ == "__main__":
    nc, _ = build()
    print("built:", nc.n_instructions(), "instructions")
